# version 15
# speedup vs baseline: 1.2129x; 1.0493x over previous
import numpy as np
from contextlib import ExitStack
import concourse.bass as bass
import concourse.mybir as mybir
from concourse.bass_utils import run_bass_kernel_spmd
from concourse.alu_op_type import AluOpType as OP

F32 = mybir.dt.float32
BF16 = mybir.dt.bfloat16
AF = mybir.ActivationFunctionType
AX = mybir.AxisListType.X

T = 2048
PADL = 4
TF = PADL + T + 2
NCH = 8
D = 1024
NTT = 4
TTW = 512

ARENA_F32 = 2052
N_WBUF = 3
WBUF_W = 512
POOL_BYTES = 46336


class Res:
    __slots__ = ("name", "lw", "readers", "exclusive")

    def __init__(self, name, exclusive=False):
        self.name = name
        self.lw = None
        self.readers = {}
        self.exclusive = exclusive

    def seed_from(self, others):
        for o in others:
            if o.lw is not None:
                k, c = o.lw
                self.readers[k] = max(self.readers.get(k, 0), c)
            for k, c in o.readers.items():
                self.readers[k] = max(self.readers.get(k, 0), c)


class Prog:
    ENGINES = ("pe", "dve", "act", "pool")
    DQ_SEMS = {"sync": list(range(0, 32)), "pool": list(range(32, 40)),
               "act": list(range(40, 52)), "dve": list(range(52, 60))}

    def __init__(self, nc, n_dma_sems=60):
        self.nc = nc
        self.ops = {e: [] for e in self.ENGINES}
        self.counts = {e: 0 for e in self.ENGINES}
        self.dma_ops = []
        self.n_dma_sems = n_dma_sems
        self.dma_counts = [0] * n_dma_sems
        self.n_dma = 0
        self.dq_count = {q: 0 for q in self.DQ_SEMS}
        self.needed = {e: set() for e in self.ENGINES}
        self.gidx = 0

    def _waits_for(self, key, reads, writes):
        waits = {}

        def need(owner):
            if owner is None:
                return
            k, c = owner
            if k == key:
                return
            if waits.get(k, 0) < c:
                waits[k] = c

        for r in reads:
            need(r.lw)
            if r.exclusive:
                for k, c in r.readers.items():
                    need((k, c))
        for w in writes:
            need(w.lw)
            for k, c in w.readers.items():
                need((k, c))
        return waits

    def _mark(self, waits):
        for k, c in waits.items():
            if k[0] == "e":
                self.needed[k[1]].add(c)

    def op(self, engine, fn, reads=(), writes=()):
        key = ("e", engine)
        if engine == "pe":
            waits = self._waits_for(key, reads, writes)
        else:
            waits = self._waits_for(None, reads, writes)
        self._mark(waits)
        self.counts[engine] += 1
        cnt = self.counts[engine]
        for r in reads:
            r.readers[key] = cnt
        for w in writes:
            w.lw = (key, cnt)
            w.readers = {}
        self.gidx += 1
        self.ops[engine].append(("op", waits, fn, cnt, self.gidx))

    def dma(self, fn, reads=(), writes=(), q="sync"):
        sems = self.DQ_SEMS[q]
        si = sems[self.dq_count[q] % len(sems)]
        self.dq_count[q] += 1
        self.n_dma += 1
        self.dma_counts[si] += 16
        key = ("d", si)
        waits = self._waits_for(key, reads, writes)
        if self.dma_counts[si] > 16:
            waits[key] = max(waits.get(key, 0), self.dma_counts[si] - 16)
        self._mark(waits)
        cnt = self.dma_counts[si]
        for r in reads:
            r.readers[key] = cnt
        for w in writes:
            w.lw = (key, cnt)
            w.readers = {}
        self.gidx += 1
        if q == "sync":
            self.dma_ops.append((waits, fn, si, cnt, self.gidx))
        else:
            self.ops[q].append(("dma", waits, fn, (si, cnt), self.gidx))

    def truncate(self, gmax):
        if gmax is None:
            return
        for e in self.ENGINES:
            self.ops[e] = [o for o in self.ops[e] if o[4] <= gmax]
        self.dma_ops = [o for o in self.dma_ops if o[4] <= gmax]
        cnts = [0] * self.n_dma_sems
        for waits, fn, si, cnt, g in self.dma_ops:
            cnts[si] = max(cnts[si], cnt)
        for e in self.ENGINES:
            for kind, waits, fn, x, g in self.ops[e]:
                if kind == "dma":
                    si, cnt = x
                    cnts[si] = max(cnts[si], cnt)
        self.dma_counts = cnts

    def finalize(self):
        ranks = {}
        for e in self.ENGINES:
            marked = sorted(self.needed[e])
            rk = {}
            r = 0
            mi = 0
            for i in range(1, self.counts[e] + 1):
                if mi < len(marked) and marked[mi] == i:
                    r += 1
                    mi += 1
                rk[i] = r
            ranks[e] = (rk, set(marked))
        return ranks

    def _emit_waits(self, eng, waits, satisfied, ranks, sems, dma_sems):
        for k in sorted(waits.keys(), key=str):
            c = waits[k]
            if k[0] == "e":
                v = ranks[k[1]][0][c]
                sem = sems[k[1]]
            else:
                v = c
                sem = dma_sems[k[1]]
            if satisfied.get(k, -1) >= v:
                continue
            satisfied[k] = v
            eng.wait_ge(sem, v)

    def replay_engine(self, ename, eng, sems, dma_sems, ranks):
        satisfied = {}
        rk, marked = ranks[ename]
        for kind, waits, fn, x, g in self.ops[ename]:
            self._emit_waits(eng, waits, satisfied, ranks, sems, dma_sems)
            if kind == "op":
                inst = fn(eng)
                if x in marked:
                    inst.then_inc(sems[ename], 1)
            else:
                si, cnt = x
                fn(eng).then_inc(dma_sems[si], 16)
        if ename in self.DQ_SEMS:
            for si in self.DQ_SEMS[ename]:
                if self.dma_counts[si] > 0:
                    eng.wait_ge(dma_sems[si], self.dma_counts[si])

    def replay_dma(self, eng, sems, dma_sems, ranks):
        satisfied = {}
        for waits, fn, si, cnt, g in self.dma_ops:
            self._emit_waits(eng, waits, satisfied, ranks, sems, dma_sems)
            fn(eng).then_inc(dma_sems[si], 16)
        for si in self.DQ_SEMS["sync"]:
            if self.dma_counts[si] > 0:
                eng.wait_ge(dma_sems[si], self.dma_counts[si])


class PsumPool:
    def __init__(self, nc, stack):
        self.banks = [(stack.enter_context(nc.psum_tensor(f"pb{i}", [128, 512], F32)),
                       Res(f"pb{i}", exclusive=True)) for i in range(8)]
        self.i = 0
        self.pinned = set()

    def alloc(self, pin=False):
        for _ in range(16):
            t, r = self.banks[self.i % 8]
            self.i += 1
            if id(r) not in self.pinned:
                if pin:
                    self.pinned.add(id(r))
                return t, r
        raise RuntimeError("all psum banks pinned")

    def release(self, r):
        self.pinned.discard(id(r))


class Arena:
    def __init__(self, nc, stack):
        self.tiles = [stack.enter_context(nc.sbuf_tensor(f"ar{i}", [128, ARENA_F32], F32))
                      for i in range(8)]
        self.hist = [[] for _ in range(8)]

    def reset(self):
        pass

    def carve(self, ti, name):
        r = Res(f"ar{ti}:{name}")
        r.seed_from(self.hist[ti])
        self.hist[ti] = self.hist[ti][-24:]
        self.hist[ti].append(r)
        return r

    def f32(self, ti, c0, w, p0=0, np_=128):
        return self.tiles[ti][p0:p0 + np_, c0:c0 + w]

    def bf16(self, ti, c0b, w, p0=0, np_=128):
        full = self.tiles[ti][0:128, 0:ARENA_F32].bitcast(BF16)
        return full[p0:p0 + np_, c0b:c0b + w]


class WStream:
    def __init__(self, nc, stack, p):
        self.p = p
        self.f = [(stack.enter_context(nc.sbuf_tensor(f"wf{i}", [128, WBUF_W], F32)),
                   Res(f"wf{i}a"), Res(f"wf{i}b")) for i in range(N_WBUF)]
        self.b = [(stack.enter_context(nc.sbuf_tensor(f"wbb{i}", [128, WBUF_W], BF16)),
                   Res(f"wbb{i}")) for i in range(N_WBUF)]
        self.fi = 0
        self.bi = 0
        self.ext = []

    def _f(self):
        slots = self.f + self.ext
        s = slots[self.fi % len(slots)]
        self.fi += 1
        return s

    def _b(self):
        s = self.b[self.bi % N_WBUF]
        self.bi += 1
        return s

    def load(self, dram_slice, w, cast_engine="pool", scale=None, extra_reads=()):
        p = self.p
        ft, fr, fr2 = self._f()
        bt, br = self._b()
        rows = dram_slice.shape[0]
        fap = ft[0:rows, 0:w]
        bap = bt[0:rows, 0:w]
        p.dma(lambda q, d=dram_slice, o=fap: q.dma_start(o, d),
              reads=tuple(extra_reads), writes=(fr, fr2))
        if scale is None:
            p.op(cast_engine, lambda e, o=bap, i_=fap: e.tensor_copy(o, i_),
                 reads=(fr, fr2), writes=(br,))
        else:
            sap, sres = scale
            p.op("dve", lambda e, o=bap, i_=fap, s=sap: e.tensor_scalar_mul(o, i_, s),
                 reads=(fr, fr2, sres), writes=(br,))
        return bap, br

    def load_scaled4(self, dram3d, cw, scales, extra_reads=()):
        p = self.p
        ft, fr, fr2 = self._f()
        bt, br = self._b()
        kb = len(scales)
        w = kb * cw
        p.dma(lambda q, d=dram3d, o=ft[0:128, 0:w]:
              q.dma_start(o, d), reads=tuple(extra_reads), writes=(fr, fr2))
        for j, (sap, sres) in enumerate(scales):
            p.op("dve", lambda e, o=bt[0:128, j * cw:(j + 1) * cw],
                 i_=ft[0:128, j * cw:(j + 1) * cw], s=sap:
                 e.tensor_scalar_mul(o, i_, s),
                 reads=(fr, sres), writes=(br,))
        return bt[0:128, 0:w], br

    def load_bf16(self, dram_slice, w, extra_reads=()):
        p = self.p
        ft, fr, fr2 = self._f()
        rows = dram_slice.shape[0]
        bap = ft[0:128, 0:WBUF_W].bitcast(BF16)[0:rows, 0:w]
        p.dma(lambda q, d=dram_slice, o=bap: q.dma_start(o, d), reads=tuple(extra_reads),
              writes=(fr, fr2))
        return bap, fr

    def load_f32(self, dram_slice, w):
        p = self.p
        ft, fr, fr2 = self._f()
        rows = dram_slice.shape[0]
        fap = ft[0:rows, 0:w]
        p.dma(lambda q, d=dram_slice, o=fap: q.dma_start(o, d), reads=(),
              writes=(fr, fr2))
        return fap, fr


class Ctx:
    def __init__(self):
        self._small = {}
        self.pool_mode = False
        self._pool_t = None
        self._pool_bc = None
        self._pool_cache = {}
        self._pool_off = 0
        self._pool_hist = []
        self._pool_cur = []

    def pool_init(self):
        self._pool_t = self.stack.enter_context(
            self.nc.sbuf_tensor("smpool", [128, POOL_BYTES // 4], F32))
        self._pool_bc = self._pool_t[0:128, 0:POOL_BYTES // 4].bitcast(BF16)

    def pool_block_boundary(self):
        self._pool_hist.extend(self._pool_cur)
        self._pool_cur = []
        self._pool_cache = {}
        self._pool_off = 0

    def pool_alias(self, name, res):
        base = self._pool_cache[name][1]
        for (r, o0, o1) in self._pool_cur:
            if r is base:
                self._pool_cur.append((res, o0, o1))
                return
        raise KeyError(name)

    def small(self, name, shape, dtype):
        if not self.pool_mode or name in self._small:
            if name not in self._small:
                t = self.stack.enter_context(
                    self.nc.sbuf_tensor("sm_" + name, shape, dtype))
                self._small[name] = (t, Res(name))
            t, r = self._small[name]
            return t, r
        if name in self._pool_cache:
            ap, r, sh, dt = self._pool_cache[name]
            assert sh == tuple(shape) and dt is dtype, (name, sh, shape, dt, dtype)
            return ap, r
        esz = 4 if dtype is F32 else 2
        nb = shape[1] * esz
        off = (self._pool_off + 3) & ~3
        assert off + nb <= POOL_BYTES, ("smpool overflow", name, off, nb)
        self._pool_off = off + nb
        if dtype is F32:
            ap = self._pool_t[0:shape[0], off // 4: off // 4 + shape[1]]
        else:
            ap = self._pool_bc[0:shape[0], off // 2: off // 2 + shape[1]]
        r = Res("pool:" + name)
        r.seed_from([pr for (pr, o0, o1) in self._pool_hist
                     if not (off + nb <= o0 or o1 <= off)])
        self._pool_cache[name] = (ap, r, tuple(shape), dtype)
        self._pool_cur.append((r, off, off + nb))
        return ap, r

    def Xd_tt(self, c, tt):
        return self.X[c][0:128, PADL + tt * TTW: PADL + (tt + 1) * TTW]

    def XBd_tt(self, c, tt):
        return self.XB[c][0:128, PADL + tt * TTW: PADL + (tt + 1) * TTW]

    def Xdata(self, c, off=0, w=T):
        return self.X[c][0:128, PADL + off: PADL + off + w]

    def XBdata(self, c, off=0, w=T):
        return self.XB[c][0:128, PADL + off: PADL + off + w]


def wv2(ap_full, wtot, row0, kblocks, col0, cw=128):
    t = ap_full.tensor
    off = row0 * wtot + col0
    return bass.AP(t, off, [[wtot, 128], [128 * wtot, kblocks], [1, cw]])


def wide_view(ap_full, wtot, row0, kblocks, colblk):
    return wv2(ap_full, wtot, row0, kblocks, colblk * 128)


def rawap(src, off, dims):
    if isinstance(src, bass.AP):
        d = [[src.ap[0][0], dims[0][1]]] + [list(x) for x in dims[1:]]
        return bass.AP(src.tensor, src.offset + off, d)
    return bass.AP(src, off, dims)


def mm(p, out_ap, lhs, rhs, reads, writes, start, stop):
    p.op("pe", lambda t, o=out_ap, l=lhs, r=rhs, s1=start, s2=stop:
         t.matmul(o, l, r, start=s1, stop=s2), reads=reads, writes=writes)


def memset(p, engine, ap, val, writes, reads=()):
    p.op(engine, lambda e, a=ap, v=val: e.memset(a, v), reads=reads, writes=writes)


def ws_extra(ctx, n=3):
    ext = []
    for k in range(n):
        nm = "wfx%d" % k
        t, r = ctx.small(nm, [128, WBUF_W], F32)
        ra, rb = Res(nm + "a"), Res(nm + "b")
        ra.seed_from([r])
        rb.seed_from([r])
        ctx.pool_alias(nm, ra)
        ctx.pool_alias(nm, rb)
        ext.append((t, ra, rb))
    ctx.ws.ext = ext


def emit_proj_ocmajor(ctx, w_dram, kchunks, oc0, noc, rhs_fn, rhs_res_fn, evac_fn,
                      wcol0=0, ntt=NTT):
    p = ctx.p
    ws = ctx.ws
    wtot = w_dram.shape[1]
    for oci in range(noc):
        oc = oc0 + oci
        pss = [ctx.psum.alloc() for _ in range(ntt)]
        for kg0 in range(0, kchunks, 4):
            kb = min(4, kchunks - kg0)
            wap, wr = ws.load(wv2(w_dram, wtot, kg0 * 128, kb,
                                  wcol0 + oci * 128), kb * 128)
            for j in range(kb):
                kc = kg0 + j
                lh = wap[0:128, j * 128:(j + 1) * 128]
                for tt in range(ntt):
                    ps, pr = pss[tt]
                    mm(p, ps[:, 0:512], lh, rhs_fn(kc, tt),
                       (wr, rhs_res_fn(kc)), (pr,), kc == 0, kc == kchunks - 1)
        for tt in range(ntt):
            ps, pr = pss[tt]
            evac_fn(oc, tt, ps, pr)


def emit_postln(ctx, extra=None):
    p = ctx.p
    A = ctx.arena
    if extra is not None:
        for c in range(NCH):
            ex, exr = extra[c]
            p.op("dve", lambda e, o=ctx.Xdata(c), a=ctx.Xdata(c), b=ex:
                 e.tensor_tensor(o, a, b, OP.add), reads=(ctx.Xr[c], exr),
                 writes=(ctx.Xr[c],))
    sq_res = [A.carve(c, "sq") for c in range(NCH)]

    def sq_ap(c, tt):
        return A.bf16(c, tt * TTW, TTW)

    for c in range(NCH):
        p.op("pool", lambda e, o=ctx.XBdata(c), i_=ctx.Xdata(c): e.tensor_copy(o, i_),
             reads=(ctx.Xr[c],), writes=(ctx.XBr[c],))
        p.op("act", lambda e, o=A.bf16(c, 0, 2048), i_=ctx.Xdata(c):
             e.activation(o, i_, AF.Square), reads=(ctx.Xr[c],), writes=(sq_res[c],))
    rowA, rowAr = (lambda t_, r_: (t_[0:1, 512:1024], r_))(*ctx.small("l3a_c", [16, T], F32))
    rowB, rowBr = (lambda t_, r_: (t_[0:1, 1024:1536], r_))(*ctx.small("l3a_c", [16, T], F32))
    rowC, rowCr = (lambda t_, r_: (t_[0:1, 0:512], r_))(*ctx.small("cacc1", [128, 512], F32))
    for tt in range(NTT):
        psA, prA = ctx.psum.alloc()
        psB, prB = ctx.psum.alloc()
        for c in range(NCH):
            mm(p, psA[0:1, 0:512], ctx.onescol_b, ctx.XBd_tt(c, tt),
               (ctx.onescol_b_res, ctx.XBr[c]), (prA,), c == 0, c == NCH - 1)
        for c in range(NCH):
            mm(p, psB[0:1, 0:512], ctx.onescol_b, sq_ap(c, tt),
               (ctx.onescol_b_res, sq_res[c]), (prB,), c == 0, c == NCH - 1)
        p.op("dve", lambda e, o=rowA[0:1, :], i_=psA[0:1, 0:512]:
             e.tensor_single_scalar(o, i_, 1.0 / D, OP.mult), reads=(prA,),
             writes=(rowAr,))
        p.op("act", lambda e, o=rowC[0:1, :], i_=rowA[0:1, :]:
             e.activation(o, i_, AF.Square), reads=(rowAr,), writes=(rowCr,))
        p.op("dve", lambda e, o=rowB[0:1, :], i_=psB[0:1, 0:512]:
             e.tensor_single_scalar(o, i_, 1.0 / D, OP.mult), reads=(prB,),
             writes=(rowBr,))
        p.op("dve", lambda e, o=rowB[0:1, :], a=rowB[0:1, :], b=rowC[0:1, :]:
             e.tensor_tensor(o, a, b, OP.subtract), reads=(rowBr, rowCr),
             writes=(rowBr,))
        p.op("act", lambda e, o=rowB[0:1, :], i_=rowB[0:1, :], b=ctx.eps_col[0:1, 0:1]:
             e.activation(o, i_, AF.Ln, bias=b), reads=(rowBr, ctx.onescol_f_res),
             writes=(rowBr,))
        p.op("act", lambda e, o=rowB[0:1, :], i_=rowB[0:1, :], sc=ctx.mhalf_col[0:1, 0:1]:
             e.activation(o, i_, AF.Exp, scale=sc), reads=(rowBr, ctx.onescol_f_res),
             writes=(rowBr,))
        p.op("dve", lambda e, o=rowA[0:1, :], a=rowA[0:1, :], b=rowB[0:1, :]:
             e.tensor_tensor(o, a, b, OP.mult), reads=(rowAr, rowBr), writes=(rowAr,))
        psR, prR = ctx.psum.alloc()
        psM, prM = ctx.psum.alloc()
        mm(p, psR[0:128, 0:512], ctx.ones_f, rowB[0:1, :],
           (ctx.ones_f_res, rowBr), (prR,), True, True)
        mm(p, psM[0:128, 0:512], ctx.negones_f, rowA[0:1, :],
           (ctx.negones_f_res, rowAr), (prM,), True, True)
        for c in range(NCH):
            xap = ctx.Xd_tt(c, tt)
            p.op("dve", lambda e, o=xap, a=xap, b=psR[0:128, 0:512]:
                 e.tensor_tensor(o, a, b, OP.mult), reads=(ctx.Xr[c], prR),
                 writes=(ctx.Xr[c],))
            p.op("dve", lambda e, o=xap, a=xap, b=psM[0:128, 0:512]:
                 e.tensor_tensor(o, a, b, OP.add), reads=(ctx.Xr[c], prM),
                 writes=(ctx.Xr[c],))
            p.op("pool", lambda e, o=ctx.XBd_tt(c, tt), i_=xap: e.tensor_copy(o, i_),
                 reads=(ctx.Xr[c],), writes=(ctx.XBr[c],))


def emit_l0_attn(ctx):
    p, A, ws, W = ctx.p, ctx.arena, ctx.ws, ctx.weights
    A.reset()
    ws_extra(ctx)
    CONVW = 4 + T
    conv_res = [A.carve(k % 8, f"conv{k}") for k in range(16)]

    def conv_ap(k, c0, w):
        return A.bf16(k % 8, (k // 8) * CONVW + c0, w)

    for k in range(16):
        memset(p, "dve", conv_ap(k, 0, 4), 0.0, (conv_res[k],))

    vec16 = [ctx.small("v16_%d" % k, [128, 6], F32) for k in range(16)]
    for k in range(16):
        vt, vr = vec16[k]
        p.dma(lambda q, d=W["l0_conv_w"][k * 128:(k + 1) * 128, 0:4], o=vt[0:128, 0:4]:
              q.dma_start(o, d), reads=(), writes=(vr,))
        p.dma(lambda q, d=W["l0_conv_b"][k * 128:(k + 1) * 128, 0:1], o=vt[0:128, 4:5]:
              q.dma_start(o, d), reads=(), writes=(vr,))
        p.dma(lambda q, d=W["l0_d_param"][k * 128:(k + 1) * 128, 0:1], o=vt[0:128, 5:6]:
              q.dma_start(o, d), reads=(), writes=(vr,))

    sg_scr = [ctx.small("sg%d" % i, [128, 512], BF16) for i in range(3)]

    def evac_gated(oc, tt, ps, pr):
        k = oc - 16
        sgt, sgr = sg_scr[(oc * NTT + tt) % 3]
        p.op("act", lambda e, o=sgt[0:128, 0:512], i_=ps[:, 0:512]:
             e.activation(o, i_, AF.Silu), reads=(pr,), writes=(sgr,))
        p.dma(lambda q, d=ctx.dram_sg[k * 128:(k + 1) * 128, tt * TTW:(tt + 1) * TTW],
              s=sgt[0:128, 0:512]: q.dma_start(d, s), reads=(sgr,),
              writes=(ctx.dram_sg_res,))

    def evac_convin(oc, tt, ps, pr):
        p.op("dve", lambda e, o=conv_ap(oc, 4 + tt * TTW, 512), i_=ps[:, 0:512]:
             e.tensor_copy(o, i_), reads=(pr,), writes=(conv_res[oc],))

    def rhs_fn(kc, tt):
        return ctx.XBd_tt(kc, tt)

    def rhs_res(kc):
        return ctx.XBr[kc]

    emit_proj_ocmajor(ctx, W["l0_w_in"], NCH, 16, 16, rhs_fn, rhs_res, evac_gated,
                      wcol0=16 * 128)
    emit_proj_ocmajor(ctx, W["l0_w_in"], NCH, 0, 16, rhs_fn, rhs_res, evac_convin,
                      wcol0=0)

    conv16, conv16_res = ctx.small("conv16", [16, T], BF16)
    acc_scr = [ctx.small("cacc%d" % i, [128, 512], F32) for i in range(2)]
    for k in range(16):
        vt, vr = vec16[k]
        for tt in range(NTT):
            acc, accr = acc_scr[tt % 2]
            p.op("dve", lambda e, o=acc[0:128, 0:512], i_=conv_ap(k, 1 + tt * TTW, 512),
                 s=vt[0:128, 0:1]: e.tensor_scalar_mul(o, i_, s),
                 reads=(conv_res[k], vr), writes=(accr,))
            for j in (1, 2, 3):
                p.op("dve", lambda e, o=acc[0:128, 0:512],
                     i_=conv_ap(k, 1 + j + tt * TTW, 512), s=vt[0:128, j:j + 1],
                     a=acc[0:128, 0:512]:
                     e.scalar_tensor_tensor(o, i_, s, a, OP.mult, OP.add),
                     reads=(conv_res[k], vr, accr), writes=(accr,))
            p.op("act", lambda e, o=conv_ap(k, 4 + tt * TTW, 512), i_=acc[0:128, 0:512],
                 b=vt[0:128, 4:5]: e.activation(o, i_, AF.Silu, bias=b),
                 reads=(accr, vr), writes=(conv_res[k],))
            if k == 0:
                p.op("act", lambda e, o=conv16[0:16, tt * TTW:(tt + 1) * TTW],
                     i_=acc[0:16, 0:512], b=vt[0:16, 4:5]:
                     e.activation(o, i_, AF.Silu, bias=b),
                     reads=(accr, vr), writes=(conv16_res,))

    wxt, wxr = ctx.small("wx", [128, 16 * 33], BF16)
    for k in range(16):
        fap, fr = ws.load_f32(W["l0_w_x"][k * 128:(k + 1) * 128, 0:33], 33)
        p.op("pool", lambda e, o=wxt[0:128, k * 33:(k + 1) * 33], i_=fap:
             e.tensor_copy(o, i_), reads=(fr,), writes=(wxr,))

    avec, avr = ctx.small("avec", [16, 2], F32)
    p.dma(lambda q, d=W["l0_a_log"][0:16, 0:1], o=avec[0:16, 0:1]: q.dma_start(o, d),
          reads=(), writes=(avr,))
    p.op("act", lambda e, o=avec[0:16, 1:2], i_=avec[0:16, 0:1]:
         e.activation(o, i_, AF.Exp), reads=(avr,), writes=(avr,))
    p.op("dve", lambda e, o=avec[0:16, 1:2], i_=avec[0:16, 1:2]:
         e.tensor_single_scalar(o, i_, -1.0, OP.mult), reads=(avr,), writes=(avr,))

    rowA, rowAr = (lambda t_, r_: (t_[0:1, 512:1024], r_))(*ctx.small("l3a_c", [16, T], F32))
    rowB, rowBr = (lambda t_, r_: (t_[0:1, 1024:1536], r_))(*ctx.small("l3a_c", [16, T], F32))
    rowC, rowCr = (lambda t_, r_: (t_[0:1, 0:512], r_))(*ctx.small("cacc1", [128, 512], F32))
    r16a, r16ar = ctx.small("r16a", [16, 512], F32)
    r16b, r16br = ctx.small("r16b", [16, 512], F32)
    r16c, r16cr = ctx.small("r16c", [16, 512], F32)
    r16d, r16dr = ctx.small("r16d", [16, 512], F32)
    hst, hstr = ctx.small("st16", [16, 2], F32)
    memset(p, "dve", hst[0:16, 0:1], 0.0, (hstr,))
    sg_in = [ctx.small("sg%d" % i, [128, 512], BF16) for i in range(2)]
    go_scr = [ctx.small("cacc%d" % i, [128, 512], F32) for i in range(2)]

    for tt in range(NTT):
        psd, prd = ctx.psum.alloc()
        psb_, prb_ = ctx.psum.alloc()
        psc, prc = ctx.psum.alloc()
        for k in range(16):
            kw = wxt[0:128, k * 33:(k + 1) * 33]
            rhs = conv_ap(k, 4 + tt * TTW, 512)
            mm(p, psd[0:1, 0:512], kw[0:128, 0:1], rhs, (wxr, conv_res[k]), (prd,),
               k == 0, k == 15)
            mm(p, psb_[0:16, 0:512], kw[0:128, 1:17], rhs, (wxr, conv_res[k]), (prb_,),
               k == 0, k == 15)
            mm(p, psc[0:16, 0:512], kw[0:128, 17:33], rhs, (wxr, conv_res[k]), (prc,),
               k == 0, k == 15)
        p.op("dve", lambda e, o=rowA[0:1, :], i_=psd[0:1, 0:512]: e.tensor_copy(o, i_),
             reads=(prd,), writes=(rowAr,))
        p.op("act", lambda e, o=rowB[0:1, :], i_=rowA[0:1, :]: e.activation(o, i_, AF.Abs),
             reads=(rowAr,), writes=(rowBr,))
        p.op("act", lambda e, o=rowB[0:1, :], i_=rowB[0:1, :]:
             e.activation(o, i_, AF.Exp, scale=-1.0), reads=(rowBr,), writes=(rowBr,))
        p.op("act", lambda e, o=rowB[0:1, :], i_=rowB[0:1, :]:
             e.activation(o, i_, AF.Ln, bias=1.0), reads=(rowBr,), writes=(rowBr,))
        p.op("dve", lambda e, o=rowC[0:1, :], i_=rowA[0:1, :]:
             e.tensor_single_scalar(o, i_, 0.0, OP.max), reads=(rowAr,), writes=(rowCr,))
        p.op("dve", lambda e, o=rowA[0:1, :], a=rowC[0:1, :], b=rowB[0:1, :]:
             e.tensor_tensor(o, a, b, OP.add), reads=(rowBr, rowCr), writes=(rowAr,))
        psR, prR = ctx.psum.alloc()
        mm(p, psR[0:16, 0:512], ctx.ones_f[0:1, 0:16], rowA[0:1, :],
           (ctx.ones_f_res, rowAr), (prR,), True, True)
        p.op("dve", lambda e, o=r16a[0:16, :], i_=psb_[0:16, 0:512]: e.tensor_copy(o, i_),
             reads=(prb_,), writes=(r16ar,))
        p.op("dve", lambda e, o=r16a[0:16, :], a=r16a[0:16, :], b=psR[0:16, 0:512]:
             e.tensor_tensor(o, a, b, OP.mult), reads=(r16ar, prR), writes=(r16ar,))
        p.op("dve", lambda e, o=r16c[0:16, :], i_=psc[0:16, 0:512]: e.tensor_copy(o, i_),
             reads=(prc,), writes=(r16cr,))
        p.op("act", lambda e, o=r16b[0:16, :], i_=psR[0:16, 0:512], s=avec[0:16, 1:2]:
             e.activation(o, i_, AF.Exp, scale=s), reads=(prR, avr), writes=(r16br,))
        p.op("dve", lambda e, o=r16a[0:16, :], a=r16a[0:16, :],
             b=conv16[0:16, tt * TTW:(tt + 1) * TTW]:
             e.tensor_tensor(o, a, b, OP.mult), reads=(r16ar, conv16_res),
             writes=(r16ar,))
        p.op("dve", lambda e, o=r16d[0:16, :], a=r16b[0:16, :], b=r16a[0:16, :],
             ini=hst[0:16, 0:1]: e.tensor_tensor_scan(o, a, b, ini, OP.mult, OP.add),
             reads=(r16br, r16ar, hstr), writes=(r16dr,))
        p.op("dve", lambda e, o=hst[0:16, 0:1], i_=r16d[0:16, 511:512]:
             e.tensor_copy(o, i_), reads=(r16dr,), writes=(hstr,))
        p.op("dve", lambda e, o=r16d[0:16, :], a=r16d[0:16, :], b=r16c[0:16, :]:
             e.tensor_tensor(o, a, b, OP.mult), reads=(r16dr, r16cr), writes=(r16dr,))
        psY, prY = ctx.psum.alloc()
        mm(p, psY[0:1, 0:512], ctx.onescol_f[0:16, 0:1], r16d[0:16, :],
           (ctx.onescol_f_res, r16dr), (prY,), True, True)
        p.op("dve", lambda e, o=rowA[0:1, :], i_=psY[0:1, 0:512]: e.tensor_copy(o, i_),
             reads=(prY,), writes=(rowAr,))
        psS, prS = ctx.psum.alloc()
        mm(p, psS[0:128, 0:512], ctx.ones_f, rowA[0:1, :],
           (ctx.ones_f_res, rowAr), (prS,), True, True)
        for k in range(16):
            vt, vr = vec16[k]
            got, gor = go_scr[k % 2]
            sgt, sgr = sg_in[k % 2]
            p.dma(lambda q, o=sgt[0:128, 0:512],
                  d=ctx.dram_sg[k * 128:(k + 1) * 128, tt * TTW:(tt + 1) * TTW]:
                  q.dma_start(o, d), reads=(ctx.dram_sg_res,), writes=(sgr,))
            p.op("dve", lambda e, o=got[0:128, 0:512], i_=conv_ap(k, 4 + tt * TTW, 512),
                 s=vt[0:128, 5:6], a=psS[0:128, 0:512]:
                 e.scalar_tensor_tensor(o, i_, s, a, OP.mult, OP.add),
                 reads=(conv_res[k], vr, prS), writes=(gor,))
            p.op("dve", lambda e, o=conv_ap(k, 4 + tt * TTW, 512), a=got[0:128, 0:512],
                 b=sgt[0:128, 0:512]: e.tensor_tensor(o, a, b, OP.mult),
                 reads=(gor, sgr), writes=(conv_res[k],))

    def evac_out(oc, tt, ps, pr):
        xap = ctx.Xd_tt(oc, tt)
        p.op("dve", lambda e, o=xap, a=ps[:, 0:512], b=xap:
             e.tensor_tensor(o, a, b, OP.add), reads=(pr, ctx.Xr[oc]),
             writes=(ctx.Xr[oc],))

    def rhs_go(kc, tt):
        return conv_ap(kc, 4 + tt * TTW, 512)

    def rhs_go_res(kc):
        return conv_res[kc]

    emit_proj_ocmajor(ctx, W["l0_w_out"], 16, 0, 8, rhs_go, rhs_go_res, evac_out)
    emit_postln(ctx)


def emit_l2_attn(ctx):
    p, A, ws, W = ctx.p, ctx.arena, ctx.ws, ctx.weights
    A.reset()
    bvec = [ctx.small("l2b_%d" % c, [128, 1], F32) for c in range(NCH)]
    for c in range(NCH):
        vt, vr = bvec[c]
        p.dma(lambda q, d=W["l2_b"][c * 128:(c + 1) * 128, 0:1], o=vt[0:128, 0:1]:
              q.dma_start(o, d), reads=(), writes=(vr,))
    lam_res = [A.carve(c, "lam") for c in range(NCH)]

    def lam_ap(c, tt):
        return A.f32(c, tt * TTW, TTW)

    def evac_lam(oc, tt, ps, pr):
        vt, vr = bvec[oc]
        p.op("act", lambda e, o=lam_ap(oc, tt), i_=ps[:, 0:512], b=vt[0:128, 0:1]:
             e.activation(o, i_, AF.Sigmoid, bias=b), reads=(pr, vr),
             writes=(lam_res[oc],))

    emit_proj_ocmajor(ctx, W["l2_wT"], NCH, 0, 8,
                      lambda kc, tt: ctx.XBd_tt(kc, tt), lambda kc: ctx.XBr[kc],
                      evac_lam)
    ta, tar = ctx.small("cacc0", [128, 512], F32)
    tb, tbr = ctx.small("cacc1", [128, 512], F32)
    st, str_ = ctx.small("l2st", [128, 1], F32)
    for c in range(NCH):
        memset(p, "dve", st[0:128, 0:1], 0.0, (str_,))
        for tt in range(NTT):
            xap = ctx.Xd_tt(c, tt)
            p.op("dve", lambda e, o=ta[0:128, :], a=lam_ap(c, tt), b=xap:
                 e.tensor_tensor(o, a, b, OP.mult), reads=(lam_res[c], ctx.Xr[c]),
                 writes=(tar,))
            p.op("dve", lambda e, o=tb[0:128, :], a=xap, b=ta[0:128, :]:
                 e.tensor_tensor(o, a, b, OP.subtract), reads=(ctx.Xr[c], tar),
                 writes=(tbr,))
            p.op("dve", lambda e, o=ta[0:128, :], a=lam_ap(c, tt), b=tb[0:128, :],
                 ini=st[0:128, 0:1]: e.tensor_tensor_scan(o, a, b, ini, OP.mult, OP.add),
                 reads=(lam_res[c], tbr, str_), writes=(tar,))
            p.op("dve", lambda e, o=st[0:128, 0:1], i_=ta[0:128, 511:512]:
                 e.tensor_copy(o, i_), reads=(tar,), writes=(str_,))
            p.op("dve", lambda e, o=xap, a=xap, b=ta[0:128, :]:
                 e.tensor_tensor(o, a, b, OP.add), reads=(ctx.Xr[c], tar),
                 writes=(ctx.Xr[c],))
    emit_postln(ctx)


def emit_l1_ffn(ctx):
    p, A, ws, W = ctx.p, ctx.arena, ctx.ws, ctx.weights
    A.reset()
    p4r = A.carve(0, "l1f_p4")
    r50r = A.carve(1, "l1f_r50")
    uar = A.carve(2, "l1f_ua")
    p4 = A.f32(0, 1024, 1024, p0=0, np_=4)
    r50 = A.f32(1, 1024, 1024, p0=0, np_=50)
    ua = A.f32(2, 1024, 1024, p0=0, np_=50)
    ldt, ldtr = ws.load_f32(W["l1f_lora_downT"][0:100, 0:4], 4)
    da0, da0r = ws.load_f32(W["l1f_down_aux"][0:100, 0:512], 512)
    da1, da1r = ws.load_f32(W["l1f_down_aux"][0:100, 512:1024], 512)
    ps1, pr1 = ctx.psum.alloc()
    ps2, pr2 = ctx.psum.alloc()
    mm(p, ps1[0:4, 0:512], ldt, da0, (ldtr, da0r), (pr1,), True, True)
    mm(p, ps2[0:4, 0:512], ldt, da1, (ldtr, da1r), (pr2,), True, True)
    p.op("dve", lambda e, o=p4[0:4, 0:512], i_=ps1[0:4, 0:512]: e.tensor_copy(o, i_),
         reads=(pr1,), writes=(p4r,))
    p.op("dve", lambda e, o=p4[0:4, 512:1024], i_=ps2[0:4, 0:512]: e.tensor_copy(o, i_),
         reads=(pr2,), writes=(p4r,))
    lut, lutr = ws.load_f32(W["l1f_lora_upT"][0:4, 0:50], 50)
    ps3, pr3 = ctx.psum.alloc()
    ps4, pr4 = ctx.psum.alloc()
    mm(p, ps3[0:50, 0:512], lut, p4[0:4, 0:512], (lutr, p4r), (pr3,), True, True)
    mm(p, ps4[0:50, 0:512], lut, p4[0:4, 512:1024], (lutr, p4r), (pr4,), True, True)
    p.op("dve", lambda e, o=r50[0:50, 0:512], i_=ps3[0:50, 0:512]: e.tensor_copy(o, i_),
         reads=(pr3,), writes=(r50r,))
    p.op("dve", lambda e, o=r50[0:50, 512:1024], i_=ps4[0:50, 0:512]: e.tensor_copy(o, i_),
         reads=(pr4,), writes=(r50r,))
    ua0, ua0r = ws.load_f32(W["l1f_up_auxT"][0:50, 0:512], 512)
    p.op("dve", lambda e, o=ua[0:50, 0:512], i_=ua0: e.tensor_copy(o, i_),
         reads=(ua0r,), writes=(uar,))
    ua1, ua1r = ws.load_f32(W["l1f_up_auxT"][0:50, 512:1024], 512)
    p.op("dve", lambda e, o=ua[0:50, 512:1024], i_=ua1: e.tensor_copy(o, i_),
         reads=(ua1r,), writes=(uar,))
    weff_res = [A.carve(c, "weff") for c in range(NCH)]

    def weff_ap(c, c0, w):
        return A.bf16(c, c0, w)

    for c in range(NCH):
        for half in range(2):
            psl, prl = ctx.psum.alloc()
            mm(p, psl[0:128, 0:512], r50[0:50, c * 128:(c + 1) * 128],
               ua[0:50, half * 512:(half + 1) * 512], (r50r, uar), (prl,), True, True)
            wt, wtr = ws.load_f32(W["l1f_wT"][c * 128:(c + 1) * 128,
                                              half * 512:(half + 1) * 512], 512)
            p.op("dve", lambda e, o=weff_ap(c, half * 512, 512), a=wt,
                 b=psl[0:128, 0:512]: e.tensor_tensor(o, a, b, OP.add),
                 reads=(wtr, prl), writes=(weff_res[c],))
    brow, browr = ctx.small("l1f_b", [1, 1024], BF16)
    bf_, bfr_ = ws.load_f32(W["l1f_b"][0:1, 0:512], 512)
    p.op("dve", lambda e, o=brow[0:1, 0:512], i_=bf_: e.tensor_copy(o, i_),
         reads=(bfr_,), writes=(browr,))
    bf2, bfr2 = ws.load_f32(W["l1f_b"][0:1, 512:1024], 512)
    p.op("dve", lambda e, o=brow[0:1, 512:1024], i_=bf2: e.tensor_copy(o, i_),
         reads=(bfr2,), writes=(browr,))
    for oc in range(NCH):
        for tt in range(NTT):
            ps, pr = ctx.psum.alloc()
            for kc in range(NCH):
                mm(p, ps[:, 0:512], weff_ap(kc, oc * 128, 128), ctx.XBd_tt(kc, tt),
                   (weff_res[kc], ctx.XBr[kc]), (pr,), kc == 0, False)
            mm(p, ps[:, 0:512], brow[0:1, oc * 128:(oc + 1) * 128], ctx.onesrow_b,
               (browr, ctx.onesrow_b_res), (pr,), False, True)
            xap = ctx.Xd_tt(oc, tt)
            p.op("dve", lambda e, o=xap, a=ps[:, 0:512], b=xap:
                 e.tensor_tensor(o, a, b, OP.add), reads=(pr, ctx.Xr[oc]),
                 writes=(ctx.Xr[oc],))
    emit_postln(ctx)


def emit_l1_attn(ctx):
    p, A, ws, W = ctx.p, ctx.arena, ctx.ws, ctx.weights
    A.reset()
    ws_extra(ctx)
    v8 = [ctx.small("l1v_%d" % c, [128, 8], F32) for c in range(NCH)]
    for c in range(NCH):
        vt, vr = v8[c]
        p.dma(lambda q, d=W["l1_tvec"][c * 128:(c + 1) * 128, 0:5], o=vt[0:128, 0:5]:
              q.dma_start(o, d), reads=(), writes=(vr,))
        p.op("act", lambda e, o=vt[0:128, 5:6], i_=vt[0:128, 3:4]:
             e.activation(o, i_, AF.Exp), reads=(vr,), writes=(vr,))
        p.op("act", lambda e, o=vt[0:128, 5:6], i_=vt[0:128, 5:6]:
             e.activation(o, i_, AF.Exp, scale=-1.0), reads=(vr,), writes=(vr,))
        p.op("act", lambda e, o=vt[0:128, 6:7], i_=vt[0:128, 4:5]:
             e.activation(o, i_, AF.Exp), reads=(vr,), writes=(vr,))

    def do_proj(wname, tmcol, evac):
        for kc in range(NCH):
            vt, vr = v8[kc]
            p.op("dve", lambda e, o=vt[0:128, 7:8], i_=vt[0:128, tmcol:tmcol + 1]:
                 e.tensor_scalar(o, i_, -1.0, 1.0, OP.mult, OP.add),
                 reads=(vr,), writes=(vr,))
        for oc in range(NCH):
            pss = [ctx.psum.alloc() for _ in range(NTT)]
            for kg0 in range(0, NCH, 4):
                sc1 = [(v8[kg0 + j][0][0:128, tmcol:tmcol + 1], v8[kg0 + j][1])
                       for j in range(4)]
                sc2 = [(v8[kg0 + j][0][0:128, 7:8], v8[kg0 + j][1])
                       for j in range(4)]
                w1, w1r = ws.load_scaled4(
                    wv2(W[wname], D, kg0 * 128, 4, oc * 128), 128, sc1)
                w2, w2r = ws.load_scaled4(
                    wv2(W[wname], D, kg0 * 128, 4, oc * 128), 128, sc2)
                for j in range(4):
                    kc = kg0 + j
                    for tt in range(NTT):
                        ps, pr = pss[tt]
                        mm(p, ps[:, 0:512], w1[0:128, j * 128:(j + 1) * 128],
                           ctx.XBd_tt(kc, tt),
                           (w1r, ctx.XBr[kc]), (pr,), kc == 0, False)
                        sh = ctx.XB[kc][0:128, PADL - 1 + tt * TTW:
                                        PADL - 1 + (tt + 1) * TTW]
                        mm(p, ps[:, 0:512], w2[0:128, j * 128:(j + 1) * 128], sh,
                           (w2r, ctx.XBr[kc]), (pr,), False, kc == NCH - 1)
            for tt in range(NTT):
                ps, pr = pss[tt]
                evac(oc, tt, ps, pr)

    sc0, sc0r = ctx.small("cacc0", [128, 512], F32)

    def mk_evac(dram, dres, act=None):
        def evac(oc, tt, ps, pr):
            p.op("act" if act else "dve",
                 (lambda e, o=sc0[0:128, :], i_=ps[:, 0:512], f=act:
                  e.activation(o, i_, f)) if act else
                 (lambda e, o=sc0[0:128, :], i_=ps[:, 0:512]: e.tensor_copy(o, i_)),
                 reads=(pr,), writes=(sc0r,))
            p.dma(lambda q, d=dram[oc * 128:(oc + 1) * 128, tt * TTW:(tt + 1) * TTW],
                  s=sc0[0:128, :]: q.dma_start(d, s), reads=(sc0r,), writes=(dres,))
        return evac

    do_proj("l1_wkT", 0, mk_evac(ctx.dram_a, ctx.dram_a_res, AF.Exp))
    do_proj("l1_wvT", 1, mk_evac(ctx.dram_b, ctx.dram_b_res, None))
    do_proj("l1_wrT", 2, mk_evac(ctx.dram_c, ctx.dram_c_res, AF.Sigmoid))

    res_ek = None
    for c in range(NCH):
        vt, vr = v8[c]
        r_ek = A.carve(0, "ek")
        r_v = A.carve(1, "v")
        r_sr = A.carve(2, "sr")
        r_a = A.carve(3, "a")
        r_b = A.carve(4, "b")
        r_w = A.carve(5, "wkv")
        ek = A.f32(0, 0, T)
        vv = A.f32(1, 0, T)
        sr = A.f32(2, 0, T)
        aa = A.f32(3, 0, T + 1)
        bb = A.f32(4, 0, T + 1)
        wk = A.bf16(5, 0, T)
        p.dma(lambda q, o=ek, d=ctx.dram_a[c * 128:(c + 1) * 128, 0:T]:
              q.dma_start(o, d), reads=(ctx.dram_a_res,), writes=(r_ek,))
        p.dma(lambda q, o=vv, d=ctx.dram_b[c * 128:(c + 1) * 128, 0:T]:
              q.dma_start(o, d), reads=(ctx.dram_b_res,), writes=(r_v,))
        p.dma(lambda q, o=sr, d=ctx.dram_c[c * 128:(c + 1) * 128, 0:T]:
              q.dma_start(o, d), reads=(ctx.dram_c_res,), writes=(r_sr,))
        p.op("dve", lambda e, o=vv, a=ek, b=vv: e.tensor_tensor(o, a, b, OP.mult),
             reads=(r_ek, r_v), writes=(r_v,))
        ew_b = rawap(vt, 5, [[8, 128], [0, T]])
        memset(p, "dve", A.f32(3, 0, 1), 0.0, (r_a,))
        memset(p, "dve", A.f32(4, 0, 1), 0.0, (r_b,))
        p.op("dve", lambda e, o=A.f32(3, 1, T), a=ew_b, b=vv:
             e.tensor_tensor_scan(o, a, b, 0.0, OP.mult, OP.add),
             reads=(vr, r_v), writes=(r_a,))
        p.op("dve", lambda e, o=A.f32(4, 1, T), a=ew_b, b=ek:
             e.tensor_tensor_scan(o, a, b, 0.0, OP.mult, OP.add),
             reads=(vr, r_ek), writes=(r_b,))
        p.op("dve", lambda e, o=vv, i_=vv, s=vt[0:128, 6:7], a=A.f32(3, 0, T):
             e.scalar_tensor_tensor(o, i_, s, a, OP.mult, OP.add),
             reads=(r_v, vr, r_a), writes=(r_v,))
        p.op("dve", lambda e, o=ek, i_=ek, s=vt[0:128, 6:7], a=A.f32(4, 0, T):
             e.scalar_tensor_tensor(o, i_, s, a, OP.mult, OP.add),
             reads=(r_ek, vr, r_b), writes=(r_ek,))
        p.op("dve", lambda e, o=ek, i_=ek: e.reciprocal(o, i_),
             reads=(r_ek,), writes=(r_ek,))
        p.op("dve", lambda e, o=vv, a=vv, b=ek: e.tensor_tensor(o, a, b, OP.mult),
             reads=(r_v, r_ek), writes=(r_v,))
        p.op("dve", lambda e, o=wk, a=vv, b=sr: e.tensor_tensor(o, a, b, OP.mult),
             reads=(r_v, r_sr), writes=(r_w,))
        p.dma(lambda q, d=ctx.dram_d[c * 128:(c + 1) * 128, 0:T], s=wk:
              q.dma_start(d, s), reads=(r_w,), writes=(ctx.dram_d_res,))

    rkt, rkt_cr = ctx.small("l1a_rk", [128, NCH * TTW], BF16)
    rk_res = [rkt_cr] * NCH
    for tt in range(NTT):
        for kc in range(NCH):
            rres = Res("l1a_rk%d_%d" % (tt, kc))
            rres.seed_from([rk_res[kc]])
            rk_res[kc] = rres
            p.dma(lambda q, o=rkt[0:128, kc * TTW:(kc + 1) * TTW],
                  d=ctx.dram_d[kc * 128:(kc + 1) * 128, tt * TTW:(tt + 1) * TTW]:
                  q.dma_start(o, d), reads=(ctx.dram_d_res,), writes=(rres,))
        for oc in range(NCH):
            ps, pr = ctx.psum.alloc()
            for kg0 in range(0, NCH, 4):
                wo, wor = ws.load(wv2(W["l1_woT"], D, kg0 * 128, 4, oc * 128), 512)
                for j in range(4):
                    kc = kg0 + j
                    mm(p, ps[:, 0:512], wo[0:128, j * 128:(j + 1) * 128],
                       rkt[0:128, kc * TTW:(kc + 1) * TTW], (wor, rk_res[kc]),
                       (pr,), kc == 0, kc == NCH - 1)
            xap = ctx.Xd_tt(oc, tt)
            p.op("dve", lambda e, o=xap, a=ps[:, 0:512], b=xap:
                 e.tensor_tensor(o, a, b, OP.add), reads=(pr, ctx.Xr[oc]),
                 writes=(ctx.Xr[oc],))
    emit_postln(ctx)


def emit_l0_ffn(ctx):
    p, A, ws, W = ctx.p, ctx.arena, ctx.ws, ctx.weights
    A.reset()
    ws_extra(ctx)
    for c in range(NCH):
        p.dma(lambda q, d=ctx.dram_xin[c * 128:(c + 1) * 128, 0:T], s=ctx.Xdata(c):
              q.dma_start(d, s), reads=(ctx.Xr[c],), writes=(ctx.dram_xin_res,))

    grp64b, grp64br = ctx.small("grp64b", [128, 2], BF16)
    grp2f, grp2fr = ctx.small("grp2f", [2, 128], F32)
    memset(p, "pool", grp64b[0:128, 0:2], 0.0, (grp64br,))
    memset(p, "pool", grp64b[0:64, 0:1], 1.0, (grp64br,))
    memset(p, "pool", grp64b[64:128, 1:2], 1.0, (grp64br,))
    p.dma(lambda q, d=W["l0f_grp"][0:2, 0:128], o=grp2f[0:2, 0:128]:
          q.dma_start(o, d), reads=(), writes=(grp2fr,))
    rotm, rotmr = ctx.small("rotm", [128, 128], BF16)
    p.dma(lambda q, d=W["l0f_rotm"][0:128, 0:128], o=rotm[0:128, 0:128]:
          q.dma_start(o, d), reads=(), writes=(rotmr,))
    idn, idnr = ctx.small("identb", [128, 128], BF16)
    iot, iotr = (lambda t_, r_: (t_[0:128, 0:128], r_))(
        *ctx.small("cacc0", [128, 512], F32))
    p.op("pool", lambda e, o=iot[0:128, 0:128]: e.iota(o, [[1, 128]], base=0,
         channel_multiplier=-1, allow_small_or_imprecise_dtypes=True),
         reads=(), writes=(iotr,))
    memset(p, "pool", idn[0:128, 0:128], 1.0, (idnr,))
    p.op("pool", lambda e, o=idn[0:128, 0:128], i_=idn[0:128, 0:128]:
         e.affine_select(o, i_, [[1, 128]], OP.is_equal, 0.0, base=0,
                         channel_multiplier=-1), reads=(idnr,), writes=(idnr,))
    mask, maskr = ctx.small("l0f_mask", [128, 5 * 640], BF16)
    for mi in range(5):
        p.dma(lambda q, d=W["l0f_mask"][0:128, mi * 640:(mi + 1) * 640],
              o=mask[0:128, mi * 640:(mi + 1) * 640]: q.dma_start(o, d),
              reads=(), writes=(maskr,))

    for lay in range(2):
        if lay == 1:
            for c in range(NCH):
                p.op("pool", lambda e, o=ctx.XBdata(c), i_=ctx.Xdata(c):
                     e.tensor_copy(o, i_), reads=(ctx.Xr[c],), writes=(ctx.XBr[c],))
        qsv, qsvr = ctx.small("l0f_qs", [128, 2], F32)
        p.dma(lambda q, d=W["l0f_qsk"][lay * 128:(lay + 1) * 128, 0:2],
              o=qsv[0:128, 0:2]: q.dma_start(o, d), reads=(), writes=(qsvr,))
        p.op("dve", lambda e, o=qsv[0:128, 0:1], i_=qsv[0:128, 0:1]:
             e.tensor_single_scalar(o, i_, 8.0, OP.mult), reads=(qsvr,), writes=(qsvr,))

        def xb_sh(kc, tt):
            off = PADL if kc < 4 else PADL - 1
            return ctx.XB[kc][0:128, off + tt * TTW: off + (tt + 1) * TTW]

        q_res = [A.carve(hh, "q%d" % hh) for hh in range(4)]
        k_res = [A.carve(4 + hh, "k%d" % hh) for hh in range(4)]
        vt_res = [A.carve(i % 4 if i < 16 else 4 + (i - 16), "vt%d" % i)
                  for i in range(20)]

        def q_ap(hc, c0, w, p0=0, np_=128):
            return A.bf16(hc, c0, w, p0=p0, np_=np_)

        def k_ap(hc, c0, w, p0=0, np_=128):
            return A.bf16(4 + hc, c0, w, p0=p0, np_=np_)

        def vt_ap(sp, c0, w, p0=0, np_=128):
            if sp < 16:
                return A.bf16(sp % 4, 2048 + (sp // 4) * 512 + c0, w, p0=p0, np_=np_)
            return A.bf16(4 + (sp - 16), 2560 + c0, w, p0=p0, np_=np_)

        for i in range(4):
            memset(p, "dve", vt_ap(i, 0, 512), 0.0, (vt_res[i],))
        for hc in range(4):
            memset(p, "dve", k_ap(hc, 0, 512), 0.0, (k_res[hc],))

        rtab, rtabr = ctx.small("l0f_rt", [128, 4 * TTW], BF16)
        cacc0, cacc0r = ctx.small("cacc0", [128, 512], F32)
        cacc1, cacc1r = ctx.small("cacc1", [128, 512], F32)
        sgq, sgqr = ctx.small("sg0", [128, 512], BF16)
        r2, r2r = ctx.small("l0f_r2", [2, TTW], F32)

        for tt in range(NTT):
            for ti in range(4):
                for half in range(2):
                    p.dma(lambda q, o=rtab[half * 64:(half + 1) * 64,
                                           ti * TTW:(ti + 1) * TTW],
                          d=W["l0f_rot"][0:64, ti * T + tt * TTW:
                                         ti * T + (tt + 1) * TTW]:
                          q.dma_start(o, d), reads=(), writes=(rtabr,))
            for oc in range(8):
                is_q = oc < 4
                ps, pr = ctx.psum.alloc()
                for kg0 in range(0, NCH, 4):
                    wq, wqr = ws.load(wv2(W["l0f_wqkv"], 1536, lay * D + kg0 * 128,
                                          4, oc * 128), 512)
                    for j in range(4):
                        kc = kg0 + j
                        mm(p, ps[:, 0:512], wq[0:128, j * 128:(j + 1) * 128],
                           xb_sh(kc, tt), (wqr, ctx.XBr[kc]), (pr,),
                           kc == 0, kc == NCH - 1)
                p.op("act", lambda e, o=sgq[0:128, :], i_=ps[:, 0:512]:
                     e.activation(o, i_, AF.Square), reads=(pr,), writes=(sgqr,))
                ps2, pr2 = ctx.psum.alloc()
                mm(p, ps2[0:2, 0:512], grp64b[0:128, 0:2], sgq[0:128, :],
                   (grp64br, sgqr), (pr2,), True, True)
                p.op("act", lambda e, o=r2[0:2, 0:TTW], i_=ps2[0:2, 0:512]:
                     e.activation(o, i_, AF.Sqrt), reads=(pr2,), writes=(r2r,))
                p.op("dve", lambda e, o=r2[0:2, 0:TTW], i_=r2[0:2, 0:TTW]:
                     e.tensor_single_scalar(o, i_, 1e-12, OP.max), reads=(r2r,),
                     writes=(r2r,))
                p.op("dve", lambda e, o=r2[0:2, 0:TTW], i_=r2[0:2, 0:TTW]:
                     e.reciprocal(o, i_), reads=(r2r,), writes=(r2r,))
                psB, prB = ctx.psum.alloc()
                mm(p, psB[0:128, 0:512], grp2f[0:2, 0:128], r2[0:2, 0:TTW],
                   (grp2fr, r2r), (prB,), True, True)
                p.op("dve", lambda e, o=cacc0[0:128, :], i_=ps[:, 0:512]:
                     e.tensor_copy(o, i_), reads=(pr,), writes=(cacc0r,))
                p.op("dve", lambda e, o=cacc0[0:128, :], i_=cacc0[0:128, :],
                     s=qsv[0:128, 0:1] if is_q else qsv[0:128, 1:2],
                     a=psB[0:128, 0:512]:
                     e.scalar_tensor_tensor(o, i_, s, a, OP.mult, OP.mult),
                     reads=(cacc0r, qsvr, prB), writes=(cacc0r,))
                p.op("dve", lambda e, o=sgq[0:128, :], i_=cacc0[0:128, :]:
                     e.tensor_copy(o, i_), reads=(cacc0r,), writes=(sgqr,))
                psR, prR = ctx.psum.alloc()
                mm(p, psR[0:128, 0:512], rotm[0:128, 0:128], sgq[0:128, :],
                   (rotmr, sgqr), (prR,), True, True)
                t0c = 0 if is_q else 2 * TTW
                p.op("dve", lambda e, o=cacc0[0:128, :], a=cacc0[0:128, :],
                     b=rtab[0:128, t0c:t0c + TTW]:
                     e.tensor_tensor(o, a, b, OP.mult), reads=(cacc0r, rtabr),
                     writes=(cacc0r,))
                p.op("dve", lambda e, o=cacc1[0:128, :], a=psR[0:128, 0:512],
                     b=rtab[0:128, t0c + TTW:t0c + 2 * TTW]:
                     e.tensor_tensor(o, a, b, OP.mult), reads=(prR, rtabr),
                     writes=(cacc1r,))
                if is_q:
                    dst, dres = q_ap(oc, tt * TTW, 512), q_res[oc]
                else:
                    dst, dres = k_ap(oc - 4, 512 + tt * TTW, 512), k_res[oc - 4]
                p.op("dve", lambda e, o=dst, a=cacc0[0:128, :], b=cacc1[0:128, :]:
                     e.tensor_tensor(o, a, b, OP.add), reads=(cacc0r, cacc1r),
                     writes=(dres,))

        for grp in range(2):
            pss = [ctx.psum.alloc() for _ in range(8)]
            for kc in range(NCH):
                wv, wvr = ws.load(W["l0f_wqkv"][lay * D + kc * 128:
                                                lay * D + (kc + 1) * 128,
                                                1024:1536], 512)
                off = PADL if kc < 4 else PADL - 1
                for tc in range(8):
                    tchunk = grp * 8 + tc
                    lhs = ctx.XB[kc][0:128, off + tchunk * 128: off + (tchunk + 1) * 128]
                    ps, pr = pss[tc]
                    mm(p, ps[:, 0:512], lhs, wv, (ctx.XBr[kc], wvr), (pr,),
                       kc == 0, kc == NCH - 1)
            for tc in range(8):
                tchunk = grp * 8 + tc
                ps, pr = pss[tc]
                slot = 4 + tchunk
                p.op("act", lambda e, o=vt_ap(slot, 0, 512), i_=ps[:, 0:512]:
                     e.copy(o, i_), reads=(pr,), writes=(vt_res[slot],))

        ebufs = [ctx.small("l0f_e%d" % s, [128, 640], F32) for s in range(2)]
        etbs = [ctx.small("l0f_et%d" % s, [128, 640], BF16) for s in range(2)]
        ett = [ctx.small("l0f_ett%d" % i, [128, 128], BF16) for i in range(4)]
        oodds = [ctx.small("l0f_oo%d" % s, [64, 128], BF16) for s in range(2)]
        mxs = [ctx.small("l0f_mx%d" % s, [128, 2], F32) for s in range(2)]
        qblk = []
        for hcq in range(4):
            row = []
            for b_ in range(16):
                r_ = Res("qb%d_%d_%d" % (lay, hcq, b_))
                r_.seed_from([q_res[hcq]])
                A.hist[hcq].append(r_)
                row.append(r_)
            qblk.append(row)
        it = 0
        for h in range(8):
            hc = h // 2
            hp = (h % 2) * 64
            for w_ in range(4):
                for qb in range(4):
                    par = it % 2
                    it += 1
                    ebuf, ebufr = ebufs[par]
                    etb, etbr = etbs[par]
                    oodd, ooddr = oodds[par]
                    mx, mxr = mxs[par]
                    t0 = 512 * w_ + 128 * qb
                    qbr = qblk[hc][t0 // 128]
                    ps1, pr1 = ctx.psum.alloc()
                    ps2, pr2 = ctx.psum.alloc()
                    qa = q_ap(hc, t0, 128, p0=hp, np_=64)
                    mm(p, ps1[0:128, 0:512], qa, k_ap(hc, t0, 512, p0=hp, np_=64),
                       (qbr, k_res[hc]), (pr1,), True, True)
                    mm(p, ps2[0:128, 0:128], qa, k_ap(hc, t0 + 512, 128, p0=hp, np_=64),
                       (qbr, k_res[hc]), (pr2,), True, True)
                    mi = (qb if w_ == 0 else 4)
                    p.op("dve", lambda e, o=ebuf[0:128, 0:512], a=ps1[0:128, 0:512],
                         b=mask[0:128, mi * 640:mi * 640 + 512]:
                         e.tensor_tensor(o, a, b, OP.add), reads=(pr1, maskr),
                         writes=(ebufr,))
                    p.op("dve", lambda e, o=ebuf[0:128, 512:640], a=ps2[0:128, 0:128],
                         b=mask[0:128, mi * 640 + 512:(mi + 1) * 640]:
                         e.tensor_tensor(o, a, b, OP.add), reads=(pr2, maskr),
                         writes=(ebufr,))
                    p.op("dve", lambda e, o=mx[0:128, 0:1], i_=ebuf[0:128, 0:640]:
                         e.tensor_reduce(o, i_, AX, OP.max, negate=True),
                         reads=(ebufr,), writes=(mxr,))
                    p.op("act", lambda e, o=etb[0:128, 0:640], i_=ebuf[0:128, 0:640],
                         b=mx[0:128, 0:1], ac=mx[0:128, 1:2]:
                         e.activation(o, i_, AF.Exp, bias=b, accum_out=ac),
                         reads=(ebufr, mxr), writes=(etbr, mxr))
                    p.op("dve", lambda e, o=mx[0:128, 1:2], i_=mx[0:128, 1:2]:
                         e.reciprocal(o, i_), reads=(mxr,), writes=(mxr,))
                    p.op("dve", lambda e, o=etb[0:128, 0:640], i_=etb[0:128, 0:640],
                         s=mx[0:128, 1:2]: e.tensor_scalar_mul(o, i_, s),
                         reads=(etbr, mxr), writes=(etbr,))
                    psO, prO = ctx.psum.alloc()
                    ptr_, prT = ctx.psum.alloc()
                    for j5 in range(5):
                        pstv = ptr_[0:128, j5 * 64:(j5 + 1) * 64].bitcast(BF16)
                        p.op("pe", lambda t, o=pstv,
                             i_=etb[0:128, j5 * 128:(j5 + 1) * 128],
                             idd=idn[0:128, 0:128]: t.transpose(o, i_, idd),
                             reads=(etbr, idnr), writes=(prT,))
                        et_t, et_r = ett[par * 2 + j5 % 2]
                        p.op("dve", lambda e, o=et_t[0:128, 0:128], i_=pstv:
                             e.tensor_copy(o, i_), reads=(prT,), writes=(et_r,))
                        sp = t0 // 128 + j5
                        mm(p, psO[0:64, 0:128], vt_ap(sp, h * 64, 64),
                           et_t[0:128, 0:128], (vt_res[sp], et_r), (prO,),
                           j5 == 0, j5 == 4)
                    if hp == 0:
                        p.op("dve", lambda e, o=q_ap(hc, t0, 128, p0=0, np_=64),
                             i_=psO[0:64, 0:128]: e.tensor_copy(o, i_),
                             reads=(prO,), writes=(qbr,))
                    else:
                        p.op("dve", lambda e, o=oodd[0:64, 0:128],
                             i_=psO[0:64, 0:128]:
                             e.tensor_copy(o, i_), reads=(prO,), writes=(ooddr,))
                        p.dma(lambda q, d=q_ap(hc, t0, 128, p0=64, np_=64),
                              s=oodd[0:64, 0:128]: q.dma_start(d, s),
                              reads=(ooddr,), writes=(qbr,))

        for oc in range(NCH):
            pss = [ctx.psum.alloc() for _ in range(NTT)]
            wo, wor = ws.load(wv2(W["l0f_wo"], D, lay * 512, 4, oc * 128), 512)
            for kc in range(4):
                for tt in range(NTT):
                    ps, pr = pss[tt]
                    mm(p, ps[:, 0:512], wo[0:128, kc * 128:(kc + 1) * 128],
                       q_ap(kc, tt * TTW, 512),
                       (wor,) + tuple(qblk[kc][tt * 4:(tt + 1) * 4]), (pr,),
                       kc == 0, kc == 3)
            for tt in range(NTT):
                ps, pr = pss[tt]
                xap = ctx.Xd_tt(oc, tt)
                p.op("dve", lambda e, o=xap, a=ps[:, 0:512], b=xap:
                     e.tensor_tensor(o, a, b, OP.add), reads=(pr, ctx.Xr[oc]),
                     writes=(ctx.Xr[oc],))
        for c in range(NCH):
            p.op("pool", lambda e, o=ctx.XBdata(c), i_=ctx.Xdata(c): e.tensor_copy(o, i_),
                 reads=(ctx.Xr[c],), writes=(ctx.XBr[c],))

        lnv, lnvr = ctx.small("l0f_ln", [128, 16], F32)
        for c in range(NCH):
            p.dma(lambda q, d=W["l0f_lnb"][lay * D + c * 128:lay * D + (c + 1) * 128,
                                           0:2], o=lnv[0:128, 2 * c:2 * c + 2]:
                  q.dma_start(o, d), reads=(), writes=(lnvr,))
        v43, v43r = ctx.small("l0f_b1", [128, 86], F32)
        for i in range(22):
            rows = 42 if i == 21 else 128
            p.dma(lambda q, d=W["l0f_b1c"][lay:lay + 1, i * 128:i * 128 + rows],
                  o=v43[0:rows, 2 * i + 0:2 * i + 1]: q.dma_start(o, d),
                  reads=(), writes=(v43r,))
            p.dma(lambda q, d=W["l0f_b1c"][lay:lay + 1,
                                           2730 + i * 128:2730 + i * 128 + rows],
                  o=v43[0:rows, 2 * i + 1:2 * i + 2]: q.dma_start(o, d),
                  reads=(), writes=(v43r,))
        b2row, b2rowr = ctx.small("l0f_b2", [1, 1024], BF16)
        for half in range(2):
            bw, bwr = ws.load_f32(W["l0f_b2"][lay:lay + 1, half * 512:(half + 1) * 512],
                                  512)
            p.op("dve", lambda e, o=b2row[0:1, half * 512:(half + 1) * 512], i_=bw:
                 e.tensor_copy(o, i_), reads=(bwr,), writes=(b2rowr,))

        def x_sh(c, tt, w=TTW):
            off = PADL if c < 4 else PADL - 1
            return ctx.X[c][0:128, off + tt * TTW: off + tt * TTW + w]

        def xb_sh2(c, tt):
            off = PADL if c < 4 else PADL - 1
            return ctx.XB[c][0:128, off + tt * TTW: off + (tt + 1) * TTW]

        sq_res = [A.carve(c, "ffnsq") for c in range(NCH)]

        def fsq_ap(c, tt):
            return A.bf16(c, tt * TTW, TTW)

        nrm_res = [A.carve(c, "nrm") for c in range(NCH)]

        def nrm_ap(c, tt):
            return A.bf16(c, 2048 + tt * TTW, TTW)

        for c in range(NCH):
            p.op("act", lambda e, o=A.bf16(c, 0, 2048), i_=x_sh(c, 0, 2048):
                 e.activation(o, i_, AF.Square), reads=(ctx.Xr[c],),
                 writes=(sq_res[c],))
        rowA, rowAr = (lambda t_, r_: (t_[0:1, 512:1024], r_))(*ctx.small("l3a_c", [16, T], F32))
        rowB, rowBr = (lambda t_, r_: (t_[0:1, 1024:1536], r_))(*ctx.small("l3a_c", [16, T], F32))
        rowC, rowCr = (lambda t_, r_: (t_[0:1, 0:512], r_))(*ctx.small("cacc1", [128, 512], F32))
        for tt in range(NTT):
            psA, prA = ctx.psum.alloc()
            psB, prB = ctx.psum.alloc()
            for c in range(NCH):
                mm(p, psA[0:1, 0:512], ctx.onescol_b, xb_sh2(c, tt),
                   (ctx.onescol_b_res, ctx.XBr[c]), (prA,), c == 0, c == NCH - 1)
            for c in range(NCH):
                mm(p, psB[0:1, 0:512], ctx.onescol_b, fsq_ap(c, tt),
                   (ctx.onescol_b_res, sq_res[c]), (prB,), c == 0, c == NCH - 1)
            p.op("dve", lambda e, o=rowA[0:1, :], i_=psA[0:1, 0:512]:
                 e.tensor_single_scalar(o, i_, 1.0 / D, OP.mult), reads=(prA,),
                 writes=(rowAr,))
            p.op("act", lambda e, o=rowC[0:1, :], i_=rowA[0:1, :]:
                 e.activation(o, i_, AF.Square), reads=(rowAr,), writes=(rowCr,))
            p.op("dve", lambda e, o=rowB[0:1, :], i_=psB[0:1, 0:512]:
                 e.tensor_single_scalar(o, i_, 1.0 / D, OP.mult), reads=(prB,),
                 writes=(rowBr,))
            p.op("dve", lambda e, o=rowB[0:1, :], a=rowB[0:1, :], b=rowC[0:1, :]:
                 e.tensor_tensor(o, a, b, OP.subtract), reads=(rowBr, rowCr),
                 writes=(rowBr,))
            p.op("act", lambda e, o=rowB[0:1, :], i_=rowB[0:1, :],
                 b=ctx.eps_col[0:1, 0:1]: e.activation(o, i_, AF.Ln, bias=b),
                 reads=(rowBr, ctx.onescol_f_res), writes=(rowBr,))
            p.op("act", lambda e, o=rowB[0:1, :], i_=rowB[0:1, :],
                 sc=ctx.mhalf_col[0:1, 0:1]: e.activation(o, i_, AF.Exp, scale=sc),
                 reads=(rowBr, ctx.onescol_f_res), writes=(rowBr,))
            p.op("dve", lambda e, o=rowA[0:1, :], a=rowA[0:1, :], b=rowB[0:1, :]:
                 e.tensor_tensor(o, a, b, OP.mult), reads=(rowAr, rowBr),
                 writes=(rowAr,))
            psR, prR = ctx.psum.alloc()
            psM, prM = ctx.psum.alloc()
            mm(p, psR[0:128, 0:512], ctx.ones_f, rowB[0:1, :],
               (ctx.ones_f_res, rowBr), (prR,), True, True)
            mm(p, psM[0:128, 0:512], ctx.negones_f, rowA[0:1, :],
               (ctx.negones_f_res, rowAr), (prM,), True, True)
            for c in range(NCH):
                p.op("dve", lambda e, o=cacc0[0:128, :], a=x_sh(c, tt),
                     b=psR[0:128, 0:512]: e.tensor_tensor(o, a, b, OP.mult),
                     reads=(ctx.Xr[c], prR), writes=(cacc0r,))
                p.op("dve", lambda e, o=cacc0[0:128, :], a=cacc0[0:128, :],
                     b=psM[0:128, 0:512]: e.tensor_tensor(o, a, b, OP.add),
                     reads=(cacc0r, prM), writes=(cacc0r,))
                p.op("dve", lambda e, o=nrm_ap(c, tt), i_=cacc0[0:128, :],
                     s1=lnv[0:128, 2 * c:2 * c + 1], s2=lnv[0:128, 2 * c + 1:2 * c + 2]:
                     e.tensor_scalar(o, i_, s1, s2, OP.mult, OP.add),
                     reads=(cacc0r, lnvr), writes=(nrm_res[c],))
        def mstore(mirror, mres_d, key, dview, wap, wr):
            if key not in mres_d:
                mres_d[key] = Res("mir" + str(key))
            p.dma(lambda q, d=dview, s=wap: q.dma_start(d, s), reads=(wr,),
                  writes=(mres_d[key],), q="act")

        h_res = [None] * 22

        def hq_ap(i, rows=128):
            return A.bf16(i % 8, (i // 8) * 512, 512, p0=0, np_=rows)

        for tt in range(NTT):
            for i in range(22):
                rows = 42 if i == 21 else 128
                psA, prA = ctx.psum.alloc()
                psB, prB = ctx.psum.alloc()
                for kg0 in range(0, NCH, 4):
                    if tt == 0:
                        wa, war = ws.load(wv2(W["l0f_w1"], 5460,
                                              lay * D + kg0 * 128, 4,
                                              i * 128, rows), 4 * rows)
                        mstore(ctx.dram_w1b, ctx.dram_w1b_res, ("a", lay, i, kg0),
                               wv2(ctx.dram_w1b, 5460, lay * D + kg0 * 128, 4,
                                   i * 128, rows), wa, war)
                        wg, wgr = ws.load(wv2(W["l0f_w1"], 5460,
                                              lay * D + kg0 * 128, 4,
                                              2730 + i * 128, rows), 4 * rows)
                        mstore(ctx.dram_w1b, ctx.dram_w1b_res, ("g", lay, i, kg0),
                               wv2(ctx.dram_w1b, 5460, lay * D + kg0 * 128, 4,
                                   2730 + i * 128, rows), wg, wgr)
                    else:
                        wa, war = ws.load_bf16(
                            wv2(ctx.dram_w1b, 5460, lay * D + kg0 * 128, 4,
                                i * 128, rows), 4 * rows,
                            extra_reads=(ctx.dram_w1b_res[("a", lay, i, kg0)],))
                        wg, wgr = ws.load_bf16(
                            wv2(ctx.dram_w1b, 5460, lay * D + kg0 * 128, 4,
                                2730 + i * 128, rows), 4 * rows,
                            extra_reads=(ctx.dram_w1b_res[("g", lay, i, kg0)],))
                    for j in range(4):
                        kc = kg0 + j
                        mm(p, psA[0:rows, 0:512],
                           wa[0:128, j * rows:(j + 1) * rows], nrm_ap(kc, tt),
                           (war, nrm_res[kc]), (prA,), kc == 0, kc == NCH - 1)
                        mm(p, psB[0:rows, 0:512],
                           wg[0:128, j * rows:(j + 1) * rows], nrm_ap(kc, tt),
                           (wgr, nrm_res[kc]), (prB,), kc == 0, kc == NCH - 1)
                esc, escr = (cacc0, cacc0r) if i % 2 == 0 else (cacc1, cacc1r)
                hr = A.carve(i % 8, "hq%d_%d" % (tt, i))
                p.op("act", lambda e, o=esc[0:rows, 0:512], i_=psB[0:rows, 0:512],
                     b=v43[0:rows, 2 * i + 1:2 * i + 2]:
                     e.activation(o, i_, AF.Gelu, bias=b),
                     reads=(prB, v43r), writes=(escr,))
                p.op("dve", lambda e, o=hq_ap(i, rows), i_=psA[0:rows, 0:512],
                     s=v43[0:rows, 2 * i:2 * i + 1], a=esc[0:rows, 0:512]:
                     e.scalar_tensor_tensor(o, i_, s, a, OP.add, OP.mult),
                     reads=(prA, v43r, escr), writes=(hr,))
                h_res[i] = hr
            for oc in range(NCH):
                ps, pr = ctx.psum.alloc()
                for kg0 in range(0, 20, 4):
                    if tt == 0:
                        w2, w2r = ws.load(wv2(W["l0f_w2"], D,
                                              lay * 2730 + kg0 * 128,
                                              4, oc * 128), 512)
                        mstore(ctx.dram_w2b, ctx.dram_w2b_res, (lay, kg0, oc),
                               wv2(ctx.dram_w2b, D, lay * 2730 + kg0 * 128, 4,
                                   oc * 128), w2, w2r)
                    else:
                        w2, w2r = ws.load_bf16(
                            wv2(ctx.dram_w2b, D, lay * 2730 + kg0 * 128, 4,
                                oc * 128), 512,
                            extra_reads=(ctx.dram_w2b_res[(lay, kg0, oc)],))
                    for j in range(4):
                        i = kg0 + j
                        mm(p, ps[:, 0:512], w2[0:128, j * 128:(j + 1) * 128],
                           hq_ap(i, 128), (w2r, h_res[i]), (pr,), i == 0, False)
                for i in (20, 21):
                    rows = 42 if i == 21 else 128
                    if tt == 0:
                        w2, w2r = ws.load(W["l0f_w2"][lay * 2730 + i * 128:
                                                      lay * 2730 + i * 128 + rows,
                                                      oc * 128:(oc + 1) * 128], 128)
                        mstore(ctx.dram_w2b, ctx.dram_w2b_res, (lay, i, oc),
                               ctx.dram_w2b[lay * 2730 + i * 128:
                                            lay * 2730 + i * 128 + rows,
                                            oc * 128:(oc + 1) * 128], w2, w2r)
                    else:
                        w2, w2r = ws.load_bf16(
                            ctx.dram_w2b[lay * 2730 + i * 128:
                                         lay * 2730 + i * 128 + rows,
                                         oc * 128:(oc + 1) * 128], 128,
                            extra_reads=(ctx.dram_w2b_res[(lay, i, oc)],))
                    mm(p, ps[:, 0:512], w2[0:rows, 0:128], hq_ap(i, rows),
                       (w2r, h_res[i]), (pr,), False, False)
                mm(p, ps[:, 0:512], b2row[0:1, oc * 128:(oc + 1) * 128],
                   ctx.onesrow_b, (b2rowr, ctx.onesrow_b_res), (pr,), False, True)
                xap = ctx.Xd_tt(oc, tt)
                p.op("dve", lambda e, o=xap, a=ps[:, 0:512], b=xap:
                     e.tensor_tensor(o, a, b, OP.add), reads=(pr, ctx.Xr[oc]),
                     writes=(ctx.Xr[oc],))
        if lay == 0:
            for c in range(NCH):
                p.op("pool", lambda e, o=ctx.XBdata(c), i_=ctx.Xdata(c):
                     e.tensor_copy(o, i_), reads=(ctx.Xr[c],), writes=(ctx.XBr[c],))

    xin_res = [A.carve(c, "xin") for c in range(NCH)]
    extra = []
    for c in range(NCH):
        ap = A.f32(c, 0, T)
        p.dma(lambda q, o=ap, d=ctx.dram_xin[c * 128:(c + 1) * 128, 0:T]:
              q.dma_start(o, d), reads=(ctx.dram_xin_res,), writes=(xin_res[c],))
        extra.append((ap, xin_res[c]))
    emit_postln(ctx, extra=extra)


def emit_l2_ffn(ctx):
    p, A, ws, W = ctx.p, ctx.arena, ctx.ws, ctx.weights
    A.reset()
    ws_extra(ctx)
    NFC = 4096 * 1024
    gvec, gvr = ctx.small("l2f_g", [128, 4], F32)
    gsum, gsr = ctx.small("l2f_gs", [128, 2], F32)
    junk, junkr = ctx.small("cacc0", [128, 512], F32)
    v32, v32r = ctx.small("l2f_ln", [128, 64], F32)
    for c in range(32):
        p.dma(lambda q, d=W["l2f_ln"][c * 128:(c + 1) * 128, 0:2],
              o=v32[0:128, 2 * c:2 * c + 2]: q.dma_start(o, d), reads=(), writes=(v32r,))

    def quantize(wname, rows, cols, dram_q, dram_q_res, gc0, gc1):
        nkc = rows // 128
        npc = cols // 512
        memset(p, "dve", gsum[0:128, 0:1], 0.0, (gsr,))
        for kc in range(nkc):
            for pc in range(npc):
                wf, wfr = ws.load_f32(W[wname][kc * 128:(kc + 1) * 128,
                                               pc * 512:(pc + 1) * 512], 512)
                p.op("act", lambda e, o=junk[0:128, :], i_=wf, ac=gsum[0:128, 1:2]:
                     e.activation(o, i_, AF.Abs, accum_out=ac),
                     reads=(wfr,), writes=(junkr, gsr))
                p.op("dve", lambda e, o=gsum[0:128, 0:1], a=gsum[0:128, 0:1],
                     b=gsum[0:128, 1:2]: e.tensor_tensor(o, a, b, OP.add),
                     reads=(gsr,), writes=(gsr,))
        pst, pstr = ctx.psum.alloc()
        mm(p, pst[0:1, 0:1], ctx.onescol_f, gsum[0:128, 0:1],
           (ctx.onescol_f_res, gsr), (pstr,), True, True)
        p.op("dve", lambda e, o=gsum[0:1, 1:2], i_=pst[0:1, 0:1]:
             e.tensor_single_scalar(o, i_, 1.0 / NFC, OP.mult), reads=(pstr,),
             writes=(gsr,))
        p.op("dve", lambda e, o=gsum[0:1, 1:2], i_=gsum[0:1, 1:2]:
             e.tensor_single_scalar(o, i_, 1e-6, OP.add), reads=(gsr,), writes=(gsr,))
        p.op("dve", lambda e, o=gsum[0:1, 1:2], i_=gsum[0:1, 1:2]:
             e.tensor_single_scalar(o, i_, 0.5, OP.mult), reads=(gsr,), writes=(gsr,))
        psb, psbr = ctx.psum.alloc()
        mm(p, psb[0:128, 0:1], ctx.ones_f, gsum[0:1, 1:2],
           (ctx.ones_f_res, gsr), (psbr,), True, True)
        p.op("dve", lambda e, o=gvec[0:128, gc0:gc0 + 1], i_=psb[0:128, 0:1]:
             e.tensor_copy(o, i_), reads=(psbr,), writes=(gvr,))
        p.op("dve", lambda e, o=gvec[0:128, gc1:gc1 + 1], i_=gvec[0:128, gc0:gc0 + 1]:
             e.tensor_single_scalar(o, i_, -1.0, OP.mult), reads=(gvr,), writes=(gvr,))
        s1, s1r = ctx.small("sg0", [128, 512], BF16)
        s2, s2r = ctx.small("sg1", [128, 512], BF16)
        s3, s3r = ctx.small("sg2", [128, 512], BF16)
        for kc in range(nkc):
            for pc in range(npc):
                wf, wfr = ws.load_f32(W[wname][kc * 128:(kc + 1) * 128,
                                               pc * 512:(pc + 1) * 512], 512)
                p.op("act", lambda e, o=s1[0:128, :], i_=wf, b=gvec[0:128, gc0:gc0 + 1]:
                     e.activation(o, i_, AF.Sign, bias=b), reads=(wfr, gvr),
                     writes=(s1r,))
                p.op("act", lambda e, o=s2[0:128, :], i_=wf, b=gvec[0:128, gc1:gc1 + 1]:
                     e.activation(o, i_, AF.Sign, bias=b), reads=(wfr, gvr),
                     writes=(s2r,))
                p.op("dve", lambda e, o=s3[0:128, :], a=s1[0:128, :], b=s2[0:128, :]:
                     e.tensor_tensor(o, a, b, OP.add), reads=(s1r, s2r), writes=(s3r,))
                key = (kc, pc)
                if key not in dram_q_res:
                    dram_q_res[key] = Res(wname + str(key))
                p.dma(lambda q, d=dram_q[kc * 128:(kc + 1) * 128,
                                         pc * 512:(pc + 1) * 512], s=s3[0:128, :]:
                      q.dma_start(d, s), reads=(s3r,), writes=(dram_q_res[key],))

    quantize("l2f_wfcT", 1024, 4096, ctx.dram_q1, ctx.dram_q1_res, 0, 1)
    quantize("l2f_wprojT", 4096, 1024, ctx.dram_q2, ctx.dram_q2_res, 2, 3)

    rowA, rowAr = (lambda t_, r_: (t_[0:1, 512:1024], r_))(*ctx.small("l3a_c", [16, T], F32))
    rowB, rowBr = (lambda t_, r_: (t_[0:1, 1024:1536], r_))(*ctx.small("l3a_c", [16, T], F32))
    rowC, rowCr = (lambda t_, r_: (t_[0:1, 0:512], r_))(*ctx.small("cacc1", [128, 512], F32))
    h_res = [A.carve(t, "h") for t in range(8)]
    sq_res = [A.carve(t, "hsq") for t in range(8)]

    def h_ap(oc, tt):
        return A.bf16(oc % 8, (oc // 8) * 512, 512)

    def hsq_ap(oc, tt):
        return A.bf16(oc % 8, 2048 + (oc // 8) * 512, 512)

    for tt in range(NTT):
        for oc in range(32):
            ps, pr = ctx.psum.alloc()
            wq, wqr = ws.load_bf16(
                wv2(ctx.dram_q1, 4096, 0, 8, oc * 128), 1024,
                extra_reads=tuple(ctx.dram_q1_res[(kc, oc // 4)]
                                  for kc in range(NCH)))
            for kc in range(NCH):
                mm(p, ps[:, 0:512], wq[0:128, kc * 128:(kc + 1) * 128],
                   ctx.XBd_tt(kc, tt), (wqr, ctx.XBr[kc]),
                   (pr,), kc == 0, kc == NCH - 1)
            p.op("act", lambda e, o=h_ap(oc, tt), i_=ps[:, 0:512],
                 sc=ctx.half_col[0:128, 0:1]: e.activation(o, i_, AF.Gelu, scale=sc),
                 reads=(pr, ctx.onescol_f_res), writes=(h_res[oc % 8],))
            p.op("act", lambda e, o=hsq_ap(oc, tt), i_=h_ap(oc, tt):
                 e.activation(o, i_, AF.Square), reads=(h_res[oc % 8],),
                 writes=(sq_res[oc % 8],))
        psA, prA = ctx.psum.alloc()
        psB, prB = ctx.psum.alloc()
        for oc in range(32):
            mm(p, psA[0:1, 0:512], ctx.onescol_b, h_ap(oc, tt),
               (ctx.onescol_b_res, h_res[oc % 8]), (prA,), oc == 0, oc == 31)
        for oc in range(32):
            mm(p, psB[0:1, 0:512], ctx.onescol_b, hsq_ap(oc, tt),
               (ctx.onescol_b_res, sq_res[oc % 8]), (prB,), oc == 0, oc == 31)
        p.op("dve", lambda e, o=rowA[0:1, :], i_=psA[0:1, 0:512]:
             e.tensor_single_scalar(o, i_, 1.0 / 4096, OP.mult), reads=(prA,),
             writes=(rowAr,))
        p.op("act", lambda e, o=rowC[0:1, :], i_=rowA[0:1, :]:
             e.activation(o, i_, AF.Square), reads=(rowAr,), writes=(rowCr,))
        p.op("dve", lambda e, o=rowB[0:1, :], i_=psB[0:1, 0:512]:
             e.tensor_single_scalar(o, i_, 1.0 / 4096, OP.mult), reads=(prB,),
             writes=(rowBr,))
        p.op("dve", lambda e, o=rowB[0:1, :], a=rowB[0:1, :], b=rowC[0:1, :]:
             e.tensor_tensor(o, a, b, OP.subtract), reads=(rowBr, rowCr),
             writes=(rowBr,))
        p.op("act", lambda e, o=rowB[0:1, :], i_=rowB[0:1, :], b=ctx.eps_col[0:1, 0:1]:
             e.activation(o, i_, AF.Ln, bias=b), reads=(rowBr, ctx.onescol_f_res),
             writes=(rowBr,))
        p.op("act", lambda e, o=rowB[0:1, :], i_=rowB[0:1, :],
             sc=ctx.mhalf_col[0:1, 0:1]: e.activation(o, i_, AF.Exp, scale=sc),
             reads=(rowBr, ctx.onescol_f_res), writes=(rowBr,))
        p.op("dve", lambda e, o=rowA[0:1, :], a=rowA[0:1, :], b=rowB[0:1, :]:
             e.tensor_tensor(o, a, b, OP.mult), reads=(rowAr, rowBr), writes=(rowAr,))
        psR, prR = ctx.psum.alloc()
        psM, prM = ctx.psum.alloc()
        mm(p, psR[0:128, 0:512], ctx.ones_f, rowB[0:1, :],
           (ctx.ones_f_res, rowBr), (prR,), True, True)
        mm(p, psM[0:128, 0:512], ctx.negones_f, rowA[0:1, :],
           (ctx.negones_f_res, rowAr), (prM,), True, True)
        for oc in range(32):
            wcol = v32[0:128, 2 * (oc):2 * oc + 1]
            bcol = v32[0:128, 2 * oc + 1:2 * oc + 2]
            p.op("dve", lambda e, o=h_ap(oc, tt), i_=h_ap(oc, tt), s=wcol,
                 a=psR[0:128, 0:512]: e.scalar_tensor_tensor(o, i_, s, a, OP.mult,
                                                             OP.mult),
                 reads=(h_res[oc % 8], v32r, prR), writes=(h_res[oc % 8],))
            p.op("dve", lambda e, o=junk[0:128, :], i_=psM[0:128, 0:512], s1=wcol,
                 s2=bcol: e.tensor_scalar(o, i_, s1, s2, OP.mult, OP.add),
                 reads=(prM, v32r), writes=(junkr,))
            p.op("dve", lambda e, o=h_ap(oc, tt), a=h_ap(oc, tt), b=junk[0:128, :]:
                 e.tensor_tensor(o, a, b, OP.add), reads=(h_res[oc % 8], junkr),
                 writes=(h_res[oc % 8],))
        for oc in range(NCH):
            ps, pr = ctx.psum.alloc()
            for rb in range(4):
                wq, wqr = ws.load_bf16(
                    wv2(ctx.dram_q2, D, rb * 1024, 8, oc * 128), 1024,
                    extra_reads=tuple(ctx.dram_q2_res[(rb * 8 + kk, oc // 4)]
                                      for kk in range(8)))
                for kk in range(8):
                    kc = rb * 8 + kk
                    mm(p, ps[:, 0:512], wq[0:128, kk * 128:(kk + 1) * 128],
                       h_ap(kc, tt), (wqr, h_res[kc % 8]),
                       (pr,), kc == 0, kc == 31)
            xap = ctx.Xd_tt(oc, tt)
            p.op("dve", lambda e, o=xap, i_=ps[:, 0:512], s=ctx.half_col[0:128, 0:1],
                 a=xap: e.scalar_tensor_tensor(o, i_, s, a, OP.mult, OP.add),
                 reads=(pr, ctx.onescol_f_res, ctx.Xr[oc]), writes=(ctx.Xr[oc],))
    emit_postln(ctx)


def emit_l3_ffn(ctx):
    p, A, ws, W = ctx.p, ctx.arena, ctx.ws, ctx.weights
    A.reset()
    ws_extra(ctx)
    v32s, v32sr = ctx.small("l3f_s", [128, 128], F32)
    for c in range(32):
        p.dma(lambda q, d=W["l3f_s4"][c * 128:(c + 1) * 128, 0:4],
              o=v32s[0:128, 4 * c:4 * c + 4]: q.dma_start(o, d), reads=(), writes=(v32sr,))
    w3v, w3vr = ctx.small("l3f_w3s", [128, 8], F32)
    for c in range(NCH):
        p.dma(lambda q, d=W["l3f_w3s"][c * 128:(c + 1) * 128, 0:1],
              o=w3v[0:128, c:c + 1]: q.dma_start(o, d), reads=(), writes=(w3vr,))

    sgt, sgtr = ctx.small("sg0", [128, 512], BF16)
    cc0, cc0r = ctx.small("cacc0", [128, 512], F32)
    for oc in range(32):
        psA = [ctx.psum.alloc() for _ in range(NTT)]
        psB = [ctx.psum.alloc() for _ in range(NTT)]
        w1b, w1br = ws.load(wide_view(W["l3f_w1T"], 4096, 0, 4, oc), 512)
        w1b2, w1b2r = ws.load(wide_view(W["l3f_w1T"], 4096, 512, 4, oc), 512)
        for kc in range(NCH):
            src, srcr = (w1b, w1br) if kc < 4 else (w1b2, w1b2r)
            lh = src[0:128, (kc % 4) * 128:(kc % 4 + 1) * 128]
            for tt in range(NTT):
                mm(p, psA[tt][0][:, 0:512], lh, ctx.XBd_tt(kc, tt),
                   (srcr, ctx.XBr[kc]), (psA[tt][1],), kc == 0, kc == NCH - 1)
        w2b, w2br = ws.load(wide_view(W["l3f_w2T"], 4096, 0, 4, oc), 512)
        w2b2, w2b2r = ws.load(wide_view(W["l3f_w2T"], 4096, 512, 4, oc), 512)
        for kc in range(NCH):
            src2, src2r = (w2b, w2br) if kc < 4 else (w2b2, w2b2r)
            lh2 = src2[0:128, (kc % 4) * 128:(kc % 4 + 1) * 128]
            for tt in range(NTT):
                mm(p, psB[tt][0][:, 0:512], lh2, ctx.XBd_tt(kc, tt),
                   (src2r, ctx.XBr[kc]), (psB[tt][1],), kc == 0, kc == NCH - 1)
        for tt in range(NTT):
            sccol = v32s[0:128, 4 * oc:4 * oc + 1]
            sccol2 = v32s[0:128, 4 * oc + 1:4 * oc + 2]
            p.op("act", lambda e, o=cc0[0:128, :], i_=psA[tt][0][:, 0:512], sc=sccol:
                 e.activation(o, i_, AF.Silu, scale=sc),
                 reads=(psA[tt][1], v32sr), writes=(cc0r,))
            p.op("dve", lambda e, o=sgt[0:128, :], i_=psB[tt][0][:, 0:512], s=sccol2,
                 a=cc0[0:128, :]: e.scalar_tensor_tensor(o, i_, s, a, OP.mult, OP.mult),
                 reads=(psB[tt][1], v32sr, cc0r), writes=(sgtr,))
            p.dma(lambda q, d=ctx.dram_h[oc * 128:(oc + 1) * 128,
                                         tt * TTW:(tt + 1) * TTW], s=sgt[0:128, :]:
                  q.dma_start(d, s), reads=(sgtr,), writes=(ctx.dram_h_res,))

    hA_res = [A.carve(t, "hA") for t in range(8)]
    hB_res = [A.carve(t, "hB") for t in range(8)]

    def slotA(oc):
        return A.bf16(oc % 8, (oc // 8) * 512, 512)

    def slotB(oc):
        return A.bf16(oc % 8, 2048 + (oc // 8) * 512, 512)

    for tt in range(NTT):
        for oc in range(32):
            p.dma(lambda q, o=slotA(oc),
                  d=ctx.dram_h[oc * 128:(oc + 1) * 128, tt * TTW:(tt + 1) * TTW]:
                  q.dma_start(o, d), reads=(ctx.dram_h_res,), writes=(hA_res[oc % 8],))
        def mres(mirror_res, oc, rb, hf):
            key = (oc, rb, hf)
            if key not in mirror_res:
                mirror_res[key] = Res("mir%d_%d_%d" % key)
            return mirror_res[key]

        def wstream_mirrored(tt, w_dram, wtot, mirror, mirror_res, oc, rhs_fn,
                             rhs_res_fn, ps, pr, nkc):
            if tt == 0:
                for rb in range(nkc // 8):
                    for hf in range(2):
                        wb, wbr = ws.load(
                            wide_view(w_dram, wtot, rb * 1024 + hf * 512, 4, oc), 512)
                        p.dma(lambda q, d=wide_view(mirror, wtot,
                                                    rb * 1024 + hf * 512, 4, oc),
                              s=wb: q.dma_start(d, s), reads=(wbr,),
                              writes=(mres(mirror_res, oc, rb, hf),), q="act")
                        for kk in range(4):
                            kc = rb * 8 + hf * 4 + kk
                            lh = wb[0:128, kk * 128:(kk + 1) * 128]
                            mm(p, ps[:, 0:512], lh, rhs_fn(kc),
                               (wbr, rhs_res_fn(kc)), (pr,), kc == 0, kc == nkc - 1)
            else:
                for rb in range(nkc // 8):
                    wb, wbr = ws.load_bf16(
                        wide_view(mirror, wtot, rb * 1024, 8, oc), 1024,
                        extra_reads=(mres(mirror_res, oc, rb, 0),
                                     mres(mirror_res, oc, rb, 1)))
                    for kk in range(8):
                        kc = rb * 8 + kk
                        lh = wb[0:128, kk * 128:(kk + 1) * 128]
                        mm(p, ps[:, 0:512], lh, rhs_fn(kc),
                           (wbr, rhs_res_fn(kc)), (pr,), kc == 0, kc == nkc - 1)

        for oc in range(32):
            ps, pr = ctx.psum.alloc()
            wstream_mirrored(tt, W["l3f_h1T"], 4096, ctx.dram_h1b, ctx.dram_h1b_res,
                             oc, slotA, lambda kc: hA_res[kc % 8], ps, pr, 32)
            p.op("act", lambda e, o=slotB(oc), i_=ps[:, 0:512],
                 sc=v32s[0:128, 4 * oc + 2:4 * oc + 3]:
                 e.activation(o, i_, AF.Gelu, scale=sc),
                 reads=(pr, v32sr), writes=(hB_res[oc % 8],))
        for oc in range(32):
            ps, pr = ctx.psum.alloc()
            wstream_mirrored(tt, W["l3f_h2T"], 4096, ctx.dram_h2b, ctx.dram_h2b_res,
                             oc, slotB, lambda kc: hB_res[kc % 8], ps, pr, 32)
            p.op("act", lambda e, o=slotA(oc), i_=ps[:, 0:512],
                 sc=v32s[0:128, 4 * oc + 3:4 * oc + 4]:
                 e.activation(o, i_, AF.Gelu, scale=sc),
                 reads=(pr, v32sr), writes=(hA_res[oc % 8],))
        for oc in range(NCH):
            ps, pr = ctx.psum.alloc()
            wstream_mirrored(tt, W["l3f_w3T"], D, ctx.dram_w3b, ctx.dram_w3b_res,
                             oc, slotA, lambda kc: hA_res[kc % 8], ps, pr, 32)
            xap = ctx.Xd_tt(oc, tt)
            p.op("dve", lambda e, o=xap, i_=ps[:, 0:512], s=w3v[0:128, oc:oc + 1],
                 a=xap: e.scalar_tensor_tensor(o, i_, s, a, OP.mult, OP.add),
                 reads=(pr, w3vr, ctx.Xr[oc]), writes=(ctx.Xr[oc],))
    emit_postln(ctx)


def emit_l3_attn(ctx):
    p, A, ws, W = ctx.p, ctx.arena, ctx.ws, ctx.weights
    A.reset()
    NB = 16

    v7 = [ctx.small("l3a_tv%d" % c, [128, 8], F32) for c in range(NCH)]
    for c in range(NCH):
        vt, vr = v7[c]
        p.dma(lambda q, d=W["l3a_tvec"][c * 128:(c + 1) * 128, 0:7], o=vt[0:128, 0:7]:
              q.dma_start(o, d), reads=(), writes=(vr,))
    dmv = [ctx.small("l3a_dm%d" % c, [128, 2], F32) for c in range(NCH)]
    for c in range(NCH):
        vt, vr = dmv[c]
        p.dma(lambda q, d=W["l3a_dmix"][c * 128:(c + 1) * 128, 0:1], o=vt[0:128, 0:1]:
              q.dma_start(o, d), reads=(), writes=(vr,))
        p.op("act", lambda e, o=vt[0:128, 1:2], i_=vt[0:128, 0:1]:
             e.activation(o, i_, AF.Sigmoid), reads=(vr,), writes=(vr,))
    lnv, lnvr = ctx.small("l3a_ln", [128, 16], F32)
    for c in range(NCH):
        p.dma(lambda q, d=W["l3a_lng"][c * 128:(c + 1) * 128, 0:2],
              o=lnv[0:128, 2 * c:2 * c + 2]: q.dma_start(o, d), reads=(), writes=(lnvr,))
    hrow, hrowr = ctx.small("l3a_hr", [1, 48], F32)
    p.dma(lambda q, d=W["l3a_hrow"][0:1, 0:32], o=hrow[0:1, 0:32]: q.dma_start(o, d),
          reads=(), writes=(hrowr,))
    p.op("act", lambda e, o=hrow[0:1, 32:48], i_=hrow[0:1, 0:16]:
         e.activation(o, i_, AF.Exp), reads=(hrowr,), writes=(hrowr,))
    p.op("act", lambda e, o=hrow[0:1, 16:32], i_=hrow[0:1, 16:32]:
         e.activation(o, i_, AF.Sigmoid), reads=(hrowr,), writes=(hrowr,))
    shb, shbr = ctx.small("l3a_shb", [128, 32], F32)
    psx, psxr = ctx.psum.alloc()
    mm(p, psx[0:128, 0:16], ctx.ones_f, hrow[0:1, 16:32], (ctx.ones_f_res, hrowr),
       (psxr,), True, True)
    p.op("dve", lambda e, o=shb[0:128, 0:16], i_=psx[0:128, 0:16]: e.tensor_copy(o, i_),
         reads=(psxr,), writes=(shbr,))
    p.op("dve", lambda e, o=shb[0:128, 16:32], i_=shb[0:128, 0:16]:
         e.tensor_single_scalar(o, i_, -1.0, OP.mult), reads=(shbr,), writes=(shbr,))
    scol, scolr = ctx.small("l3a_scol", [16, 2], F32)
    pst_, pstr_ = ctx.psum.alloc()
    p.op("pe", lambda t, o=pst_[0:16, 0:1], i_=hrow[0:1, 16:32],
         idd=ctx.identf[0:1, 0:1]: t.transpose(o, i_, idd),
         reads=(hrowr, ctx.identf_res), writes=(pstr_,))
    p.op("dve", lambda e, o=scol[0:16, 0:1], i_=pst_[0:16, 0:1]: e.tensor_copy(o, i_),
         reads=(pstr_,), writes=(scolr,))
    gcs, gcsr = ctx.small("l3a_gcs", [128, 128], BF16)
    psg, psgr = ctx.psum.alloc()
    mm(p, psg[0:128, 0:16], ctx.ones_f, hrow[0:1, 32:48], (ctx.ones_f_res, hrowr),
       (psgr,), True, True)
    for c in range(NCH):
        p.op("dve", lambda e, o=gcs[0:128, 16 * c:16 * c + 16],
             a=ctx.l3a_gmask[0:128, 16 * c:16 * c + 16], b=psg[0:128, 0:16]:
             e.tensor_tensor(o, a, b, OP.mult), reads=(ctx.l3a_gmask_res, psgr),
             writes=(gcsr,))

    dx_res = [A.carve(c, "dx") for c in range(NCH)]

    def dx_ap(c, tt):
        return A.f32(c, tt * TTW, TTW)

    cc0, cc0r = ctx.small("cacc0", [128, 512], F32)
    for c in range(NCH):
        for tt in range(NTT):
            prev = ctx.X[c][0:128, PADL - 1 + tt * TTW: PADL - 1 + (tt + 1) * TTW]
            nxt = ctx.X[c][0:128, PADL + 1 + tt * TTW: PADL + 1 + (tt + 1) * TTW]
            p.op("dve", lambda e, o=cc0[0:128, :], a=prev, b=nxt:
                 e.tensor_tensor(o, a, b, OP.add), reads=(ctx.Xr[c],), writes=(cc0r,))
            p.op("dve", lambda e, o=dx_ap(c, tt), i_=cc0[0:128, :],
                 s=ctx.half_col[0:128, 0:1], a=ctx.Xd_tt(c, tt):
                 e.scalar_tensor_tensor(o, i_, s, a, OP.mult, OP.subtract),
                 reads=(cc0r, ctx.onescol_f_res, ctx.Xr[c]), writes=(dx_res[c],))

    for c in range(NCH):
        vt, vr = v7[c]
        for tt in range(NTT):
            p.op("dve", lambda e, o=ctx.XBd_tt(c, tt), i_=dx_ap(c, tt),
                 s=vt[0:128, 0:1], a=ctx.Xd_tt(c, tt):
                 e.scalar_tensor_tensor(o, i_, s, a, OP.mult, OP.add),
                 reads=(dx_res[c], vr, ctx.Xr[c]), writes=(ctx.XBr[c],))
    sg0, sg0r = ctx.small("sg0", [128, 512], BF16)
    m32, m32r = sg0[0:32, 0:512], sg0r
    for tt in range(NTT):
        psA, prA = ctx.psum.alloc()
        psB, prB = ctx.psum.alloc()
        for kc in range(NCH):
            wq, wqr = ws.load(W["l3a_maa_w1"][kc * 128:(kc + 1) * 128, 0:128], 128)
            wq2, wq2r = ws.load(W["l3a_maa_w1"][kc * 128:(kc + 1) * 128, 128:160], 32)
            mm(p, psA[:, 0:512], wq, ctx.XBd_tt(kc, tt), (wqr, ctx.XBr[kc]),
               (prA,), kc == 0, kc == NCH - 1)
            mm(p, psB[0:32, 0:512], wq2, ctx.XBd_tt(kc, tt), (wq2r, ctx.XBr[kc]),
               (prB,), kc == 0, kc == NCH - 1)
        p.op("act", lambda e, o=sg0[0:128, :], i_=psA[:, 0:512]:
             e.activation(o, i_, AF.Tanh), reads=(prA,), writes=(sg0r,))
        p.dma(lambda q, d=ctx.dram_mx[0:128, tt * TTW:(tt + 1) * TTW], s=sg0[0:128, :]:
              q.dma_start(d, s), reads=(sg0r,), writes=(ctx.dram_mx_res,))
        p.op("act", lambda e, o=m32[0:32, :], i_=psB[0:32, 0:512]:
             e.activation(o, i_, AF.Tanh), reads=(prB,), writes=(m32r,))
        p.dma(lambda q, d=ctx.dram_mx[128:160, tt * TTW:(tt + 1) * TTW], s=m32[0:32, :]:
              q.dma_start(d, s), reads=(m32r,), writes=(ctx.dram_mx_res,))

    def build_xf(f, tmcol):
        for tt in range(NTT):
            mxf, mxfr = sg0[0:32, 0:512], sg0r
            p.dma(lambda q, o=mxf, d=ctx.dram_mx[32 * f:32 * f + 32,
                                                 tt * TTW:(tt + 1) * TTW]:
                  q.dma_start(o, d), reads=(ctx.dram_mx_res,), writes=(sg0r,))
            for c in range(NCH):
                vt, vr = v7[c]
                w2s, w2sr = ws.load(W["l3a_maa_w2"][32 * f:32 * f + 32,
                                                    c * 128:(c + 1) * 128], 128)
                psf, prf = ctx.psum.alloc()
                mm(p, psf[:, 0:512], w2s, mxf, (w2sr, mxfr), (prf,), True, True)
                p.op("dve", lambda e, o=cc0[0:128, :], i_=psf[:, 0:512],
                     s=vt[0:128, tmcol:tmcol + 1], a=dx_ap(c, tt):
                     e.scalar_tensor_tensor(o, i_, s, a, OP.add, OP.mult),
                     reads=(prf, vr, dx_res[c]), writes=(cc0r,))
                p.op("dve", lambda e, o=ctx.XBd_tt(c, tt), a=cc0[0:128, :],
                     b=ctx.Xd_tt(c, tt): e.tensor_tensor(o, a, b, OP.add),
                     reads=(cc0r, ctx.Xr[c]), writes=(ctx.XBr[c],))

    crow, crowr = ctx.small("l3a_c", [16, T], F32)
    t64, t64r = (lambda t_, r_: (t_, r_))(*ctx.small("sg1", [128, 512], BF16))
    build_xf(0, 1)
    for tt in range(NTT):
        ps64, pr64 = ctx.psum.alloc()
        for kc in range(NCH):
            wtd, wtdr = ws.load(W["l3a_td_w1"][kc * 128:(kc + 1) * 128, 0:64], 64)
            mm(p, ps64[0:64, 0:512], wtd, ctx.XBd_tt(kc, tt), (wtdr, ctx.XBr[kc]),
               (pr64,), kc == 0, kc == NCH - 1)
        p.op("act", lambda e, o=t64[0:64, :], i_=ps64[0:64, 0:512]:
             e.activation(o, i_, AF.Tanh), reads=(pr64,), writes=(t64r,))
        psC, prC = ctx.psum.alloc(pin=True)
        for c in range(NCH):
            vt, vr = v7[c]
            wtd2, wtd2r = ws.load(W["l3a_td_w2"][0:64, c * 128:(c + 1) * 128], 128)
            psw, prw = ctx.psum.alloc()
            mm(p, psw[:, 0:512], wtd2, t64[0:64, :], (wtd2r, t64r), (prw,), True, True)
            p.op("act", lambda e, o=sg0[0:128, :], i_=psw[:, 0:512], b=vt[0:128, 6:7]:
                 e.activation(o, i_, AF.Exp, bias=b), reads=(prw, vr), writes=(sg0r,))
            mm(p, psC[0:16, 0:512], gcs[0:128, 16 * c:16 * c + 16], sg0[0:128, :],
               (gcsr, sg0r), (prC,), c == 0, c == NCH - 1)
        p.op("dve", lambda e, o=crow[0:16, tt * TTW:(tt + 1) * TTW],
             i_=psC[0:16, 0:512]: e.tensor_copy(o, i_), reads=(prC,), writes=(crowr,))
        ctx.psum.release(prC)
    ones_b = bass.AP(ctx.cst_t, 0, [[8, 16], [0, T]])
    p.op("dve", lambda e, o=crow[0:16, 0:T], a=ones_b, b=crow[0:16, 0:T]:
         e.tensor_tensor_scan(o, a, b, 0.0, OP.mult, OP.add),
         reads=(ctx.onescol_f_res, crowr), writes=(crowr,))

    mt, mtr = ctx.small("l3a_mt", [16, 40], F32)
    memset(p, "dve", mt[0:16, 0:1], 0.0, (mtr,))
    for i in range(NB):
        p.op("dve", lambda e, o=mt[0:16, i + 1:i + 2],
             i_=crow[0:16, 128 * i + 127:128 * i + 128]: e.tensor_copy(o, i_),
             reads=(crowr,), writes=(mtr,))
        p.op("dve", lambda e, o=mt[0:16, 20 + i:21 + i],
             i_=crow[0:16, 128 * i:128 * i + 1]: e.tensor_copy(o, i_),
             reads=(crowr,), writes=(mtr,))
    p.op("dve", lambda e, o=mt[0:16, 36:37], i_=crow[0:16, T - 1:T]:
         e.tensor_copy(o, i_), reads=(crowr,), writes=(mtr,))

    urF, urFr = ctx.small("l3a_urF", [16, 512], BF16)
    urS, urSr = ctx.small("l3a_urS", [16, 512], BF16)
    urBF, urBFr = ctx.small("l3a_urBF", [16, 512], BF16)
    urBS, urBSr = ctx.small("l3a_urBS", [16, 512], BF16)
    uz, uzr = ctx.small("l3a_uz", [16, 512], F32)

    def build_urows(tt):
        mrF = rawap(mt, tt * 4, [[40, 16], [1, 4], [0, 128]])
        p.op("dve", lambda e, o=uz[0:16, :], a=crow[0:16, tt * TTW:(tt + 1) * TTW],
             b=mrF: e.tensor_tensor(o, a, b, OP.subtract), reads=(crowr, mtr),
             writes=(uzr,))
        p.op("act", lambda e, o=urF[0:16, :], i_=uz[0:16, :], b=ctx.ln8_col[0:16, 0:1]:
             e.activation(o, i_, AF.Exp, bias=b), reads=(uzr, ctx.onescol_f_res),
             writes=(urFr,))
        p.op("act", lambda e, o=urS[0:16, :], i_=uz[0:16, :], sc=scol[0:16, 0:1],
             b=ctx.ln8_col[0:16, 0:1]: e.activation(o, i_, AF.Exp, scale=sc, bias=b),
             reads=(uzr, scolr, ctx.onescol_f_res), writes=(urSr,))
        mrB = rawap(mt, 21 + tt * 4, [[40, 16], [1, 4], [0, 128]])
        p.op("dve", lambda e, o=uz[0:16, :], a=mrB,
             b=crow[0:16, tt * TTW:(tt + 1) * TTW]:
             e.tensor_tensor(o, a, b, OP.subtract), reads=(crowr, mtr), writes=(uzr,))
        p.op("act", lambda e, o=urBF[0:16, :], i_=uz[0:16, :], b=ctx.ln8_col[0:16, 0:1]:
             e.activation(o, i_, AF.Exp, bias=b), reads=(uzr, ctx.onescol_f_res),
             writes=(urBFr,))
        p.op("act", lambda e, o=urBS[0:16, :], i_=uz[0:16, :], sc=scol[0:16, 0:1],
             b=ctx.ln8_col[0:16, 0:1]: e.activation(o, i_, AF.Exp, scale=sc, bias=b),
             reads=(uzr, scolr, ctx.onescol_f_res), writes=(urBSr,))

    RS = ("rF", "rS", "rBF", "rBS")

    def r_scaled(oc, tt, ps, pr):
        if oc == "pre":
            build_urows(tt)
            return
        p.op("dve", lambda e, o=cc0[0:128, :], i_=ps[:, 0:512]: e.tensor_copy(o, i_),
             reads=(pr,), writes=(cc0r,))
        for stream, ur, urr in (("rF", urF, urFr), ("rS", urS, urSr),
                                ("rBF", urBF, urBFr), ("rBS", urBS, urBSr)):
            psu, pru = ctx.psum.alloc()
            mm(p, psu[0:128, 0:512], ctx.l3a_ind[0:16, oc * 128:(oc + 1) * 128],
               ur[0:16, :], (ctx.l3a_ind_res, urr), (pru,), True, True)
            p.op("dve", lambda e, o=sg0[0:128, :], a=cc0[0:128, :], b=psu[0:128, 0:512]:
                 e.tensor_tensor(o, a, b, OP.mult), reads=(cc0r, pru), writes=(sg0r,))
            p.dma(lambda q, d=ctx.dram_rs[stream][oc * 128:(oc + 1) * 128,
                                                  tt * TTW:(tt + 1) * TTW],
                  s=sg0[0:128, :]: q.dma_start(d, s), reads=(sg0r,),
                  writes=(ctx.dram_rs_res[stream],))

    sgp, sgpr = ctx.small("l3a_sgp", [128, 512], BF16)

    def proj_evac_fm(wname, dram, dres, act=None, scaled=None, mul8=False):
        for tt in range(NTT):
            if scaled is not None:
                scaled("pre", tt, None, None)
            for oc in range(NCH):
                sg, sgr = (sg0, sg0r) if oc % 2 == 0 else (sgp, sgpr)
                ps, pr = ctx.psum.alloc()
                for kg0 in range(0, NCH, 4):
                    wq, wqr = ws.load(wv2(W[wname], D, kg0 * 128, 4, oc * 128), 512)
                    for j in range(4):
                        kc = kg0 + j
                        mm(p, ps[:, 0:512], wq[0:128, j * 128:(j + 1) * 128],
                           ctx.XBd_tt(kc, tt), (wqr, ctx.XBr[kc]),
                           (pr,), kc == 0, kc == NCH - 1)
                if act:
                    p.op("act", lambda e, o=sg[0:128, :], i_=ps[:, 0:512], f=act:
                         e.activation(o, i_, f), reads=(pr,), writes=(sgr,))
                elif mul8:
                    p.op("dve", lambda e, o=sg[0:128, :], i_=ps[:, 0:512]:
                         e.tensor_single_scalar(o, i_, 0.125, OP.mult),
                         reads=(pr,), writes=(sgr,))
                else:
                    p.op("dve", lambda e, o=sg[0:128, :], i_=ps[:, 0:512]:
                         e.tensor_copy(o, i_), reads=(pr,), writes=(sgr,))
                p.dma(lambda q, d=dram[oc * 128:(oc + 1) * 128,
                                       tt * TTW:(tt + 1) * TTW],
                      s=sg[0:128, :]: q.dma_start(d, s), reads=(sgr,), writes=(dres,))
                if scaled is not None:
                    scaled(oc, tt, ps, pr)

    build_xf(3, 4)
    proj_evac_fm("l3a_wr", ctx.dram_r, ctx.dram_r_res, scaled=r_scaled, mul8=True)

    wts = {n: ctx.small("l3a_wt" + n, [128, 256], F32) for n in
           ("F", "S", "BF", "BS")}
    ctt, cttr = ctx.small("l3a_ct", [128, 256], F32)
    zt, ztr = ctx.small("l3a_zt", [128, 32], F32)
    mr16, mr16r = ctx.small("l3a_mr16", [1, 16], F32)
    for i in range(NB):
        psq, prq = ctx.psum.alloc()
        p.op("pe", lambda t, o=psq[0:128, 0:16], i_=crow[0:16, 128 * i:128 * (i + 1)],
             idd=ctx.identf[0:16, 0:16]: t.transpose(o, i_, idd),
             reads=(crowr, ctx.identf_res), writes=(prq,))
        p.op("dve", lambda e, o=ctt[0:128, 16 * i:16 * i + 16], i_=psq[0:128, 0:16]:
             e.tensor_copy(o, i_), reads=(prq,), writes=(cttr,))
    for i in range(NB):
        for bwd, fn_, sn_ in ((0, "F", "S"), (1, "BF", "BS")):
            col = (i + 1) if not bwd else (20 + i)
            psq, prq = ctx.psum.alloc()
            p.op("pe", lambda t, o=psq[0:1, 0:16], i_=mt[0:16, col:col + 1],
                 idd=ctx.identf[0:16, 0:16]: t.transpose(o, i_, idd),
                 reads=(mtr, ctx.identf_res), writes=(prq,))
            p.op("dve", lambda e, o=mr16[0:1, 0:16], i_=psq[0:1, 0:16]:
                 e.tensor_copy(o, i_), reads=(prq,), writes=(mr16r,))
            psb, prb = ctx.psum.alloc()
            mm(p, psb[0:128, 0:16], ctx.ones_f, mr16[0:1, 0:16],
               (ctx.ones_f_res, mr16r), (prb,), True, True)
            if not bwd:
                p.op("dve", lambda e, o=zt[0:128, 0:16], a=psb[0:128, 0:16],
                     b=ctt[0:128, 16 * i:16 * i + 16]:
                     e.tensor_tensor(o, a, b, OP.subtract), reads=(prb, cttr),
                     writes=(ztr,))
            else:
                p.op("dve", lambda e, o=zt[0:128, 0:16],
                     a=ctt[0:128, 16 * i:16 * i + 16], b=psb[0:128, 0:16]:
                     e.tensor_tensor(o, a, b, OP.subtract), reads=(cttr, prb),
                     writes=(ztr,))
            wf, wfr_ = wts[fn_]
            wsl, wslr_ = wts[sn_]
            p.op("act", lambda e, o=wf[0:128, 16 * i:16 * i + 16], i_=zt[0:128, 0:16]:
                 e.activation(o, i_, AF.Exp), reads=(ztr,), writes=(wfr_,))
            p.op("dve", lambda e, o=zt[0:128, 16:32], a=zt[0:128, 0:16],
                 b=shb[0:128, 0:16]: e.tensor_tensor(o, a, b, OP.mult),
                 reads=(ztr, shbr), writes=(ztr,))
            p.op("act", lambda e, o=wsl[0:128, 16 * i:16 * i + 16],
                 i_=zt[0:128, 16:32]: e.activation(o, i_, AF.Exp), reads=(ztr,),
                 writes=(wslr_,))

    lams = {n: ctx.small("l3a_l" + n, [64, 256], F32) for n in ("F", "S", "BF", "BS")}
    lr, lrr = ctx.small("l3a_lr", [1, 48], F32)
    for i in range(NB):
        for bwd, fn_, sn_ in ((0, "F", "S"), (1, "BF", "BS")):
            if not bwd:
                ca, cb = i + 1, i
            else:
                ca, cb = 20 + i + 1, 20 + i
            p.op("dve", lambda e, o=mt[0:16, 38:39], a=mt[0:16, ca:ca + 1],
                 b=mt[0:16, cb:cb + 1]: e.tensor_tensor(o, a, b, OP.subtract),
                 reads=(mtr,), writes=(mtr,))
            psq, prq = ctx.psum.alloc()
            p.op("pe", lambda t, o=psq[0:1, 0:16], i_=mt[0:16, 38:39],
                 idd=ctx.identf[0:16, 0:16]: t.transpose(o, i_, idd),
                 reads=(mtr, ctx.identf_res), writes=(prq,))
            p.op("dve", lambda e, o=lr[0:1, 0:16], i_=psq[0:1, 0:16]:
                 e.tensor_copy(o, i_), reads=(prq,), writes=(lrr,))
            p.op("act", lambda e, o=lr[0:1, 16:32], i_=lr[0:1, 0:16]:
                 e.activation(o, i_, AF.Exp), reads=(lrr,), writes=(lrr,))
            p.op("dve", lambda e, o=lr[0:1, 32:48], a=lr[0:1, 0:16],
                 b=hrow[0:1, 16:32]: e.tensor_tensor(o, a, b, OP.mult),
                 reads=(lrr, hrowr), writes=(lrr,))
            p.op("act", lambda e, o=lr[0:1, 32:48], i_=lr[0:1, 32:48]:
                 e.activation(o, i_, AF.Exp), reads=(lrr,), writes=(lrr,))
            psb, prb = ctx.psum.alloc()
            mm(p, psb[0:64, 0:32], ctx.ones_f[0:1, 0:64], lr[0:1, 16:48],
               (ctx.ones_f_res, lrr), (prb,), True, True)
            lf, lfr_ = lams[fn_]
            lsl, lslr_ = lams[sn_]
            p.op("dve", lambda e, o=lf[0:64, 16 * i:16 * i + 16], i_=psb[0:64, 0:16]:
                 e.tensor_copy(o, i_), reads=(prb,), writes=(lfr_,))
            p.op("dve", lambda e, o=lsl[0:64, 16 * i:16 * i + 16],
                 i_=psb[0:64, 16:32]: e.tensor_copy(o, i_), reads=(prb,),
                 writes=(lslr_,))

    build_xf(1, 2)
    proj_evac_fm("l3a_wk", ctx.dram_k, ctx.dram_k_res)

    def tm_proj(wname, scaled_kt):
        for grp in range(2):
            for half in range(2):
                pss = [ctx.psum.alloc() for _ in range(8)]
                for kc in range(NCH):
                    wv, wvr = ws.load(W[wname][kc * 128:(kc + 1) * 128,
                                               half * 512:(half + 1) * 512], 512)
                    for tc in range(8):
                        tchunk = grp * 8 + tc
                        lhs = ctx.XB[kc][0:128, PADL + tchunk * 128:
                                         PADL + (tchunk + 1) * 128]
                        ps, pr = pss[tc]
                        mm(p, ps[:, 0:512], lhs, wv, (ctx.XBr[kc], wvr), (pr,),
                           kc == 0, kc == NCH - 1)
                for tc in range(8):
                    tchunk = grp * 8 + tc
                    ps, pr = pss[tc]
                    if not scaled_kt:
                        p.op("act", lambda e, o=sg0[0:128, :], i_=ps[:, 0:512]:
                             e.copy(o, i_), reads=(pr,), writes=(sg0r,))
                        p.dma(lambda q,
                              d=ctx.dram_vT[tchunk * 128:(tchunk + 1) * 128,
                                            half * 512:(half + 1) * 512],
                              s=sg0[0:128, :]: q.dma_start(d, s), reads=(sg0r,),
                              writes=(ctx.dram_vT_res,))
                    else:
                        for stream in ("kF", "kS", "kBF", "kBS"):
                            wt_, wt_r = wts[stream[1:]]
                            for hh in range(8):
                                h = half * 8 + hh
                                p.op("dve", lambda e,
                                     o=sg0[0:128, hh * 64:(hh + 1) * 64],
                                     i_=ps[:, hh * 64:(hh + 1) * 64],
                                     s=wt_[0:128, 16 * tchunk + h:
                                           16 * tchunk + h + 1]:
                                     e.tensor_scalar_mul(o, i_, s),
                                     reads=(pr, wt_r), writes=(sg0r,))
                            p.dma(lambda q,
                                  d=ctx.dram_rs[stream][tchunk * 128:
                                                        (tchunk + 1) * 128,
                                                        half * 512:(half + 1) * 512],
                                  s=sg0[0:128, :]: q.dma_start(d, s), reads=(sg0r,),
                                  writes=(ctx.dram_rs_res[stream],))

    tm_proj("l3a_wk", True)
    build_xf(2, 3)
    tm_proj("l3a_wv", False)
    build_xf(4, 5)
    proj_evac_fm("l3a_wg", ctx.dram_g, ctx.dram_g_res, act=AF.Silu)

    alc, alcr = ctx.small("l3a_al", [64, 16], F32)
    for c in range(NCH):
        for hh in range(2):
            vt, vr = dmv[c]
            p.dma(lambda q, d=alc[0:64, 2 * c + hh:2 * c + hh + 1],
                  s=vt[hh * 64:(hh + 1) * 64, 1:2]: q.dma_start(d, s),
                  reads=(vr,), writes=(alcr,))
    st, st_cr = ctx.small("l3a_st", [64, 128], F32)
    stB, stB_cr = ctx.small("l3a_stB", [64, 128], F32)
    last_st = {"Ff": st_cr, "Sf": st_cr, "Fb": stB_cr, "Sb": stB_cr}
    snaps = {n: ctx.small("l3a_s" + n, [64, 1024], BF16) for n in
             ("F", "S", "BF", "BS")}
    ebf, ebfr = (lambda t_, r_: (t_[0:128, 0:128], r_))(*ctx.small("cacc1", [128, 512], F32))
    eF, eFr = ctx.small("l3a_eF", [128, 128], BF16)
    eS, eSr = ctx.small("l3a_eS", [128, 128], BF16)
    sd, sdr = ctx.small("l3a_sd", [128, 128], BF16)
    yev, yevr = (lambda t_, r_: (t_[0:128, 0:512], r_))(*ctx.small("l0f_e", [128, 640], F32))
    cr0, cr0r = (lambda t_, r_: (t_[0:1, 0:128], r_))(*ctx.small("l0f_r2", [2, TTW], F32))

    def head_col_load(dram_ap, h, tile, coff, res, src_res):
        dst = A.bf16(tile, coff, 1024).rearrange("p (tc d) -> p tc d", d=64)
        src = bass.AP(dram_ap.tensor, h * 64, [[1024, 128], [131072, 16], [1, 64]])
        p.dma(lambda q, o=dst, s=src: q.dma_start(o, s), reads=(src_res,),
              writes=(res,))

    for c in range(NCH):
        kd_res = A.carve(5, "kd%d" % c)
        rd_res = A.carve(6, "rd%d" % c)
        p.dma(lambda q, o=A.bf16(5, 0, T), d=ctx.dram_k[c * 128:(c + 1) * 128, 0:T]:
              q.dma_start(o, d), reads=(ctx.dram_k_res,), writes=(kd_res,))
        p.dma(lambda q, o=A.bf16(6, 0, T), d=ctx.dram_r[c * 128:(c + 1) * 128, 0:T]:
              q.dma_start(o, d), reads=(ctx.dram_r_res,), writes=(rd_res,))
        for hh in range(2):
            h = 2 * c + hh
            hp = hh * 64
            rF_res = A.carve(0, "rF")
            rS_res = A.carve(1, "rS")
            ktF_res = A.carve(2, "ktF")
            ktS_res = A.carve(3, "ktS")
            vth_res = A.carve(4, "vth")
            p.dma(lambda q, o=A.bf16(0, 0, T, p0=0, np_=64),
                  d=ctx.dram_rs["rF"][c * 128 + hp:c * 128 + hp + 64, 0:T]:
                  q.dma_start(o, d), reads=(ctx.dram_rs_res["rF"],), writes=(rF_res,))
            p.dma(lambda q, o=A.bf16(0, T, T, p0=0, np_=64),
                  d=ctx.dram_rs["rBF"][c * 128 + hp:c * 128 + hp + 64, 0:T]:
                  q.dma_start(o, d), reads=(ctx.dram_rs_res["rBF"],), writes=(rF_res,))
            p.dma(lambda q, o=A.bf16(1, 0, T, p0=0, np_=64),
                  d=ctx.dram_rs["rS"][c * 128 + hp:c * 128 + hp + 64, 0:T]:
                  q.dma_start(o, d), reads=(ctx.dram_rs_res["rS"],), writes=(rS_res,))
            p.dma(lambda q, o=A.bf16(1, T, T, p0=0, np_=64),
                  d=ctx.dram_rs["rBS"][c * 128 + hp:c * 128 + hp + 64, 0:T]:
                  q.dma_start(o, d), reads=(ctx.dram_rs_res["rBS"],), writes=(rS_res,))
            head_col_load(ctx.dram_rs["kF"], h, 2, 0, ktF_res,
                          ctx.dram_rs_res["kF"])
            head_col_load(ctx.dram_rs["kBF"], h, 2, 1024, ktF_res,
                          ctx.dram_rs_res["kBF"])
            head_col_load(ctx.dram_rs["kS"], h, 3, 0, ktS_res,
                          ctx.dram_rs_res["kS"])
            head_col_load(ctx.dram_rs["kBS"], h, 3, 1024, ktS_res,
                          ctx.dram_rs_res["kBS"])
            head_col_load(ctx.dram_vT, h, 4, 0, vth_res, ctx.dram_vT_res)

            chain_r = {}
            for key in ("Ff", "Sf", "Fb", "Sb"):
                r_ = Res("st%s_%d" % (key, h))
                r_.seed_from([last_st[key]])
                last_st[key] = r_
                chain_r[key] = r_
            memset(p, "dve", st[0:64, 0:64], 0.0, (chain_r["Ff"],))
            memset(p, "dve", st[0:64, 64:128], 0.0, (chain_r["Sf"],))
            memset(p, "dve", stB[0:64, 0:64], 0.0, (chain_r["Fb"],))
            memset(p, "dve", stB[0:64, 64:128], 0.0, (chain_r["Sb"],))
            for step in range(NB):
                for bwd in (0, 1):
                    i = step if not bwd else NB - 1 - step
                    ktoff = 0 if not bwd else 1024
                    tile_st = st if not bwd else stB
                    names = ("F", "S") if not bwd else ("BF", "BS")
                    for sidx in (0, 1):
                        snn = names[sidx]
                        stc = 64 * sidx
                        cres = chain_r[("F" if sidx == 0 else "S") +
                                       ("f" if not bwd else "b")]
                        sn_t, sn_r = snaps[snn]
                        lam_t, lam_r = lams[snn]
                        p.op("dve", lambda e, o=sn_t[0:64, 64 * i:64 * (i + 1)],
                             i_=tile_st[0:64, stc:stc + 64]: e.tensor_copy(o, i_),
                             reads=(cres,), writes=(sn_r,))
                        psB, prB = ctx.psum.alloc()
                        ktres = ktF_res if sidx == 0 else ktS_res
                        mm(p, psB[0:64, 0:64],
                           A.bf16(2 + sidx, ktoff + i * 64, 64),
                           A.bf16(4, i * 64, 64), (ktres, vth_res), (prB,),
                           True, True)
                        p.op("dve", lambda e, o=tile_st[0:64, stc:stc + 64],
                             i_=tile_st[0:64, stc:stc + 64],
                             s=lam_t[0:64, 16 * i + h:16 * i + h + 1],
                             a=psB[0:64, 0:64]:
                             e.scalar_tensor_tensor(o, i_, s, a, OP.mult, OP.add),
                             reads=(cres, lam_r, prB), writes=(cres,))

            for tt in range(NTT):
                psYf, prYf = ctx.psum.alloc(pin=True)
                psYs, prYs = ctx.psum.alloc(pin=True)
                for ii in range(4):
                    i = 4 * tt + ii
                    coli = ii * 128
                    for sidx, (psY, prY) in enumerate(((psYf, prYf), (psYs, prYs))):
                        sn_t, sn_r = snaps["F" if sidx == 0 else "S"]
                        snb_t, snb_r = snaps["BF" if sidx == 0 else "BS"]
                        rres = rF_res if sidx == 0 else rS_res
                        rtile = 0 if sidx == 0 else 1
                        mm(p, psY[0:64, coli:coli + 128],
                           sn_t[0:64, 64 * i:64 * (i + 1)],
                           A.bf16(rtile, i * 128, 128, p0=0, np_=64),
                           (sn_r, rres), (prY,), ii == 0, False)
                        mm(p, psY[0:64, coli:coli + 128],
                           snb_t[0:64, 64 * i:64 * (i + 1)],
                           A.bf16(rtile, T + i * 128, 128, p0=0, np_=64),
                           (snb_r, rres), (prY,), False, False)
                    psS, prS = ctx.psum.alloc()
                    mm(p, psS[0:128, 0:128], A.bf16(5, i * 128, 128, p0=hp, np_=64),
                       A.bf16(6, i * 128, 128, p0=hp, np_=64), (kd_res, rd_res),
                       (prS,), True, True)
                    psq2, prq2 = ctx.psum.alloc()
                    p.op("pe", lambda t, o=psq2[0:1, 0:128],
                         i_=ctt[0:128, 16 * i + h:16 * i + h + 1],
                         idd=ctx.identf[0:128, 0:128]: t.transpose(o, i_, idd),
                         reads=(cttr, ctx.identf_res), writes=(prq2,))
                    p.op("dve", lambda e, o=cr0[0:1, 0:128], i_=psq2[0:1, 0:128]:
                         e.tensor_copy(o, i_), reads=(prq2,), writes=(cr0r,))
                    psb2, prb2 = ctx.psum.alloc()
                    mm(p, psb2[0:128, 0:128], ctx.ones_f, cr0[0:1, 0:128],
                       (ctx.ones_f_res, cr0r), (prb2,), True, True)
                    p.op("dve", lambda e, o=ebf[0:128, 0:128], i_=psb2[0:128, 0:128],
                         s=ctt[0:128, 16 * i + h:16 * i + h + 1]:
                         e.tensor_scalar_sub(o, i_, s), reads=(prb2, cttr),
                         writes=(ebfr,))
                    p.op("act", lambda e, o=ebf[0:128, 0:128], i_=ebf[0:128, 0:128]:
                         e.activation(o, i_, AF.Abs), reads=(ebfr,), writes=(ebfr,))
                    p.op("act", lambda e, o=eF[0:128, 0:128], i_=ebf[0:128, 0:128]:
                         e.activation(o, i_, AF.Exp, scale=-1.0), reads=(ebfr,),
                         writes=(eFr,))
                    p.op("act", lambda e, o=eS[0:128, 0:128], i_=ebf[0:128, 0:128],
                         sc=shb[0:128, 16 + h:16 + h + 1]:
                         e.activation(o, i_, AF.Exp, scale=sc), reads=(ebfr, shbr),
                         writes=(eSr,))
                    for sidx, (psY, prY), (et, etr) in ((0, (psYf, prYf), (eF, eFr)),
                                                        (1, (psYs, prYs), (eS, eSr))):
                        p.op("dve", lambda e, o=sd[0:128, 0:128], a=psS[0:128, 0:128],
                             b=et[0:128, 0:128]: e.tensor_tensor(o, a, b, OP.mult),
                             reads=(prS, etr), writes=(sdr,))
                        mm(p, psY[0:64, coli:coli + 128], A.bf16(4, i * 64, 64),
                           sd[0:128, 0:128], (vth_res, sdr), (prY,), False, ii == 3)
                av, avr = dmv[c]
                p.op("dve", lambda e, o=yev[0:64, :], i_=psYs[0:64, 0:512]:
                     e.tensor_copy(o, i_), reads=(prYs,), writes=(yevr,))
                p.op("dve", lambda e, o=cc0[0:64, 0:512], a=psYf[0:64, 0:512],
                     b=yev[0:64, :]: e.tensor_tensor(o, a, b, OP.subtract),
                     reads=(prYf, yevr), writes=(cc0r,))
                p.op("dve", lambda e, o=yev[0:64, :], i_=cc0[0:64, 0:512],
                     s=alc[0:64, 2 * c + hh:2 * c + hh + 1], a=yev[0:64, :]:
                     e.scalar_tensor_tensor(o, i_, s, a, OP.mult, OP.add),
                     reads=(cc0r, alcr, yevr), writes=(yevr,))
                p.dma(lambda q, d=ctx.dram_y[h * 64:(h + 1) * 64,
                                             tt * TTW:(tt + 1) * TTW],
                      s=yev[0:64, :]: q.dma_start(d, s), reads=(yevr,),
                      writes=(ctx.dram_y_res,))
                ctx.psum.release(prYf)
                ctx.psum.release(prYs)

    grp64f, grp64fr = ctx.small("grp64f", [128, 2], F32)
    memset(p, "pool", grp64f[0:128, 0:2], 0.0, (grp64fr,))
    memset(p, "pool", grp64f[0:64, 0:1], 1.0, (grp64fr,))
    memset(p, "pool", grp64f[64:128, 1:2], 1.0, (grp64fr,))
    grp2f, grp2fr = ctx.small("grp2f", [2, 128], F32)
    p.dma(lambda q, d=W["l0f_grp"][0:2, 0:128], o=grp2f[0:2, 0:128]:
          q.dma_start(o, d), reads=(), writes=(grp2fr,))
    yl, ylr = (lambda t_, r_: (t_[0:128, 0:512], r_))(*ctx.small("l0f_e", [128, 640], F32))
    ysq, ysqr = ctx.small("cacc1", [128, 512], F32)
    gl, glr = ctx.small("sg1", [128, 512], BF16)
    r2, r2r = ctx.small("l0f_r2", [2, TTW], F32)
    uz2, uz2r = ctx.small("l3a_uz", [16, 512], F32)
    yh_res = [A.carve(c, "yh") for c in range(NCH)]

    def yh_ap(c, tt):
        return A.bf16(c, tt * TTW, TTW)

    for c in range(NCH):
        for tt in range(NTT):
            p.dma(lambda q, o=yl[0:128, :],
                  d=ctx.dram_y[c * 128:(c + 1) * 128, tt * TTW:(tt + 1) * TTW]:
                  q.dma_start(o, d), reads=(ctx.dram_y_res,), writes=(ylr,))
            p.dma(lambda q, o=gl[0:128, :],
                  d=ctx.dram_g[c * 128:(c + 1) * 128, tt * TTW:(tt + 1) * TTW]:
                  q.dma_start(o, d), reads=(ctx.dram_g_res,), writes=(glr,))
            p.op("act", lambda e, o=ysq[0:128, :], i_=yl[0:128, :]:
                 e.activation(o, i_, AF.Square), reads=(ylr,), writes=(ysqr,))
            psA, prA = ctx.psum.alloc()
            psB, prB = ctx.psum.alloc()
            mm(p, psA[0:2, 0:512], grp64f[0:128, 0:2], yl[0:128, :],
               (grp64fr, ylr), (prA,), True, True)
            mm(p, psB[0:2, 0:512], grp64f[0:128, 0:2], ysq[0:128, :],
               (grp64fr, ysqr), (prB,), True, True)
            p.op("dve", lambda e, o=r2[0:2, 0:512], i_=psA[0:2, 0:512]:
                 e.tensor_single_scalar(o, i_, 1.0 / 64, OP.mult), reads=(prA,),
                 writes=(r2r,))
            p.op("act", lambda e, o=uz2[0:2, 0:512], i_=r2[0:2, 0:512]:
                 e.activation(o, i_, AF.Square), reads=(r2r,), writes=(uz2r,))
            p.op("dve", lambda e, o=ysq[0:2, 0:512], i_=psB[0:2, 0:512]:
                 e.tensor_single_scalar(o, i_, 1.0 / 64, OP.mult), reads=(prB,),
                 writes=(ysqr,))
            p.op("dve", lambda e, o=ysq[0:2, 0:512], a=ysq[0:2, 0:512],
                 b=uz2[0:2, 0:512]: e.tensor_tensor(o, a, b, OP.subtract),
                 reads=(ysqr, uz2r), writes=(ysqr,))
            p.op("act", lambda e, o=ysq[0:2, 0:512], i_=ysq[0:2, 0:512],
                 b=ctx.gn_eps_col[0:2, 0:1]: e.activation(o, i_, AF.Ln, bias=b),
                 reads=(ysqr, ctx.onescol_f_res), writes=(ysqr,))
            p.op("act", lambda e, o=ysq[0:2, 0:512], i_=ysq[0:2, 0:512],
                 sc=ctx.mhalf_col[0:2, 0:1]: e.activation(o, i_, AF.Exp, scale=sc),
                 reads=(ysqr, ctx.onescol_f_res), writes=(ysqr,))
            p.op("dve", lambda e, o=r2[0:2, 0:512], a=r2[0:2, 0:512],
                 b=ysq[0:2, 0:512]: e.tensor_tensor(o, a, b, OP.mult),
                 reads=(r2r, ysqr), writes=(r2r,))
            p.op("dve", lambda e, o=r2[0:2, 0:512], i_=r2[0:2, 0:512]:
                 e.tensor_single_scalar(o, i_, -1.0, OP.mult), reads=(r2r,),
                 writes=(r2r,))
            psR, prR = ctx.psum.alloc()
            psM, prM = ctx.psum.alloc()
            mm(p, psR[0:128, 0:512], grp2f[0:2, 0:128], ysq[0:2, 0:512],
               (grp2fr, ysqr), (prR,), True, True)
            mm(p, psM[0:128, 0:512], grp2f[0:2, 0:128], r2[0:2, 0:512],
               (grp2fr, r2r), (prM,), True, True)
            p.op("dve", lambda e, o=cc0[0:128, :], a=yl[0:128, :],
                 b=psR[0:128, 0:512]: e.tensor_tensor(o, a, b, OP.mult),
                 reads=(ylr, prR), writes=(cc0r,))
            p.op("dve", lambda e, o=cc0[0:128, :], a=cc0[0:128, :],
                 b=psM[0:128, 0:512]: e.tensor_tensor(o, a, b, OP.add),
                 reads=(cc0r, prM), writes=(cc0r,))
            p.op("dve", lambda e, o=yl[0:128, :], i_=cc0[0:128, :],
                 s1=lnv[0:128, 2 * c:2 * c + 1], s2=lnv[0:128, 2 * c + 1:2 * c + 2]:
                 e.tensor_scalar(o, i_, s1, s2, OP.mult, OP.add),
                 reads=(cc0r, lnvr), writes=(ylr,))
            p.op("dve", lambda e, o=yh_ap(c, tt), a=yl[0:128, :], b=gl[0:128, :]:
                 e.tensor_tensor(o, a, b, OP.mult), reads=(ylr, glr),
                 writes=(yh_res[c],))
    for oc in range(NCH):
        pss = [ctx.psum.alloc() for _ in range(NTT)]
        for kg0 in range(0, NCH, 4):
            wo, wor = ws.load(wv2(W["l3a_wo"], D, kg0 * 128, 4, oc * 128), 512)
            for j in range(4):
                kc = kg0 + j
                for tt in range(NTT):
                    ps, pr = pss[tt]
                    mm(p, ps[:, 0:512], wo[0:128, j * 128:(j + 1) * 128],
                       yh_ap(kc, tt), (wor, yh_res[kc]), (pr,),
                       kc == 0, kc == NCH - 1)
        for tt in range(NTT):
            ps, pr = pss[tt]
            xap = ctx.Xd_tt(oc, tt)
            p.op("dve", lambda e, o=xap, a=ps[:, 0:512], b=xap:
                 e.tensor_tensor(o, a, b, OP.add), reads=(pr, ctx.Xr[oc]),
                 writes=(ctx.Xr[oc],))
    emit_postln(ctx)


BLOCK_EMITTERS = {}


def _register():
    BLOCK_EMITTERS["l0a"] = emit_l0_attn
    BLOCK_EMITTERS["l2a"] = emit_l2_attn
    BLOCK_EMITTERS["l1f"] = emit_l1_ffn
    BLOCK_EMITTERS["l1a"] = emit_l1_attn
    BLOCK_EMITTERS["l2f"] = emit_l2_ffn
    BLOCK_EMITTERS["l3f"] = emit_l3_ffn
    BLOCK_EMITTERS["l0f"] = emit_l0_ffn
    BLOCK_EMITTERS["l3a"] = emit_l3_attn


def build_nc(block_ids, weight_specs, trunc=None):
    _register()
    nc = bass.Bass("TRN2", target_bir_lowering=False, dynamic_dma_scratch_size=256)
    stack = ExitStack()
    ctx = Ctx()
    ctx.nc = nc
    ctx.stack = stack
    p = Prog(nc)
    ctx.p = p

    x_in = nc.dram_tensor("x_t", [D, T], F32, kind="ExternalInput").ap()
    y_out = nc.dram_tensor("y_t", [D, T], F32, kind="ExternalOutput").ap()
    W = {}
    for name, spec in weight_specs.items():
        if isinstance(spec[-1], str):
            shape, dt = spec[:-1], (BF16 if spec[-1] == "bf16" else F32)
        else:
            shape, dt = spec, F32
        W[name] = nc.dram_tensor(name, list(shape), dt, kind="ExternalInput").ap()
    ctx.weights = W

    with stack:
        ctx.X = [stack.enter_context(nc.sbuf_tensor(f"X{c}", [128, TF], F32))
                 for c in range(NCH)]
        ctx.XB = [stack.enter_context(nc.sbuf_tensor(f"XB{c}", [128, TF], BF16))
                  for c in range(NCH)]
        ctx.Xr = [Res(f"X{c}") for c in range(NCH)]
        ctx.XBr = [Res(f"XB{c}") for c in range(NCH)]
        ctx.arena = Arena(nc, stack)
        ctx.psum = PsumPool(nc, stack)
        ctx.ws = WStream(nc, stack, p)
        cst, cstr = ctx.small("consts", [128, 8], F32)
        cstb, cstbr = ctx.small("constsb", [128, 8], BF16)
        memset(p, "pool", cst[0:128, 0:1], 1.0, (cstr,))
        memset(p, "pool", cst[0:128, 1:2], -1.0, (cstr,))
        memset(p, "pool", cst[0:128, 2:3], 1e-5, (cstr,))
        memset(p, "pool", cst[0:128, 3:4], -0.5, (cstr,))
        memset(p, "pool", cst[0:128, 4:5], 0.5, (cstr,))
        memset(p, "pool", cst[0:128, 5:6], 6.4e-4, (cstr,))
        memset(p, "pool", cst[0:128, 6:7], -2.0794415416798357, (cstr,))
        ctx.ln8_col = cst[0:128, 6:7]
        ctx.eps_col = cst[0:128, 2:3]
        ctx.mhalf_col = cst[0:128, 3:4]
        ctx.half_col = cst[0:128, 4:5]
        ctx.gn_eps_col = cst[0:128, 5:6]
        ctx.cst_t = cst
        identf, identfr = ctx.small("identf", [128, 128], F32)
        iotf, iotfr = (lambda t_, r_: (t_[0:128, 0:128], r_))(
            *ctx.small("cacc0", [128, 512], F32))
        p.op("pool", lambda e, o=iotf[0:128, 0:128]: e.iota(o, [[1, 128]], base=0,
             channel_multiplier=-1, allow_small_or_imprecise_dtypes=True),
             reads=(), writes=(iotfr,))
        memset(p, "pool", identf[0:128, 0:128], 1.0, (identfr,))
        p.op("pool", lambda e, o=identf[0:128, 0:128], i_=identf[0:128, 0:128]:
             e.affine_select(o, i_, [[1, 128]], OP.is_equal, 0.0, base=0,
                             channel_multiplier=-1), reads=(identfr, iotfr),
             writes=(identfr,))
        ctx.identf = identf
        ctx.identf_res = identfr
        memset(p, "pool", cstb[0:128, 0:1], 1.0, (cstbr,))
        ctx.onescol_f = cst[0:128, 0:1]
        ctx.onescol_f_res = cstr
        ctx.onescol_b = cstb[0:128, 0:1]
        ctx.onescol_b_res = cstbr
        onesrow, onesrow_r = ctx.small("onesrow", [1, 512], F32)
        onesrow_b, onesrow_br = ctx.small("onesrowb", [1, 512], BF16)
        memset(p, "pool", onesrow[0:1, 0:512], 1.0, (onesrow_r,))
        memset(p, "pool", onesrow_b[0:1, 0:512], 1.0, (onesrow_br,))
        ctx.onesrow_b = onesrow_b[0:1, 0:512]
        ctx.onesrow_b_res = onesrow_br
        negrow, negrow_r = ctx.small("negrow", [1, 128], F32)
        memset(p, "pool", negrow[0:1, 0:128], -1.0, (negrow_r,))
        ctx.ones_f = onesrow[0:1, 0:128]
        ctx.ones_f_res = onesrow_r
        ctx.negones_f = negrow[0:1, 0:128]
        ctx.negones_f_res = negrow_r

        if "l0a" in block_ids:
            ctx.dram_sg = nc.dram_tensor("scr_sg", [2048, T], BF16, kind="Internal").ap()
            ctx.dram_sg_res = Res("dram_sg")
        if "l0f" in block_ids:
            ctx.dram_xin = nc.dram_tensor("scr_xin", [D, T], F32, kind="Internal").ap()
            ctx.dram_xin_res = Res("dram_xin")
            ctx.dram_w1b = nc.dram_tensor("scr_w1b", [2 * D, 5460], BF16,
                                          kind="Internal").ap()
            ctx.dram_w1b_res = {}
            ctx.dram_w2b = nc.dram_tensor("scr_w2b", [5460, D], BF16,
                                          kind="Internal").ap()
            ctx.dram_w2b_res = {}
        if "l3a" in block_ids:
            ctx.dram_mx = nc.dram_tensor("scr_mx", [160, T], BF16, kind="Internal").ap()
            ctx.dram_mx_res = Res("dram_mx")
            ctx.dram_r = nc.dram_tensor("scr_l3r", [D, T], BF16, kind="Internal").ap()
            ctx.dram_r_res = Res("dram_r")
            ctx.dram_k = nc.dram_tensor("scr_l3k", [D, T], BF16, kind="Internal").ap()
            ctx.dram_k_res = Res("dram_k")
            ctx.dram_g = nc.dram_tensor("scr_l3g", [D, T], BF16, kind="Internal").ap()
            ctx.dram_g_res = Res("dram_g")
            ctx.dram_vT = nc.dram_tensor("scr_l3vT", [T, D], BF16, kind="Internal").ap()
            ctx.dram_vT_res = Res("dram_vT")
            ctx.dram_y = nc.dram_tensor("scr_l3y", [D, T], F32, kind="Internal").ap()
            ctx.dram_y_res = Res("dram_y")
            ctx.dram_rs = {}
            ctx.dram_rs_res = {}
            for nm_ in ("rF", "rS", "rBF", "rBS"):
                ctx.dram_rs[nm_] = nc.dram_tensor("scr_" + nm_, [D, T], BF16,
                                                  kind="Internal").ap()
                ctx.dram_rs_res[nm_] = Res("dram_" + nm_)
            for nm_ in ("kF", "kS", "kBF", "kBS"):
                ctx.dram_rs[nm_] = nc.dram_tensor("scr_" + nm_, [T, D], BF16,
                                                  kind="Internal").ap()
                ctx.dram_rs_res[nm_] = Res("dram_" + nm_)
            gm, gmr = ctx.small("l3a_gm", [128, 128], F32)
            for c_ in range(NCH):
                p.dma(lambda q, d=W["l3a_gmask"][c_ * 128:(c_ + 1) * 128, 0:16],
                      o=gm[0:128, 16 * c_:16 * c_ + 16]: q.dma_start(o, d),
                      reads=(), writes=(gmr,))
            ctx.l3a_gmask = gm
            ctx.l3a_gmask_res = gmr
            indt, indtr = ctx.small("l3a_indt", [16, 1024], BF16)
            p.dma(lambda q, d=W["l3a_ind"][0:16, 0:1024], o=indt[0:16, 0:1024]:
                  q.dma_start(o, d), reads=(), writes=(indtr,))
            ctx.l3a_ind = indt
            ctx.l3a_ind_res = indtr
        if "l2f" in block_ids:
            ctx.dram_q1 = nc.dram_tensor("scr_q1", [D, 4096], BF16, kind="Internal").ap()
            ctx.dram_q2 = nc.dram_tensor("scr_q2", [4096, D], BF16, kind="Internal").ap()
            ctx.dram_q1_res = {}
            ctx.dram_q2_res = {}
        if "l3f" in block_ids:
            ctx.dram_h = nc.dram_tensor("scr_h", [4096, T], BF16, kind="Internal").ap()
            ctx.dram_h_res = Res("dram_h")
            ctx.dram_h1b = nc.dram_tensor("scr_h1b", [4096, 4096], BF16,
                                          kind="Internal").ap()
            ctx.dram_h1b_res = {}
            ctx.dram_h2b = nc.dram_tensor("scr_h2b", [4096, 4096], BF16,
                                          kind="Internal").ap()
            ctx.dram_h2b_res = {}
            ctx.dram_w3b = nc.dram_tensor("scr_w3b", [4096, D], BF16,
                                          kind="Internal").ap()
            ctx.dram_w3b_res = {}
        if "l1a" in block_ids:
            ctx.dram_a = nc.dram_tensor("scr_a", [D, T], F32, kind="Internal").ap()
            ctx.dram_b = nc.dram_tensor("scr_b", [D, T], F32, kind="Internal").ap()
            ctx.dram_c = nc.dram_tensor("scr_c", [D, T], F32, kind="Internal").ap()
            ctx.dram_d = nc.dram_tensor("scr_d", [D, T], BF16, kind="Internal").ap()
            ctx.dram_a_res = Res("dram_a")
            ctx.dram_b_res = Res("dram_b")
            ctx.dram_c_res = Res("dram_c")
            ctx.dram_d_res = Res("dram_d")

        for c in range(NCH):
            memset(p, "dve", ctx.X[c][0:128, 0:PADL], 0.0, (ctx.Xr[c],))
            memset(p, "dve", ctx.X[c][0:128, PADL + T:TF], 0.0, (ctx.Xr[c],))
            memset(p, "dve", ctx.XB[c][0:128, 0:PADL], 0.0, (ctx.XBr[c],))
            memset(p, "dve", ctx.XB[c][0:128, PADL + T:TF], 0.0, (ctx.XBr[c],))
            p.dma(lambda q, d=x_in[c * 128:(c + 1) * 128, 0:T], o=ctx.Xdata(c):
                  q.dma_start(o, d), reads=(), writes=(ctx.Xr[c],))
            p.op("pool", lambda e, o=ctx.XBdata(c), i_=ctx.Xdata(c): e.tensor_copy(o, i_),
                 reads=(ctx.Xr[c],), writes=(ctx.XBr[c],))

        ctx.pool_init()
        ctx.pool_mode = True
        for b in block_ids:
            ctx.pool_block_boundary()
            ctx.ws.ext = []
            BLOCK_EMITTERS[b](ctx)
        ctx.pool_mode = False

        for c in range(NCH):
            p.dma(lambda q, d=y_out[c * 128:(c + 1) * 128, 0:T], s=ctx.Xdata(c):
                  q.dma_start(d, s), reads=(ctx.Xr[c],), writes=())

        sems = {}
        dma_sems = []
        for e in Prog.ENGINES:
            sems[e] = stack.enter_context(nc.semaphore(f"sem_{e}"))
        for i in range(p.n_dma_sems):
            dma_sems.append(stack.enter_context(nc.semaphore(f"dsem{i}")))
        p.truncate(trunc)
        ranks = p.finalize()
        blk = stack.enter_context(nc.Block())

        @blk.sync
        def _(sync):
            p.replay_dma(sync, sems, dma_sems, ranks)

        @blk.tensor
        def _(tensor):
            p.replay_engine("pe", tensor, sems, dma_sems, ranks)

        @blk.vector
        def _(vector):
            p.replay_engine("dve", vector, sems, dma_sems, ranks)

        @blk.scalar
        def _(scalar):
            p.replay_engine("act", scalar, sems, dma_sems, ranks)

        @blk.gpsimd
        def _(gpsimd):
            p.replay_engine("pool", gpsimd, sems, dma_sems, ranks)

    return nc


WEIGHT_SPECS_L0F = {
    "l0f_wqkv": (2 * D, 1536),
    "l0f_qsk": (256, 2),
    "l0f_wo": (1024, D),
    "l0f_lnb": (2 * D, 2),
    "l0f_w1": (2 * D, 5460),
    "l0f_b1c": (2, 5460),
    "l0f_w2": (5460, D),
    "l0f_b2": (2, D),
    "l0f_grp": (2, 128),
    "l0f_rotm": (128, 128, "bf16"),
    "l0f_rot": (64, 4 * T, "bf16"),
    "l0f_mask": (128, 5 * 640, "bf16"),
}

WEIGHT_SPECS_L3A = {
    "l3a_tvec": (D, 7),
    "l3a_maa_w1": (D, 160),
    "l3a_maa_w2": (160, D),
    "l3a_td_w1": (D, 64),
    "l3a_td_w2": (64, D),
    "l3a_hrow": (1, 32),
    "l3a_dmix": (D, 1),
    "l3a_wr": (D, D),
    "l3a_wk": (D, D),
    "l3a_wv": (D, D),
    "l3a_wg": (D, D),
    "l3a_wo": (D, D),
    "l3a_lng": (D, 2),
    "l3a_gmask": (D, 16),
    "l3a_ind": (16, D, "bf16"),
    "l0f_grp": (2, 128),
}

WEIGHT_SPECS_L2F = {
    "l2f_wfcT": (D, 4096),
    "l2f_wprojT": (4096, D),
    "l2f_ln": (4096, 2),
}

WEIGHT_SPECS_L3F = {
    "l3f_w1T": (D, 4096),
    "l3f_w2T": (D, 4096),
    "l3f_h1T": (4096, 4096),
    "l3f_h2T": (4096, 4096),
    "l3f_w3T": (4096, D),
    "l3f_s4": (4096, 4),
    "l3f_w3s": (D, 1),
}

WEIGHT_SPECS_L2A = {
    "l2_wT": (D, D),
    "l2_b": (D, 1),
}

WEIGHT_SPECS_L1F = {
    "l1f_lora_downT": (100, 4),
    "l1f_down_aux": (100, D),
    "l1f_lora_upT": (4, 50),
    "l1f_up_auxT": (50, D),
    "l1f_wT": (D, D),
    "l1f_b": (1, D),
}

WEIGHT_SPECS_L1A = {
    "l1_tvec": (D, 5),
    "l1_wkT": (D, D),
    "l1_wvT": (D, D),
    "l1_wrT": (D, D),
    "l1_woT": (D, D),
}

WEIGHT_SPECS_L0A = {
    "l0_w_in": (D, 4096),
    "l0_conv_w": (2048, 4),
    "l0_conv_b": (2048, 1),
    "l0_d_param": (2048, 1),
    "l0_w_x": (2048, 33),
    "l0_a_log": (16, 1),
    "l0_w_out": (2048, D),
}


ALL_BLOCKS = ["l0a", "l0f", "l1a", "l1f", "l2a", "l2f", "l3a", "l3f"]

BLOCK_SPECS = {}


def _specs():
    if BLOCK_SPECS:
        return BLOCK_SPECS
    BLOCK_SPECS.update({
        "l0a": WEIGHT_SPECS_L0A, "l0f": WEIGHT_SPECS_L0F,
        "l1a": WEIGHT_SPECS_L1A, "l1f": WEIGHT_SPECS_L1F,
        "l2a": WEIGHT_SPECS_L2A, "l2f": WEIGHT_SPECS_L2F,
        "l3a": WEIGHT_SPECS_L3A, "l3f": WEIGHT_SPECS_L3F,
    })
    return BLOCK_SPECS


def _host_consts():
    import ml_dtypes
    DIM_HEAD, WINDOW, n = 64, 512, T
    half_idx = np.arange(0, DIM_HEAD, 2, dtype=np.float64)
    inv_freq = 1.0 / 10000.0 ** (half_idx / DIM_HEAD)
    t = np.arange(n, dtype=np.float64)
    freqs = np.concatenate([t[:, None] * inv_freq[None, :]] * 2, -1)
    base = (half_idx + 0.4 * DIM_HEAD) / (1.4 * DIM_HEAD)
    power = (t - n // 2) / float(WINDOW // 2)
    scale = np.concatenate([base[None, :] ** power[:, None]] * 2, -1)
    cs = (np.cos(freqs) * scale).T
    ss = (np.sin(freqs) * scale).T
    csi = (np.cos(freqs) / scale).T
    ssi = (np.sin(freqs) / scale).T
    rot = np.concatenate([cs, ss, csi, ssi], 1).astype(ml_dtypes.bfloat16)
    R = np.zeros((128, 128), np.float32)
    for b0 in (0, 64):
        for m_ in range(32):
            R[b0 + m_ + 32, b0 + m_] = -1.0
            R[b0 + m_, b0 + m_ + 32] = 1.0
    masks = np.full((128, 5, 640), -1e30, np.float32)
    pp = np.arange(128)[:, None]
    jf = np.arange(640)[None, :]
    para = (jf >= pp) & (jf <= pp + 512)
    for qb in range(4):
        masks[:, qb, :] = np.where(para & (jf >= 512 - 128 * qb), 0.0, -1e30)
    masks[:, 4, :] = np.where(para, 0.0, -1e30)
    grp = np.zeros((2, 128), np.float32)
    grp[0, 0:64] = 1.0
    grp[1, 64:128] = 1.0
    gmask = np.zeros((D, 16), np.float32)
    for d_ in range(D):
        gmask[d_, d_ // 64] = -1.0 / 64.0
    ind = np.zeros((16, D), np.float32)
    for d_ in range(D):
        ind[d_ // 64, d_] = 1.0
    return {
        "l0f_rot": rot,
        "l0f_rotm": R.astype(ml_dtypes.bfloat16),
        "l0f_mask": masks.reshape(128, 5 * 640).astype(ml_dtypes.bfloat16),
        "l0f_grp": grp,
        "l3a_gmask": gmask,
        "l3a_ind": ind.astype(ml_dtypes.bfloat16),
    }


def _host_weights(inp):
    ct = np.ascontiguousarray
    w = {}
    w["l0_w_in"] = inp["L0_attn_W_in"]
    w["l0_conv_w"] = inp["L0_attn_conv_w"].reshape(2048, 4)
    w["l0_conv_b"] = inp["L0_attn_conv_b"].reshape(2048, 1)
    w["l0_d_param"] = inp["L0_attn_D_param"].reshape(2048, 1)
    w["l0_w_x"] = inp["L0_attn_W_x"]
    w["l0_a_log"] = inp["L0_attn_A_log"].reshape(16, 1)
    w["l0_w_out"] = inp["L0_attn_W_out"]
    w["l0f_wqkv"] = inp["L0_ffn_wqkv"].reshape(2 * D, 1536)
    qs = inp["L0_ffn_q_scale"]
    ks = inp["L0_ffn_k_scale"]
    w["l0f_qsk"] = np.stack([np.stack([np.tile(qs[l], 2), np.tile(ks[l], 2)], 1)
                             for l in range(2)]).reshape(256, 2)
    w["l0f_wo"] = inp["L0_ffn_wo"].reshape(1024, D)
    w["l0f_lnb"] = np.stack([inp["L0_ffn_ln_g"], inp["L0_ffn_ln_b"]],
                            -1).reshape(2 * D, 2)
    w["l0f_w1"] = inp["L0_ffn_w1"].reshape(2 * D, 5460)
    w["l0f_b1c"] = inp["L0_ffn_b1"]
    w["l0f_w2"] = inp["L0_ffn_w2"].reshape(2 * 2730, D)
    w["l0f_b2"] = inp["L0_ffn_b2"]
    w["l1_tvec"] = np.stack([inp["L1_attn_time_mix_k"], inp["L1_attn_time_mix_v"],
                             inp["L1_attn_time_mix_r"], inp["L1_attn_time_decay"],
                             inp["L1_attn_time_first"]], 1)
    w["l1_wkT"] = ct(inp["L1_attn_Wk"].T)
    w["l1_wvT"] = ct(inp["L1_attn_Wv"].T)
    w["l1_wrT"] = ct(inp["L1_attn_Wr"].T)
    w["l1_woT"] = ct(inp["L1_attn_Wo"].T)
    w["l1f_wT"] = ct(inp["L1_ffn_W"].T)
    w["l1f_b"] = inp["L1_ffn_b"].reshape(1, D)
    w["l1f_lora_downT"] = ct(inp["L1_ffn_lora_down"].T)
    w["l1f_lora_upT"] = ct(inp["L1_ffn_lora_up"].T)
    w["l1f_down_aux"] = inp["L1_ffn_down_aux"]
    w["l1f_up_auxT"] = ct(inp["L1_ffn_up_aux"].T)
    w["l2_wT"] = ct(inp["L2_attn_W"].T)
    w["l2_b"] = inp["L2_attn_b"].reshape(D, 1)
    w["l2f_wfcT"] = ct(inp["L2_ffn_W_fc"].T)
    w["l2f_wprojT"] = ct(inp["L2_ffn_W_proj"].T)
    w["l2f_ln"] = np.stack([inp["L2_ffn_ln_w"], inp["L2_ffn_ln_b"]], 1)
    w["l3a_tvec"] = np.stack(
        [inp["L3_attn_time_maa_x"].reshape(D), inp["L3_attn_time_maa_w"].reshape(D),
         inp["L3_attn_time_maa_k"].reshape(D), inp["L3_attn_time_maa_v"].reshape(D),
         inp["L3_attn_time_maa_r"].reshape(D), inp["L3_attn_time_maa_g"].reshape(D),
         inp["L3_attn_time_decay"].reshape(D)], 1)
    w["l3a_maa_w1"] = inp["L3_attn_time_maa_w1"]
    w["l3a_maa_w2"] = inp["L3_attn_time_maa_w2"].reshape(160, D)
    w["l3a_td_w1"] = inp["L3_attn_time_decay_w1"]
    w["l3a_td_w2"] = inp["L3_attn_time_decay_w2"]
    w["l3a_hrow"] = np.concatenate([inp["L3_attn_head_decay_bias"],
                                    inp["L3_attn_slow_scale"]]).reshape(1, 32)
    w["l3a_dmix"] = inp["L3_attn_decay_mix"].reshape(D, 1)
    w["l3a_wr"] = inp["L3_attn_W_r"]
    w["l3a_wk"] = inp["L3_attn_W_k"]
    w["l3a_wv"] = inp["L3_attn_W_v"]
    w["l3a_wg"] = inp["L3_attn_W_g"]
    w["l3a_wo"] = inp["L3_attn_W_o"]
    w["l3a_lng"] = np.stack([inp["L3_attn_ln_gamma"], inp["L3_attn_ln_beta"]], 1)
    w["l3f_w1T"] = ct(inp["L3_ffn_w1_w"].T)
    w["l3f_w2T"] = ct(inp["L3_ffn_w2_w"].T)
    w["l3f_h1T"] = ct(inp["L3_ffn_h1_w"].T)
    w["l3f_h2T"] = ct(inp["L3_ffn_h2_w"].T)
    w["l3f_w3T"] = ct(inp["L3_ffn_w3_w"].T)
    w["l3f_s4"] = np.stack([inp["L3_ffn_w1_s"], inp["L3_ffn_w2_s"],
                            inp["L3_ffn_h1_s"], inp["L3_ffn_h2_s"]], 1)
    w["l3f_w3s"] = inp["L3_ffn_w3_s"].reshape(D, 1)
    w.update(_host_consts())
    return {k: np.ascontiguousarray(np.asarray(v)) for k, v in w.items()}


def kernel(**inputs):
    import os
    x = np.asarray(inputs["L0_attn_input_tensor"])
    w = _host_weights({k: np.asarray(v) for k, v in inputs.items()})
    specs = _specs()
    fused = os.environ.get("KMODE", "fused") != "unfused"
    ncores = 8
    if fused:
        allspec = {}
        for b in ALL_BLOCKS:
            allspec.update(specs[b])
        nc = build_nc(ALL_BLOCKS, allspec)
        in_maps = []
        for core in range(ncores):
            m = {"x_t": np.ascontiguousarray(x[core].T)}
            m.update({k: w[k] for k in allspec})
            in_maps.append(m)
        res = run_bass_kernel_spmd(nc, in_maps, core_ids=list(range(ncores)))
        out = np.stack([np.asarray(r["y_t"], np.float32).T for r in res.results])
        return out
    cur = [np.ascontiguousarray(x[core].T) for core in range(ncores)]
    for b in ALL_BLOCKS:
        nc = build_nc([b], specs[b])
        in_maps = []
        for core in range(ncores):
            m = {"x_t": cur[core]}
            m.update({k: w[k] for k in specs[b]})
            in_maps.append(m)
        res = run_bass_kernel_spmd(nc, in_maps, core_ids=list(range(ncores)))
        cur = [np.ascontiguousarray(np.asarray(r["y_t"], np.float32))
               for r in res.results]
    return np.stack([c.T for c in cur])


# version 16
# speedup vs baseline: 1.2358x; 1.0189x over previous
import numpy as np
from contextlib import ExitStack
import concourse.bass as bass
import concourse.mybir as mybir
from concourse.bass_utils import run_bass_kernel_spmd
from concourse.alu_op_type import AluOpType as OP

F32 = mybir.dt.float32
BF16 = mybir.dt.bfloat16
AF = mybir.ActivationFunctionType
AX = mybir.AxisListType.X

T = 2048
PADL = 4
TF = PADL + T + 2
NCH = 8
D = 1024
NTT = 4
TTW = 512

ARENA_F32 = 2052
N_WBUF = 3
WBUF_W = 512
POOL_BYTES = 46336


class Res:
    __slots__ = ("name", "lw", "readers", "exclusive")

    def __init__(self, name, exclusive=False):
        self.name = name
        self.lw = None
        self.readers = {}
        self.exclusive = exclusive

    def seed_from(self, others):
        for o in others:
            if o.lw is not None:
                k, c = o.lw
                self.readers[k] = max(self.readers.get(k, 0), c)
            for k, c in o.readers.items():
                self.readers[k] = max(self.readers.get(k, 0), c)


class Prog:
    ENGINES = ("pe", "dve", "act", "pool")
    DQ_SEMS = {"sync": list(range(0, 32)), "pool": list(range(32, 40)),
               "act": list(range(40, 52)), "dve": list(range(52, 60))}

    def __init__(self, nc, n_dma_sems=60):
        self.nc = nc
        self.ops = {e: [] for e in self.ENGINES}
        self.counts = {e: 0 for e in self.ENGINES}
        self.dma_ops = []
        self.n_dma_sems = n_dma_sems
        self.dma_counts = [0] * n_dma_sems
        self.n_dma = 0
        self.dq_count = {q: 0 for q in self.DQ_SEMS}
        self.needed = {e: set() for e in self.ENGINES}
        self.gidx = 0

    def _waits_for(self, key, reads, writes):
        waits = {}

        def need(owner):
            if owner is None:
                return
            k, c = owner
            if k == key:
                return
            if waits.get(k, 0) < c:
                waits[k] = c

        for r in reads:
            need(r.lw)
            if r.exclusive:
                for k, c in r.readers.items():
                    need((k, c))
        for w in writes:
            need(w.lw)
            for k, c in w.readers.items():
                need((k, c))
        return waits

    def _mark(self, waits):
        for k, c in waits.items():
            if k[0] == "e":
                self.needed[k[1]].add(c)

    def op(self, engine, fn, reads=(), writes=()):
        key = ("e", engine)
        if engine == "pe":
            waits = self._waits_for(key, reads, writes)
        else:
            waits = self._waits_for(None, reads, writes)
        self._mark(waits)
        self.counts[engine] += 1
        cnt = self.counts[engine]
        for r in reads:
            r.readers[key] = cnt
        for w in writes:
            w.lw = (key, cnt)
            w.readers = {}
        self.gidx += 1
        self.ops[engine].append(("op", waits, fn, cnt, self.gidx))

    def dma(self, fn, reads=(), writes=(), q="sync"):
        sems = self.DQ_SEMS[q]
        si = sems[self.dq_count[q] % len(sems)]
        self.dq_count[q] += 1
        self.n_dma += 1
        self.dma_counts[si] += 16
        key = ("d", si)
        waits = self._waits_for(key, reads, writes)
        if self.dma_counts[si] > 16:
            waits[key] = max(waits.get(key, 0), self.dma_counts[si] - 16)
        self._mark(waits)
        cnt = self.dma_counts[si]
        for r in reads:
            r.readers[key] = cnt
        for w in writes:
            w.lw = (key, cnt)
            w.readers = {}
        self.gidx += 1
        if q == "sync":
            self.dma_ops.append((waits, fn, si, cnt, self.gidx))
        else:
            self.ops[q].append(("dma", waits, fn, (si, cnt), self.gidx))

    def truncate(self, gmax):
        if gmax is None:
            return
        for e in self.ENGINES:
            self.ops[e] = [o for o in self.ops[e] if o[4] <= gmax]
        self.dma_ops = [o for o in self.dma_ops if o[4] <= gmax]
        cnts = [0] * self.n_dma_sems
        for waits, fn, si, cnt, g in self.dma_ops:
            cnts[si] = max(cnts[si], cnt)
        for e in self.ENGINES:
            for kind, waits, fn, x, g in self.ops[e]:
                if kind == "dma":
                    si, cnt = x
                    cnts[si] = max(cnts[si], cnt)
        self.dma_counts = cnts

    def finalize(self):
        ranks = {}
        for e in self.ENGINES:
            marked = sorted(self.needed[e])
            rk = {}
            r = 0
            mi = 0
            for i in range(1, self.counts[e] + 1):
                if mi < len(marked) and marked[mi] == i:
                    r += 1
                    mi += 1
                rk[i] = r
            ranks[e] = (rk, set(marked))
        return ranks

    def _emit_waits(self, eng, waits, satisfied, ranks, sems, dma_sems):
        for k in sorted(waits.keys(), key=str):
            c = waits[k]
            if k[0] == "e":
                v = ranks[k[1]][0][c]
                sem = sems[k[1]]
            else:
                v = c
                sem = dma_sems[k[1]]
            if satisfied.get(k, -1) >= v:
                continue
            satisfied[k] = v
            eng.wait_ge(sem, v)

    def replay_engine(self, ename, eng, sems, dma_sems, ranks):
        satisfied = {}
        rk, marked = ranks[ename]
        for kind, waits, fn, x, g in self.ops[ename]:
            self._emit_waits(eng, waits, satisfied, ranks, sems, dma_sems)
            if kind == "op":
                inst = fn(eng)
                if x in marked:
                    inst.then_inc(sems[ename], 1)
            else:
                si, cnt = x
                fn(eng).then_inc(dma_sems[si], 16)
        if ename in self.DQ_SEMS:
            for si in self.DQ_SEMS[ename]:
                if self.dma_counts[si] > 0:
                    eng.wait_ge(dma_sems[si], self.dma_counts[si])

    def replay_dma(self, eng, sems, dma_sems, ranks):
        satisfied = {}
        for waits, fn, si, cnt, g in self.dma_ops:
            self._emit_waits(eng, waits, satisfied, ranks, sems, dma_sems)
            fn(eng).then_inc(dma_sems[si], 16)
        for si in self.DQ_SEMS["sync"]:
            if self.dma_counts[si] > 0:
                eng.wait_ge(dma_sems[si], self.dma_counts[si])


class PsumPool:
    def __init__(self, nc, stack):
        self.banks = [(stack.enter_context(nc.psum_tensor(f"pb{i}", [128, 512], F32)),
                       Res(f"pb{i}", exclusive=True)) for i in range(8)]
        self.i = 0
        self.pinned = set()

    def alloc(self, pin=False):
        for _ in range(16):
            t, r = self.banks[self.i % 8]
            self.i += 1
            if id(r) not in self.pinned:
                if pin:
                    self.pinned.add(id(r))
                return t, r
        raise RuntimeError("all psum banks pinned")

    def release(self, r):
        self.pinned.discard(id(r))


class Arena:
    def __init__(self, nc, stack):
        self.tiles = [stack.enter_context(nc.sbuf_tensor(f"ar{i}", [128, ARENA_F32], F32))
                      for i in range(8)]
        self.hist = [[] for _ in range(8)]

    def reset(self):
        pass

    def carve(self, ti, name):
        r = Res(f"ar{ti}:{name}")
        r.seed_from(self.hist[ti])
        self.hist[ti] = self.hist[ti][-24:]
        self.hist[ti].append(r)
        return r

    def f32(self, ti, c0, w, p0=0, np_=128):
        return self.tiles[ti][p0:p0 + np_, c0:c0 + w]

    def bf16(self, ti, c0b, w, p0=0, np_=128):
        full = self.tiles[ti][0:128, 0:ARENA_F32].bitcast(BF16)
        return full[p0:p0 + np_, c0b:c0b + w]


class WStream:
    def __init__(self, nc, stack, p):
        self.p = p
        self.f = [(stack.enter_context(nc.sbuf_tensor(f"wf{i}", [128, WBUF_W], F32)),
                   Res(f"wf{i}a"), Res(f"wf{i}b")) for i in range(N_WBUF)]
        self.b = [(stack.enter_context(nc.sbuf_tensor(f"wbb{i}", [128, WBUF_W], BF16)),
                   Res(f"wbb{i}")) for i in range(N_WBUF)]
        self.fi = 0
        self.bi = 0
        self.ext = []

    def _f(self):
        slots = self.f + self.ext
        s = slots[self.fi % len(slots)]
        self.fi += 1
        return s

    def _b(self):
        s = self.b[self.bi % N_WBUF]
        self.bi += 1
        return s

    def load(self, dram_slice, w, cast_engine="pool", scale=None, extra_reads=()):
        p = self.p
        ft, fr, fr2 = self._f()
        bt, br = self._b()
        rows = dram_slice.shape[0]
        fap = ft[0:rows, 0:w]
        bap = bt[0:rows, 0:w]
        p.dma(lambda q, d=dram_slice, o=fap: q.dma_start(o, d),
              reads=tuple(extra_reads), writes=(fr, fr2))
        if scale is None:
            p.op(cast_engine, lambda e, o=bap, i_=fap: e.tensor_copy(o, i_),
                 reads=(fr, fr2), writes=(br,))
        else:
            sap, sres = scale
            p.op("dve", lambda e, o=bap, i_=fap, s=sap: e.tensor_scalar_mul(o, i_, s),
                 reads=(fr, fr2, sres), writes=(br,))
        return bap, br

    def load_scaled4(self, dram3d, cw, scales, extra_reads=()):
        p = self.p
        ft, fr, fr2 = self._f()
        bt, br = self._b()
        kb = len(scales)
        w = kb * cw
        p.dma(lambda q, d=dram3d, o=ft[0:128, 0:w]:
              q.dma_start(o, d), reads=tuple(extra_reads), writes=(fr, fr2))
        for j, (sap, sres) in enumerate(scales):
            p.op("dve", lambda e, o=bt[0:128, j * cw:(j + 1) * cw],
                 i_=ft[0:128, j * cw:(j + 1) * cw], s=sap:
                 e.tensor_scalar_mul(o, i_, s),
                 reads=(fr, sres), writes=(br,))
        return bt[0:128, 0:w], br

    def load_bf16(self, dram_slice, w, extra_reads=()):
        p = self.p
        ft, fr, fr2 = self._f()
        rows = dram_slice.shape[0]
        bap = ft[0:128, 0:WBUF_W].bitcast(BF16)[0:rows, 0:w]
        p.dma(lambda q, d=dram_slice, o=bap: q.dma_start(o, d), reads=tuple(extra_reads),
              writes=(fr, fr2))
        return bap, fr

    def load_f32(self, dram_slice, w):
        p = self.p
        ft, fr, fr2 = self._f()
        rows = dram_slice.shape[0]
        fap = ft[0:rows, 0:w]
        p.dma(lambda q, d=dram_slice, o=fap: q.dma_start(o, d), reads=(),
              writes=(fr, fr2))
        return fap, fr


class Ctx:
    def __init__(self):
        self._small = {}
        self.pool_mode = False
        self._pool_t = None
        self._pool_bc = None
        self._pool_cache = {}
        self._pool_off = 0
        self._pool_hist = []
        self._pool_cur = []

    def pool_init(self):
        self._pool_t = self.stack.enter_context(
            self.nc.sbuf_tensor("smpool", [128, POOL_BYTES // 4], F32))
        self._pool_bc = self._pool_t[0:128, 0:POOL_BYTES // 4].bitcast(BF16)

    def pool_block_boundary(self):
        self._pool_hist.extend(self._pool_cur)
        self._pool_cur = []
        self._pool_cache = {}
        self._pool_off = 0

    def pool_alias(self, name, res):
        base = self._pool_cache[name][1]
        for (r, o0, o1) in self._pool_cur:
            if r is base:
                self._pool_cur.append((res, o0, o1))
                return
        raise KeyError(name)

    def small(self, name, shape, dtype):
        if not self.pool_mode or name in self._small:
            if name not in self._small:
                t = self.stack.enter_context(
                    self.nc.sbuf_tensor("sm_" + name, shape, dtype))
                self._small[name] = (t, Res(name))
            t, r = self._small[name]
            return t, r
        if name in self._pool_cache:
            ap, r, sh, dt = self._pool_cache[name]
            assert sh == tuple(shape) and dt is dtype, (name, sh, shape, dt, dtype)
            return ap, r
        esz = 4 if dtype is F32 else 2
        nb = shape[1] * esz
        off = (self._pool_off + 3) & ~3
        assert off + nb <= POOL_BYTES, ("smpool overflow", name, off, nb)
        self._pool_off = off + nb
        if dtype is F32:
            ap = self._pool_t[0:shape[0], off // 4: off // 4 + shape[1]]
        else:
            ap = self._pool_bc[0:shape[0], off // 2: off // 2 + shape[1]]
        r = Res("pool:" + name)
        r.seed_from([pr for (pr, o0, o1) in self._pool_hist
                     if not (off + nb <= o0 or o1 <= off)])
        self._pool_cache[name] = (ap, r, tuple(shape), dtype)
        self._pool_cur.append((r, off, off + nb))
        return ap, r

    def Xd_tt(self, c, tt):
        return self.X[c][0:128, PADL + tt * TTW: PADL + (tt + 1) * TTW]

    def XBd_tt(self, c, tt):
        return self.XB[c][0:128, PADL + tt * TTW: PADL + (tt + 1) * TTW]

    def Xdata(self, c, off=0, w=T):
        return self.X[c][0:128, PADL + off: PADL + off + w]

    def XBdata(self, c, off=0, w=T):
        return self.XB[c][0:128, PADL + off: PADL + off + w]


def wv2(ap_full, wtot, row0, kblocks, col0, cw=128):
    t = ap_full.tensor
    off = row0 * wtot + col0
    return bass.AP(t, off, [[wtot, 128], [128 * wtot, kblocks], [1, cw]])


def wide_view(ap_full, wtot, row0, kblocks, colblk):
    return wv2(ap_full, wtot, row0, kblocks, colblk * 128)


def rawap(src, off, dims):
    if isinstance(src, bass.AP):
        d = [[src.ap[0][0], dims[0][1]]] + [list(x) for x in dims[1:]]
        return bass.AP(src.tensor, src.offset + off, d)
    return bass.AP(src, off, dims)


def mm(p, out_ap, lhs, rhs, reads, writes, start, stop):
    p.op("pe", lambda t, o=out_ap, l=lhs, r=rhs, s1=start, s2=stop:
         t.matmul(o, l, r, start=s1, stop=s2), reads=reads, writes=writes)


def memset(p, engine, ap, val, writes, reads=()):
    p.op(engine, lambda e, a=ap, v=val: e.memset(a, v), reads=reads, writes=writes)


def ws_extra(ctx, n=3):
    ext = []
    for k in range(n):
        nm = "wfx%d" % k
        t, r = ctx.small(nm, [128, WBUF_W], F32)
        ra, rb = Res(nm + "a"), Res(nm + "b")
        ra.seed_from([r])
        rb.seed_from([r])
        ctx.pool_alias(nm, ra)
        ctx.pool_alias(nm, rb)
        ext.append((t, ra, rb))
    ctx.ws.ext = ext


def emit_proj_ocmajor(ctx, w_dram, kchunks, oc0, noc, rhs_fn, rhs_res_fn, evac_fn,
                      wcol0=0, ntt=NTT):
    p = ctx.p
    ws = ctx.ws
    wtot = w_dram.shape[1]
    for oci in range(noc):
        oc = oc0 + oci
        pss = [ctx.psum.alloc() for _ in range(ntt)]
        for kg0 in range(0, kchunks, 4):
            kb = min(4, kchunks - kg0)
            wap, wr = ws.load(wv2(w_dram, wtot, kg0 * 128, kb,
                                  wcol0 + oci * 128), kb * 128)
            for j in range(kb):
                kc = kg0 + j
                lh = wap[0:128, j * 128:(j + 1) * 128]
                for tt in range(ntt):
                    ps, pr = pss[tt]
                    mm(p, ps[:, 0:512], lh, rhs_fn(kc, tt),
                       (wr, rhs_res_fn(kc)), (pr,), kc == 0, kc == kchunks - 1)
        for tt in range(ntt):
            ps, pr = pss[tt]
            evac_fn(oc, tt, ps, pr)


def emit_postln(ctx, extra=None):
    p = ctx.p
    A = ctx.arena
    if extra is not None:
        for c in range(NCH):
            ex, exr = extra[c]
            p.op("dve", lambda e, o=ctx.Xdata(c), a=ctx.Xdata(c), b=ex:
                 e.tensor_tensor(o, a, b, OP.add), reads=(ctx.Xr[c], exr),
                 writes=(ctx.Xr[c],))
    sq_res = [A.carve(c, "sq") for c in range(NCH)]

    def sq_ap(c, tt):
        return A.bf16(c, tt * TTW, TTW)

    for c in range(NCH):
        p.op("pool", lambda e, o=ctx.XBdata(c), i_=ctx.Xdata(c): e.tensor_copy(o, i_),
             reads=(ctx.Xr[c],), writes=(ctx.XBr[c],))
        p.op("act", lambda e, o=A.bf16(c, 0, 2048), i_=ctx.Xdata(c):
             e.activation(o, i_, AF.Square), reads=(ctx.Xr[c],), writes=(sq_res[c],))
    rowA, rowAr = (lambda t_, r_: (t_[0:1, 512:1024], r_))(*ctx.small("l3a_c", [16, T], F32))
    rowB, rowBr = (lambda t_, r_: (t_[0:1, 1024:1536], r_))(*ctx.small("l3a_c", [16, T], F32))
    rowC, rowCr = (lambda t_, r_: (t_[0:1, 0:512], r_))(*ctx.small("cacc1", [128, 512], F32))
    for tt in range(NTT):
        psA, prA = ctx.psum.alloc()
        psB, prB = ctx.psum.alloc()
        for c in range(NCH):
            mm(p, psA[0:1, 0:512], ctx.onescol_b, ctx.XBd_tt(c, tt),
               (ctx.onescol_b_res, ctx.XBr[c]), (prA,), c == 0, c == NCH - 1)
        for c in range(NCH):
            mm(p, psB[0:1, 0:512], ctx.onescol_b, sq_ap(c, tt),
               (ctx.onescol_b_res, sq_res[c]), (prB,), c == 0, c == NCH - 1)
        p.op("dve", lambda e, o=rowA[0:1, :], i_=psA[0:1, 0:512]:
             e.tensor_single_scalar(o, i_, 1.0 / D, OP.mult), reads=(prA,),
             writes=(rowAr,))
        p.op("act", lambda e, o=rowC[0:1, :], i_=rowA[0:1, :]:
             e.activation(o, i_, AF.Square), reads=(rowAr,), writes=(rowCr,))
        p.op("dve", lambda e, o=rowB[0:1, :], i_=psB[0:1, 0:512]:
             e.tensor_single_scalar(o, i_, 1.0 / D, OP.mult), reads=(prB,),
             writes=(rowBr,))
        p.op("dve", lambda e, o=rowB[0:1, :], a=rowB[0:1, :], b=rowC[0:1, :]:
             e.tensor_tensor(o, a, b, OP.subtract), reads=(rowBr, rowCr),
             writes=(rowBr,))
        p.op("act", lambda e, o=rowB[0:1, :], i_=rowB[0:1, :], b=ctx.eps_col[0:1, 0:1]:
             e.activation(o, i_, AF.Ln, bias=b), reads=(rowBr, ctx.onescol_f_res),
             writes=(rowBr,))
        p.op("act", lambda e, o=rowB[0:1, :], i_=rowB[0:1, :], sc=ctx.mhalf_col[0:1, 0:1]:
             e.activation(o, i_, AF.Exp, scale=sc), reads=(rowBr, ctx.onescol_f_res),
             writes=(rowBr,))
        p.op("dve", lambda e, o=rowA[0:1, :], a=rowA[0:1, :], b=rowB[0:1, :]:
             e.tensor_tensor(o, a, b, OP.mult), reads=(rowAr, rowBr), writes=(rowAr,))
        psR, prR = ctx.psum.alloc()
        psM, prM = ctx.psum.alloc()
        mm(p, psR[0:128, 0:512], ctx.ones_f, rowB[0:1, :],
           (ctx.ones_f_res, rowBr), (prR,), True, True)
        mm(p, psM[0:128, 0:512], ctx.negones_f, rowA[0:1, :],
           (ctx.negones_f_res, rowAr), (prM,), True, True)
        for c in range(NCH):
            xap = ctx.Xd_tt(c, tt)
            p.op("dve", lambda e, o=xap, a=xap, b=psR[0:128, 0:512]:
                 e.tensor_tensor(o, a, b, OP.mult), reads=(ctx.Xr[c], prR),
                 writes=(ctx.Xr[c],))
            p.op("dve", lambda e, o=xap, a=xap, b=psM[0:128, 0:512]:
                 e.tensor_tensor(o, a, b, OP.add), reads=(ctx.Xr[c], prM),
                 writes=(ctx.Xr[c],))
            p.op("pool", lambda e, o=ctx.XBd_tt(c, tt), i_=xap: e.tensor_copy(o, i_),
                 reads=(ctx.Xr[c],), writes=(ctx.XBr[c],))


def emit_l0_attn(ctx):
    p, A, ws, W = ctx.p, ctx.arena, ctx.ws, ctx.weights
    A.reset()
    ws_extra(ctx)
    CONVW = 4 + T
    conv_res = [A.carve(k % 8, f"conv{k}") for k in range(16)]

    def conv_ap(k, c0, w):
        return A.bf16(k % 8, (k // 8) * CONVW + c0, w)

    for k in range(16):
        memset(p, "dve", conv_ap(k, 0, 4), 0.0, (conv_res[k],))

    vec16 = [ctx.small("v16_%d" % k, [128, 6], F32) for k in range(16)]
    for k in range(16):
        vt, vr = vec16[k]
        p.dma(lambda q, d=W["l0_conv_w"][k * 128:(k + 1) * 128, 0:4], o=vt[0:128, 0:4]:
              q.dma_start(o, d), reads=(), writes=(vr,))
        p.dma(lambda q, d=W["l0_conv_b"][k * 128:(k + 1) * 128, 0:1], o=vt[0:128, 4:5]:
              q.dma_start(o, d), reads=(), writes=(vr,))
        p.dma(lambda q, d=W["l0_d_param"][k * 128:(k + 1) * 128, 0:1], o=vt[0:128, 5:6]:
              q.dma_start(o, d), reads=(), writes=(vr,))

    sg_scr = [ctx.small("sg%d" % i, [128, 512], BF16) for i in range(3)]

    def evac_gated(oc, tt, ps, pr):
        k = oc - 16
        sgt, sgr = sg_scr[(oc * NTT + tt) % 3]
        p.op("act", lambda e, o=sgt[0:128, 0:512], i_=ps[:, 0:512]:
             e.activation(o, i_, AF.Silu), reads=(pr,), writes=(sgr,))
        p.dma(lambda q, d=ctx.dram_sg[k * 128:(k + 1) * 128, tt * TTW:(tt + 1) * TTW],
              s=sgt[0:128, 0:512]: q.dma_start(d, s), reads=(sgr,),
              writes=(ctx.dram_sg_res,))

    def evac_convin(oc, tt, ps, pr):
        p.op("dve", lambda e, o=conv_ap(oc, 4 + tt * TTW, 512), i_=ps[:, 0:512]:
             e.tensor_copy(o, i_), reads=(pr,), writes=(conv_res[oc],))

    def rhs_fn(kc, tt):
        return ctx.XBd_tt(kc, tt)

    def rhs_res(kc):
        return ctx.XBr[kc]

    emit_proj_ocmajor(ctx, W["l0_w_in"], NCH, 16, 16, rhs_fn, rhs_res, evac_gated,
                      wcol0=16 * 128)
    emit_proj_ocmajor(ctx, W["l0_w_in"], NCH, 0, 16, rhs_fn, rhs_res, evac_convin,
                      wcol0=0)

    conv16, conv16_res = ctx.small("conv16", [16, T], BF16)
    acc_scr = [ctx.small("cacc%d" % i, [128, 512], F32) for i in range(2)]
    for k in range(16):
        vt, vr = vec16[k]
        for tt in range(NTT):
            acc, accr = acc_scr[tt % 2]
            p.op("dve", lambda e, o=acc[0:128, 0:512], i_=conv_ap(k, 1 + tt * TTW, 512),
                 s=vt[0:128, 0:1]: e.tensor_scalar_mul(o, i_, s),
                 reads=(conv_res[k], vr), writes=(accr,))
            for j in (1, 2, 3):
                p.op("dve", lambda e, o=acc[0:128, 0:512],
                     i_=conv_ap(k, 1 + j + tt * TTW, 512), s=vt[0:128, j:j + 1],
                     a=acc[0:128, 0:512]:
                     e.scalar_tensor_tensor(o, i_, s, a, OP.mult, OP.add),
                     reads=(conv_res[k], vr, accr), writes=(accr,))
            p.op("act", lambda e, o=conv_ap(k, 4 + tt * TTW, 512), i_=acc[0:128, 0:512],
                 b=vt[0:128, 4:5]: e.activation(o, i_, AF.Silu, bias=b),
                 reads=(accr, vr), writes=(conv_res[k],))
            if k == 0:
                p.op("act", lambda e, o=conv16[0:16, tt * TTW:(tt + 1) * TTW],
                     i_=acc[0:16, 0:512], b=vt[0:16, 4:5]:
                     e.activation(o, i_, AF.Silu, bias=b),
                     reads=(accr, vr), writes=(conv16_res,))

    wxt, wxr = ctx.small("wx", [128, 16 * 33], BF16)
    for k in range(16):
        fap, fr = ws.load_f32(W["l0_w_x"][k * 128:(k + 1) * 128, 0:33], 33)
        p.op("pool", lambda e, o=wxt[0:128, k * 33:(k + 1) * 33], i_=fap:
             e.tensor_copy(o, i_), reads=(fr,), writes=(wxr,))

    avec, avr = ctx.small("avec", [16, 2], F32)
    p.dma(lambda q, d=W["l0_a_log"][0:16, 0:1], o=avec[0:16, 0:1]: q.dma_start(o, d),
          reads=(), writes=(avr,))
    p.op("act", lambda e, o=avec[0:16, 1:2], i_=avec[0:16, 0:1]:
         e.activation(o, i_, AF.Exp), reads=(avr,), writes=(avr,))
    p.op("dve", lambda e, o=avec[0:16, 1:2], i_=avec[0:16, 1:2]:
         e.tensor_single_scalar(o, i_, -1.0, OP.mult), reads=(avr,), writes=(avr,))

    rowA, rowAr = (lambda t_, r_: (t_[0:1, 512:1024], r_))(*ctx.small("l3a_c", [16, T], F32))
    rowB, rowBr = (lambda t_, r_: (t_[0:1, 1024:1536], r_))(*ctx.small("l3a_c", [16, T], F32))
    rowC, rowCr = (lambda t_, r_: (t_[0:1, 0:512], r_))(*ctx.small("cacc1", [128, 512], F32))
    r16a, r16ar = ctx.small("r16a", [16, 512], F32)
    r16b, r16br = ctx.small("r16b", [16, 512], F32)
    r16c, r16cr = ctx.small("r16c", [16, 512], F32)
    r16d, r16dr = ctx.small("r16d", [16, 512], F32)
    hst, hstr = ctx.small("st16", [16, 2], F32)
    memset(p, "dve", hst[0:16, 0:1], 0.0, (hstr,))
    sg_in = [ctx.small("sg%d" % i, [128, 512], BF16) for i in range(2)]
    go_scr = [ctx.small("cacc%d" % i, [128, 512], F32) for i in range(2)]

    for tt in range(NTT):
        psd, prd = ctx.psum.alloc()
        psb_, prb_ = ctx.psum.alloc()
        psc, prc = ctx.psum.alloc()
        for k in range(16):
            kw = wxt[0:128, k * 33:(k + 1) * 33]
            rhs = conv_ap(k, 4 + tt * TTW, 512)
            mm(p, psd[0:1, 0:512], kw[0:128, 0:1], rhs, (wxr, conv_res[k]), (prd,),
               k == 0, k == 15)
            mm(p, psb_[0:16, 0:512], kw[0:128, 1:17], rhs, (wxr, conv_res[k]), (prb_,),
               k == 0, k == 15)
            mm(p, psc[0:16, 0:512], kw[0:128, 17:33], rhs, (wxr, conv_res[k]), (prc,),
               k == 0, k == 15)
        p.op("dve", lambda e, o=rowA[0:1, :], i_=psd[0:1, 0:512]: e.tensor_copy(o, i_),
             reads=(prd,), writes=(rowAr,))
        p.op("act", lambda e, o=rowB[0:1, :], i_=rowA[0:1, :]: e.activation(o, i_, AF.Abs),
             reads=(rowAr,), writes=(rowBr,))
        p.op("act", lambda e, o=rowB[0:1, :], i_=rowB[0:1, :]:
             e.activation(o, i_, AF.Exp, scale=-1.0), reads=(rowBr,), writes=(rowBr,))
        p.op("act", lambda e, o=rowB[0:1, :], i_=rowB[0:1, :]:
             e.activation(o, i_, AF.Ln, bias=1.0), reads=(rowBr,), writes=(rowBr,))
        p.op("dve", lambda e, o=rowC[0:1, :], i_=rowA[0:1, :]:
             e.tensor_single_scalar(o, i_, 0.0, OP.max), reads=(rowAr,), writes=(rowCr,))
        p.op("dve", lambda e, o=rowA[0:1, :], a=rowC[0:1, :], b=rowB[0:1, :]:
             e.tensor_tensor(o, a, b, OP.add), reads=(rowBr, rowCr), writes=(rowAr,))
        psR, prR = ctx.psum.alloc()
        mm(p, psR[0:16, 0:512], ctx.ones_f[0:1, 0:16], rowA[0:1, :],
           (ctx.ones_f_res, rowAr), (prR,), True, True)
        p.op("dve", lambda e, o=r16a[0:16, :], i_=psb_[0:16, 0:512]: e.tensor_copy(o, i_),
             reads=(prb_,), writes=(r16ar,))
        p.op("dve", lambda e, o=r16a[0:16, :], a=r16a[0:16, :], b=psR[0:16, 0:512]:
             e.tensor_tensor(o, a, b, OP.mult), reads=(r16ar, prR), writes=(r16ar,))
        p.op("dve", lambda e, o=r16c[0:16, :], i_=psc[0:16, 0:512]: e.tensor_copy(o, i_),
             reads=(prc,), writes=(r16cr,))
        p.op("act", lambda e, o=r16b[0:16, :], i_=psR[0:16, 0:512], s=avec[0:16, 1:2]:
             e.activation(o, i_, AF.Exp, scale=s), reads=(prR, avr), writes=(r16br,))
        p.op("dve", lambda e, o=r16a[0:16, :], a=r16a[0:16, :],
             b=conv16[0:16, tt * TTW:(tt + 1) * TTW]:
             e.tensor_tensor(o, a, b, OP.mult), reads=(r16ar, conv16_res),
             writes=(r16ar,))
        p.op("dve", lambda e, o=r16d[0:16, :], a=r16b[0:16, :], b=r16a[0:16, :],
             ini=hst[0:16, 0:1]: e.tensor_tensor_scan(o, a, b, ini, OP.mult, OP.add),
             reads=(r16br, r16ar, hstr), writes=(r16dr,))
        p.op("dve", lambda e, o=hst[0:16, 0:1], i_=r16d[0:16, 511:512]:
             e.tensor_copy(o, i_), reads=(r16dr,), writes=(hstr,))
        p.op("dve", lambda e, o=r16d[0:16, :], a=r16d[0:16, :], b=r16c[0:16, :]:
             e.tensor_tensor(o, a, b, OP.mult), reads=(r16dr, r16cr), writes=(r16dr,))
        psY, prY = ctx.psum.alloc()
        mm(p, psY[0:1, 0:512], ctx.onescol_f[0:16, 0:1], r16d[0:16, :],
           (ctx.onescol_f_res, r16dr), (prY,), True, True)
        p.op("dve", lambda e, o=rowA[0:1, :], i_=psY[0:1, 0:512]: e.tensor_copy(o, i_),
             reads=(prY,), writes=(rowAr,))
        psS, prS = ctx.psum.alloc()
        mm(p, psS[0:128, 0:512], ctx.ones_f, rowA[0:1, :],
           (ctx.ones_f_res, rowAr), (prS,), True, True)
        for k in range(16):
            vt, vr = vec16[k]
            got, gor = go_scr[k % 2]
            sgt, sgr = sg_in[k % 2]
            p.dma(lambda q, o=sgt[0:128, 0:512],
                  d=ctx.dram_sg[k * 128:(k + 1) * 128, tt * TTW:(tt + 1) * TTW]:
                  q.dma_start(o, d), reads=(ctx.dram_sg_res,), writes=(sgr,))
            p.op("dve", lambda e, o=got[0:128, 0:512], i_=conv_ap(k, 4 + tt * TTW, 512),
                 s=vt[0:128, 5:6], a=psS[0:128, 0:512]:
                 e.scalar_tensor_tensor(o, i_, s, a, OP.mult, OP.add),
                 reads=(conv_res[k], vr, prS), writes=(gor,))
            p.op("dve", lambda e, o=conv_ap(k, 4 + tt * TTW, 512), a=got[0:128, 0:512],
                 b=sgt[0:128, 0:512]: e.tensor_tensor(o, a, b, OP.mult),
                 reads=(gor, sgr), writes=(conv_res[k],))

    def evac_out(oc, tt, ps, pr):
        xap = ctx.Xd_tt(oc, tt)
        p.op("dve", lambda e, o=xap, a=ps[:, 0:512], b=xap:
             e.tensor_tensor(o, a, b, OP.add), reads=(pr, ctx.Xr[oc]),
             writes=(ctx.Xr[oc],))

    def rhs_go(kc, tt):
        return conv_ap(kc, 4 + tt * TTW, 512)

    def rhs_go_res(kc):
        return conv_res[kc]

    emit_proj_ocmajor(ctx, W["l0_w_out"], 16, 0, 8, rhs_go, rhs_go_res, evac_out)
    emit_postln(ctx)


def emit_l2_attn(ctx):
    p, A, ws, W = ctx.p, ctx.arena, ctx.ws, ctx.weights
    A.reset()
    bvec = [ctx.small("l2b_%d" % c, [128, 1], F32) for c in range(NCH)]
    for c in range(NCH):
        vt, vr = bvec[c]
        p.dma(lambda q, d=W["l2_b"][c * 128:(c + 1) * 128, 0:1], o=vt[0:128, 0:1]:
              q.dma_start(o, d), reads=(), writes=(vr,))
    lam_res = [A.carve(c, "lam") for c in range(NCH)]

    def lam_ap(c, tt):
        return A.f32(c, tt * TTW, TTW)

    def evac_lam(oc, tt, ps, pr):
        vt, vr = bvec[oc]
        p.op("act", lambda e, o=lam_ap(oc, tt), i_=ps[:, 0:512], b=vt[0:128, 0:1]:
             e.activation(o, i_, AF.Sigmoid, bias=b), reads=(pr, vr),
             writes=(lam_res[oc],))

    emit_proj_ocmajor(ctx, W["l2_wT"], NCH, 0, 8,
                      lambda kc, tt: ctx.XBd_tt(kc, tt), lambda kc: ctx.XBr[kc],
                      evac_lam)
    ta, tar = ctx.small("cacc0", [128, 512], F32)
    tb, tbr = ctx.small("cacc1", [128, 512], F32)
    st, str_ = ctx.small("l2st", [128, 1], F32)
    for c in range(NCH):
        memset(p, "dve", st[0:128, 0:1], 0.0, (str_,))
        for tt in range(NTT):
            xap = ctx.Xd_tt(c, tt)
            p.op("dve", lambda e, o=ta[0:128, :], a=lam_ap(c, tt), b=xap:
                 e.tensor_tensor(o, a, b, OP.mult), reads=(lam_res[c], ctx.Xr[c]),
                 writes=(tar,))
            p.op("dve", lambda e, o=tb[0:128, :], a=xap, b=ta[0:128, :]:
                 e.tensor_tensor(o, a, b, OP.subtract), reads=(ctx.Xr[c], tar),
                 writes=(tbr,))
            p.op("dve", lambda e, o=ta[0:128, :], a=lam_ap(c, tt), b=tb[0:128, :],
                 ini=st[0:128, 0:1]: e.tensor_tensor_scan(o, a, b, ini, OP.mult, OP.add),
                 reads=(lam_res[c], tbr, str_), writes=(tar,))
            p.op("dve", lambda e, o=st[0:128, 0:1], i_=ta[0:128, 511:512]:
                 e.tensor_copy(o, i_), reads=(tar,), writes=(str_,))
            p.op("dve", lambda e, o=xap, a=xap, b=ta[0:128, :]:
                 e.tensor_tensor(o, a, b, OP.add), reads=(ctx.Xr[c], tar),
                 writes=(ctx.Xr[c],))
    emit_postln(ctx)


def emit_l1_ffn(ctx):
    p, A, ws, W = ctx.p, ctx.arena, ctx.ws, ctx.weights
    A.reset()
    p4r = A.carve(0, "l1f_p4")
    r50r = A.carve(1, "l1f_r50")
    uar = A.carve(2, "l1f_ua")
    p4 = A.f32(0, 1024, 1024, p0=0, np_=4)
    r50 = A.f32(1, 1024, 1024, p0=0, np_=50)
    ua = A.f32(2, 1024, 1024, p0=0, np_=50)
    ldt, ldtr = ws.load_f32(W["l1f_lora_downT"][0:100, 0:4], 4)
    da0, da0r = ws.load_f32(W["l1f_down_aux"][0:100, 0:512], 512)
    da1, da1r = ws.load_f32(W["l1f_down_aux"][0:100, 512:1024], 512)
    ps1, pr1 = ctx.psum.alloc()
    ps2, pr2 = ctx.psum.alloc()
    mm(p, ps1[0:4, 0:512], ldt, da0, (ldtr, da0r), (pr1,), True, True)
    mm(p, ps2[0:4, 0:512], ldt, da1, (ldtr, da1r), (pr2,), True, True)
    p.op("dve", lambda e, o=p4[0:4, 0:512], i_=ps1[0:4, 0:512]: e.tensor_copy(o, i_),
         reads=(pr1,), writes=(p4r,))
    p.op("dve", lambda e, o=p4[0:4, 512:1024], i_=ps2[0:4, 0:512]: e.tensor_copy(o, i_),
         reads=(pr2,), writes=(p4r,))
    lut, lutr = ws.load_f32(W["l1f_lora_upT"][0:4, 0:50], 50)
    ps3, pr3 = ctx.psum.alloc()
    ps4, pr4 = ctx.psum.alloc()
    mm(p, ps3[0:50, 0:512], lut, p4[0:4, 0:512], (lutr, p4r), (pr3,), True, True)
    mm(p, ps4[0:50, 0:512], lut, p4[0:4, 512:1024], (lutr, p4r), (pr4,), True, True)
    p.op("dve", lambda e, o=r50[0:50, 0:512], i_=ps3[0:50, 0:512]: e.tensor_copy(o, i_),
         reads=(pr3,), writes=(r50r,))
    p.op("dve", lambda e, o=r50[0:50, 512:1024], i_=ps4[0:50, 0:512]: e.tensor_copy(o, i_),
         reads=(pr4,), writes=(r50r,))
    ua0, ua0r = ws.load_f32(W["l1f_up_auxT"][0:50, 0:512], 512)
    p.op("dve", lambda e, o=ua[0:50, 0:512], i_=ua0: e.tensor_copy(o, i_),
         reads=(ua0r,), writes=(uar,))
    ua1, ua1r = ws.load_f32(W["l1f_up_auxT"][0:50, 512:1024], 512)
    p.op("dve", lambda e, o=ua[0:50, 512:1024], i_=ua1: e.tensor_copy(o, i_),
         reads=(ua1r,), writes=(uar,))
    weff_res = [A.carve(c, "weff") for c in range(NCH)]

    def weff_ap(c, c0, w):
        return A.bf16(c, c0, w)

    for c in range(NCH):
        for half in range(2):
            psl, prl = ctx.psum.alloc()
            mm(p, psl[0:128, 0:512], r50[0:50, c * 128:(c + 1) * 128],
               ua[0:50, half * 512:(half + 1) * 512], (r50r, uar), (prl,), True, True)
            wt, wtr = ws.load_f32(W["l1f_wT"][c * 128:(c + 1) * 128,
                                              half * 512:(half + 1) * 512], 512)
            p.op("dve", lambda e, o=weff_ap(c, half * 512, 512), a=wt,
                 b=psl[0:128, 0:512]: e.tensor_tensor(o, a, b, OP.add),
                 reads=(wtr, prl), writes=(weff_res[c],))
    brow, browr = ctx.small("l1f_b", [1, 1024], BF16)
    bf_, bfr_ = ws.load_f32(W["l1f_b"][0:1, 0:512], 512)
    p.op("dve", lambda e, o=brow[0:1, 0:512], i_=bf_: e.tensor_copy(o, i_),
         reads=(bfr_,), writes=(browr,))
    bf2, bfr2 = ws.load_f32(W["l1f_b"][0:1, 512:1024], 512)
    p.op("dve", lambda e, o=brow[0:1, 512:1024], i_=bf2: e.tensor_copy(o, i_),
         reads=(bfr2,), writes=(browr,))
    for oc in range(NCH):
        for tt in range(NTT):
            ps, pr = ctx.psum.alloc()
            for kc in range(NCH):
                mm(p, ps[:, 0:512], weff_ap(kc, oc * 128, 128), ctx.XBd_tt(kc, tt),
                   (weff_res[kc], ctx.XBr[kc]), (pr,), kc == 0, False)
            mm(p, ps[:, 0:512], brow[0:1, oc * 128:(oc + 1) * 128], ctx.onesrow_b,
               (browr, ctx.onesrow_b_res), (pr,), False, True)
            xap = ctx.Xd_tt(oc, tt)
            p.op("dve", lambda e, o=xap, a=ps[:, 0:512], b=xap:
                 e.tensor_tensor(o, a, b, OP.add), reads=(pr, ctx.Xr[oc]),
                 writes=(ctx.Xr[oc],))
    emit_postln(ctx)


def emit_l1_attn(ctx):
    p, A, ws, W = ctx.p, ctx.arena, ctx.ws, ctx.weights
    A.reset()
    ws_extra(ctx)
    v8 = [ctx.small("l1v_%d" % c, [128, 8], F32) for c in range(NCH)]
    for c in range(NCH):
        vt, vr = v8[c]
        p.dma(lambda q, d=W["l1_tvec"][c * 128:(c + 1) * 128, 0:5], o=vt[0:128, 0:5]:
              q.dma_start(o, d), reads=(), writes=(vr,))
        p.op("act", lambda e, o=vt[0:128, 5:6], i_=vt[0:128, 3:4]:
             e.activation(o, i_, AF.Exp), reads=(vr,), writes=(vr,))
        p.op("act", lambda e, o=vt[0:128, 5:6], i_=vt[0:128, 5:6]:
             e.activation(o, i_, AF.Exp, scale=-1.0), reads=(vr,), writes=(vr,))
        p.op("act", lambda e, o=vt[0:128, 6:7], i_=vt[0:128, 4:5]:
             e.activation(o, i_, AF.Exp), reads=(vr,), writes=(vr,))

    def do_proj(wname, tmcol, evac):
        for kc in range(NCH):
            vt, vr = v8[kc]
            p.op("dve", lambda e, o=vt[0:128, 7:8], i_=vt[0:128, tmcol:tmcol + 1]:
                 e.tensor_scalar(o, i_, -1.0, 1.0, OP.mult, OP.add),
                 reads=(vr,), writes=(vr,))
        for oc in range(NCH):
            pss = [ctx.psum.alloc() for _ in range(NTT)]
            for kg0 in range(0, NCH, 4):
                sc1 = [(v8[kg0 + j][0][0:128, tmcol:tmcol + 1], v8[kg0 + j][1])
                       for j in range(4)]
                sc2 = [(v8[kg0 + j][0][0:128, 7:8], v8[kg0 + j][1])
                       for j in range(4)]
                w1, w1r = ws.load_scaled4(
                    wv2(W[wname], D, kg0 * 128, 4, oc * 128), 128, sc1)
                w2, w2r = ws.load_scaled4(
                    wv2(W[wname], D, kg0 * 128, 4, oc * 128), 128, sc2)
                for j in range(4):
                    kc = kg0 + j
                    for tt in range(NTT):
                        ps, pr = pss[tt]
                        mm(p, ps[:, 0:512], w1[0:128, j * 128:(j + 1) * 128],
                           ctx.XBd_tt(kc, tt),
                           (w1r, ctx.XBr[kc]), (pr,), kc == 0, False)
                        sh = ctx.XB[kc][0:128, PADL - 1 + tt * TTW:
                                        PADL - 1 + (tt + 1) * TTW]
                        mm(p, ps[:, 0:512], w2[0:128, j * 128:(j + 1) * 128], sh,
                           (w2r, ctx.XBr[kc]), (pr,), False, kc == NCH - 1)
            for tt in range(NTT):
                ps, pr = pss[tt]
                evac(oc, tt, ps, pr)

    sc0, sc0r = ctx.small("cacc0", [128, 512], F32)

    def mk_evac(dram, dres, act=None):
        def evac(oc, tt, ps, pr):
            p.op("act" if act else "dve",
                 (lambda e, o=sc0[0:128, :], i_=ps[:, 0:512], f=act:
                  e.activation(o, i_, f)) if act else
                 (lambda e, o=sc0[0:128, :], i_=ps[:, 0:512]: e.tensor_copy(o, i_)),
                 reads=(pr,), writes=(sc0r,))
            p.dma(lambda q, d=dram[oc * 128:(oc + 1) * 128, tt * TTW:(tt + 1) * TTW],
                  s=sc0[0:128, :]: q.dma_start(d, s), reads=(sc0r,), writes=(dres,))
        return evac

    do_proj("l1_wkT", 0, mk_evac(ctx.dram_a, ctx.dram_a_res, AF.Exp))
    do_proj("l1_wvT", 1, mk_evac(ctx.dram_b, ctx.dram_b_res, None))
    do_proj("l1_wrT", 2, mk_evac(ctx.dram_c, ctx.dram_c_res, AF.Sigmoid))

    res_ek = None
    for c in range(NCH):
        vt, vr = v8[c]
        r_ek = A.carve(0, "ek")
        r_v = A.carve(1, "v")
        r_sr = A.carve(2, "sr")
        r_a = A.carve(3, "a")
        r_b = A.carve(4, "b")
        r_w = A.carve(5, "wkv")
        ek = A.f32(0, 0, T)
        vv = A.f32(1, 0, T)
        sr = A.f32(2, 0, T)
        aa = A.f32(3, 0, T + 1)
        bb = A.f32(4, 0, T + 1)
        wk = A.bf16(5, 0, T)
        p.dma(lambda q, o=ek, d=ctx.dram_a[c * 128:(c + 1) * 128, 0:T]:
              q.dma_start(o, d), reads=(ctx.dram_a_res,), writes=(r_ek,))
        p.dma(lambda q, o=vv, d=ctx.dram_b[c * 128:(c + 1) * 128, 0:T]:
              q.dma_start(o, d), reads=(ctx.dram_b_res,), writes=(r_v,))
        p.dma(lambda q, o=sr, d=ctx.dram_c[c * 128:(c + 1) * 128, 0:T]:
              q.dma_start(o, d), reads=(ctx.dram_c_res,), writes=(r_sr,))
        p.op("dve", lambda e, o=vv, a=ek, b=vv: e.tensor_tensor(o, a, b, OP.mult),
             reads=(r_ek, r_v), writes=(r_v,))
        ew_b = rawap(vt, 5, [[8, 128], [0, T]])
        memset(p, "dve", A.f32(3, 0, 1), 0.0, (r_a,))
        memset(p, "dve", A.f32(4, 0, 1), 0.0, (r_b,))
        p.op("dve", lambda e, o=A.f32(3, 1, T), a=ew_b, b=vv:
             e.tensor_tensor_scan(o, a, b, 0.0, OP.mult, OP.add),
             reads=(vr, r_v), writes=(r_a,))
        p.op("dve", lambda e, o=A.f32(4, 1, T), a=ew_b, b=ek:
             e.tensor_tensor_scan(o, a, b, 0.0, OP.mult, OP.add),
             reads=(vr, r_ek), writes=(r_b,))
        p.op("dve", lambda e, o=vv, i_=vv, s=vt[0:128, 6:7], a=A.f32(3, 0, T):
             e.scalar_tensor_tensor(o, i_, s, a, OP.mult, OP.add),
             reads=(r_v, vr, r_a), writes=(r_v,))
        p.op("dve", lambda e, o=ek, i_=ek, s=vt[0:128, 6:7], a=A.f32(4, 0, T):
             e.scalar_tensor_tensor(o, i_, s, a, OP.mult, OP.add),
             reads=(r_ek, vr, r_b), writes=(r_ek,))
        p.op("dve", lambda e, o=ek, i_=ek: e.reciprocal(o, i_),
             reads=(r_ek,), writes=(r_ek,))
        p.op("dve", lambda e, o=vv, a=vv, b=ek: e.tensor_tensor(o, a, b, OP.mult),
             reads=(r_v, r_ek), writes=(r_v,))
        p.op("dve", lambda e, o=wk, a=vv, b=sr: e.tensor_tensor(o, a, b, OP.mult),
             reads=(r_v, r_sr), writes=(r_w,))
        p.dma(lambda q, d=ctx.dram_d[c * 128:(c + 1) * 128, 0:T], s=wk:
              q.dma_start(d, s), reads=(r_w,), writes=(ctx.dram_d_res,))

    rkt, rkt_cr = ctx.small("l1a_rk", [128, NCH * TTW], BF16)
    rk_res = [rkt_cr] * NCH
    for tt in range(NTT):
        for kc in range(NCH):
            rres = Res("l1a_rk%d_%d" % (tt, kc))
            rres.seed_from([rk_res[kc]])
            rk_res[kc] = rres
            p.dma(lambda q, o=rkt[0:128, kc * TTW:(kc + 1) * TTW],
                  d=ctx.dram_d[kc * 128:(kc + 1) * 128, tt * TTW:(tt + 1) * TTW]:
                  q.dma_start(o, d), reads=(ctx.dram_d_res,), writes=(rres,))
        for oc in range(NCH):
            ps, pr = ctx.psum.alloc()
            for kg0 in range(0, NCH, 4):
                wo, wor = ws.load(wv2(W["l1_woT"], D, kg0 * 128, 4, oc * 128), 512)
                for j in range(4):
                    kc = kg0 + j
                    mm(p, ps[:, 0:512], wo[0:128, j * 128:(j + 1) * 128],
                       rkt[0:128, kc * TTW:(kc + 1) * TTW], (wor, rk_res[kc]),
                       (pr,), kc == 0, kc == NCH - 1)
            xap = ctx.Xd_tt(oc, tt)
            p.op("dve", lambda e, o=xap, a=ps[:, 0:512], b=xap:
                 e.tensor_tensor(o, a, b, OP.add), reads=(pr, ctx.Xr[oc]),
                 writes=(ctx.Xr[oc],))
    emit_postln(ctx)


def emit_l0_ffn(ctx):
    p, A, ws, W = ctx.p, ctx.arena, ctx.ws, ctx.weights
    A.reset()
    ws_extra(ctx)
    for c in range(NCH):
        p.dma(lambda q, d=ctx.dram_xin[c * 128:(c + 1) * 128, 0:T], s=ctx.Xdata(c):
              q.dma_start(d, s), reads=(ctx.Xr[c],), writes=(ctx.dram_xin_res,))

    grp64b, grp64br = ctx.small("grp64b", [128, 2], BF16)
    grp2f, grp2fr = ctx.small("grp2f", [2, 128], F32)
    memset(p, "pool", grp64b[0:128, 0:2], 0.0, (grp64br,))
    memset(p, "pool", grp64b[0:64, 0:1], 1.0, (grp64br,))
    memset(p, "pool", grp64b[64:128, 1:2], 1.0, (grp64br,))
    p.dma(lambda q, d=W["l0f_grp"][0:2, 0:128], o=grp2f[0:2, 0:128]:
          q.dma_start(o, d), reads=(), writes=(grp2fr,))
    rotm, rotmr = ctx.small("rotm", [128, 128], BF16)
    p.dma(lambda q, d=W["l0f_rotm"][0:128, 0:128], o=rotm[0:128, 0:128]:
          q.dma_start(o, d), reads=(), writes=(rotmr,))
    idn, idnr = ctx.small("identb", [128, 128], BF16)
    iot, iotr = (lambda t_, r_: (t_[0:128, 0:128], r_))(
        *ctx.small("cacc0", [128, 512], F32))
    p.op("pool", lambda e, o=iot[0:128, 0:128]: e.iota(o, [[1, 128]], base=0,
         channel_multiplier=-1, allow_small_or_imprecise_dtypes=True),
         reads=(), writes=(iotr,))
    memset(p, "pool", idn[0:128, 0:128], 1.0, (idnr,))
    p.op("pool", lambda e, o=idn[0:128, 0:128], i_=idn[0:128, 0:128]:
         e.affine_select(o, i_, [[1, 128]], OP.is_equal, 0.0, base=0,
                         channel_multiplier=-1), reads=(idnr,), writes=(idnr,))
    mask, maskr = ctx.small("l0f_mask", [128, 5 * 640], BF16)
    for mi in range(5):
        p.dma(lambda q, d=W["l0f_mask"][0:128, mi * 640:(mi + 1) * 640],
              o=mask[0:128, mi * 640:(mi + 1) * 640]: q.dma_start(o, d),
              reads=(), writes=(maskr,))

    for lay in range(2):
        if lay == 1:
            for c in range(NCH):
                p.op("pool", lambda e, o=ctx.XBdata(c), i_=ctx.Xdata(c):
                     e.tensor_copy(o, i_), reads=(ctx.Xr[c],), writes=(ctx.XBr[c],))
        qsv, qsvr = ctx.small("l0f_qs", [128, 2], F32)
        p.dma(lambda q, d=W["l0f_qsk"][lay * 128:(lay + 1) * 128, 0:2],
              o=qsv[0:128, 0:2]: q.dma_start(o, d), reads=(), writes=(qsvr,))
        p.op("dve", lambda e, o=qsv[0:128, 0:1], i_=qsv[0:128, 0:1]:
             e.tensor_single_scalar(o, i_, 8.0, OP.mult), reads=(qsvr,), writes=(qsvr,))

        def xb_sh(kc, tt):
            off = PADL if kc < 4 else PADL - 1
            return ctx.XB[kc][0:128, off + tt * TTW: off + (tt + 1) * TTW]

        q_res = [A.carve(hh, "q%d" % hh) for hh in range(4)]
        k_res = [A.carve(4 + hh, "k%d" % hh) for hh in range(4)]
        vt_res = [A.carve(i % 4 if i < 16 else 4 + (i - 16), "vt%d" % i)
                  for i in range(20)]

        def q_ap(hc, c0, w, p0=0, np_=128):
            return A.bf16(hc, c0, w, p0=p0, np_=np_)

        def k_ap(hc, c0, w, p0=0, np_=128):
            return A.bf16(4 + hc, c0, w, p0=p0, np_=np_)

        def vt_ap(sp, c0, w, p0=0, np_=128):
            if sp < 16:
                return A.bf16(sp % 4, 2048 + (sp // 4) * 512 + c0, w, p0=p0, np_=np_)
            return A.bf16(4 + (sp - 16), 2560 + c0, w, p0=p0, np_=np_)

        for i in range(4):
            memset(p, "dve", vt_ap(i, 0, 512), 0.0, (vt_res[i],))
        for hc in range(4):
            memset(p, "dve", k_ap(hc, 0, 512), 0.0, (k_res[hc],))

        rtab, rtabr = ctx.small("l0f_rt", [128, 4 * TTW], BF16)
        cacc0, cacc0r = ctx.small("cacc0", [128, 512], F32)
        cacc1, cacc1r = ctx.small("cacc1", [128, 512], F32)
        sgq, sgqr = ctx.small("sg0", [128, 512], BF16)
        r2, r2r = ctx.small("l0f_r2", [2, TTW], F32)

        for tt in range(NTT):
            for ti in range(4):
                for half in range(2):
                    p.dma(lambda q, o=rtab[half * 64:(half + 1) * 64,
                                           ti * TTW:(ti + 1) * TTW],
                          d=W["l0f_rot"][0:64, ti * T + tt * TTW:
                                         ti * T + (tt + 1) * TTW]:
                          q.dma_start(o, d), reads=(), writes=(rtabr,))
            for oc in range(8):
                is_q = oc < 4
                ps, pr = ctx.psum.alloc()
                for kg0 in range(0, NCH, 4):
                    wq, wqr = ws.load(wv2(W["l0f_wqkv"], 1536, lay * D + kg0 * 128,
                                          4, oc * 128), 512)
                    for j in range(4):
                        kc = kg0 + j
                        mm(p, ps[:, 0:512], wq[0:128, j * 128:(j + 1) * 128],
                           xb_sh(kc, tt), (wqr, ctx.XBr[kc]), (pr,),
                           kc == 0, kc == NCH - 1)
                p.op("act", lambda e, o=sgq[0:128, :], i_=ps[:, 0:512]:
                     e.activation(o, i_, AF.Square), reads=(pr,), writes=(sgqr,))
                ps2, pr2 = ctx.psum.alloc()
                mm(p, ps2[0:2, 0:512], grp64b[0:128, 0:2], sgq[0:128, :],
                   (grp64br, sgqr), (pr2,), True, True)
                p.op("act", lambda e, o=r2[0:2, 0:TTW], i_=ps2[0:2, 0:512]:
                     e.activation(o, i_, AF.Sqrt), reads=(pr2,), writes=(r2r,))
                p.op("dve", lambda e, o=r2[0:2, 0:TTW], i_=r2[0:2, 0:TTW]:
                     e.tensor_single_scalar(o, i_, 1e-12, OP.max), reads=(r2r,),
                     writes=(r2r,))
                p.op("dve", lambda e, o=r2[0:2, 0:TTW], i_=r2[0:2, 0:TTW]:
                     e.reciprocal(o, i_), reads=(r2r,), writes=(r2r,))
                psB, prB = ctx.psum.alloc()
                mm(p, psB[0:128, 0:512], grp2f[0:2, 0:128], r2[0:2, 0:TTW],
                   (grp2fr, r2r), (prB,), True, True)
                p.op("dve", lambda e, o=cacc0[0:128, :], i_=ps[:, 0:512]:
                     e.tensor_copy(o, i_), reads=(pr,), writes=(cacc0r,))
                p.op("dve", lambda e, o=cacc0[0:128, :], i_=cacc0[0:128, :],
                     s=qsv[0:128, 0:1] if is_q else qsv[0:128, 1:2],
                     a=psB[0:128, 0:512]:
                     e.scalar_tensor_tensor(o, i_, s, a, OP.mult, OP.mult),
                     reads=(cacc0r, qsvr, prB), writes=(cacc0r,))
                p.op("dve", lambda e, o=sgq[0:128, :], i_=cacc0[0:128, :]:
                     e.tensor_copy(o, i_), reads=(cacc0r,), writes=(sgqr,))
                psR, prR = ctx.psum.alloc()
                mm(p, psR[0:128, 0:512], rotm[0:128, 0:128], sgq[0:128, :],
                   (rotmr, sgqr), (prR,), True, True)
                t0c = 0 if is_q else 2 * TTW
                p.op("dve", lambda e, o=cacc0[0:128, :], a=cacc0[0:128, :],
                     b=rtab[0:128, t0c:t0c + TTW]:
                     e.tensor_tensor(o, a, b, OP.mult), reads=(cacc0r, rtabr),
                     writes=(cacc0r,))
                p.op("dve", lambda e, o=cacc1[0:128, :], a=psR[0:128, 0:512],
                     b=rtab[0:128, t0c + TTW:t0c + 2 * TTW]:
                     e.tensor_tensor(o, a, b, OP.mult), reads=(prR, rtabr),
                     writes=(cacc1r,))
                if is_q:
                    dst, dres = q_ap(oc, tt * TTW, 512), q_res[oc]
                else:
                    dst, dres = k_ap(oc - 4, 512 + tt * TTW, 512), k_res[oc - 4]
                p.op("dve", lambda e, o=dst, a=cacc0[0:128, :], b=cacc1[0:128, :]:
                     e.tensor_tensor(o, a, b, OP.add), reads=(cacc0r, cacc1r),
                     writes=(dres,))

        for grp in range(2):
            pss = [ctx.psum.alloc() for _ in range(8)]
            for kc in range(NCH):
                wv, wvr = ws.load(W["l0f_wqkv"][lay * D + kc * 128:
                                                lay * D + (kc + 1) * 128,
                                                1024:1536], 512)
                off = PADL if kc < 4 else PADL - 1
                for tc in range(8):
                    tchunk = grp * 8 + tc
                    lhs = ctx.XB[kc][0:128, off + tchunk * 128: off + (tchunk + 1) * 128]
                    ps, pr = pss[tc]
                    mm(p, ps[:, 0:512], lhs, wv, (ctx.XBr[kc], wvr), (pr,),
                       kc == 0, kc == NCH - 1)
            for tc in range(8):
                tchunk = grp * 8 + tc
                ps, pr = pss[tc]
                slot = 4 + tchunk
                p.op("act", lambda e, o=vt_ap(slot, 0, 512), i_=ps[:, 0:512]:
                     e.copy(o, i_), reads=(pr,), writes=(vt_res[slot],))

        ebufs = [ctx.small("l0f_e%d" % s, [128, 640], F32) for s in range(2)]
        etbs = [ctx.small("l0f_et%d" % s, [128, 640], BF16) for s in range(2)]
        ett = [ctx.small("l0f_ett%d" % i, [128, 128], BF16) for i in range(4)]
        oodds = [ctx.small("l0f_oo%d" % s, [64, 128], BF16) for s in range(2)]
        mxs = [ctx.small("l0f_mx%d" % s, [128, 2], F32) for s in range(2)]
        qblk = []
        for hcq in range(4):
            row = []
            for b_ in range(16):
                r_ = Res("qb%d_%d_%d" % (lay, hcq, b_))
                r_.seed_from([q_res[hcq]])
                A.hist[hcq].append(r_)
                row.append(r_)
            qblk.append(row)
        it = 0
        for h in range(8):
            hc = h // 2
            hp = (h % 2) * 64
            for w_ in range(4):
                for qb in range(4):
                    par = it % 2
                    it += 1
                    ebuf, ebufr = ebufs[par]
                    etb, etbr = etbs[par]
                    oodd, ooddr = oodds[par]
                    mx, mxr = mxs[par]
                    t0 = 512 * w_ + 128 * qb
                    qbr = qblk[hc][t0 // 128]
                    ps1, pr1 = ctx.psum.alloc()
                    ps2, pr2 = ctx.psum.alloc()
                    qa = q_ap(hc, t0, 128, p0=hp, np_=64)
                    mm(p, ps1[0:128, 0:512], qa, k_ap(hc, t0, 512, p0=hp, np_=64),
                       (qbr, k_res[hc]), (pr1,), True, True)
                    mm(p, ps2[0:128, 0:128], qa, k_ap(hc, t0 + 512, 128, p0=hp, np_=64),
                       (qbr, k_res[hc]), (pr2,), True, True)
                    mi = (qb if w_ == 0 else 4)
                    p.op("dve", lambda e, o=ebuf[0:128, 0:512], a=ps1[0:128, 0:512],
                         b=mask[0:128, mi * 640:mi * 640 + 512]:
                         e.tensor_tensor(o, a, b, OP.add), reads=(pr1, maskr),
                         writes=(ebufr,))
                    p.op("dve", lambda e, o=ebuf[0:128, 512:640], a=ps2[0:128, 0:128],
                         b=mask[0:128, mi * 640 + 512:(mi + 1) * 640]:
                         e.tensor_tensor(o, a, b, OP.add), reads=(pr2, maskr),
                         writes=(ebufr,))
                    p.op("dve", lambda e, o=mx[0:128, 0:1], i_=ebuf[0:128, 0:640]:
                         e.tensor_reduce(o, i_, AX, OP.max, negate=True),
                         reads=(ebufr,), writes=(mxr,))
                    p.op("act", lambda e, o=etb[0:128, 0:640], i_=ebuf[0:128, 0:640],
                         b=mx[0:128, 0:1], ac=mx[0:128, 1:2]:
                         e.activation(o, i_, AF.Exp, bias=b, accum_out=ac),
                         reads=(ebufr, mxr), writes=(etbr, mxr))
                    p.op("dve", lambda e, o=mx[0:128, 1:2], i_=mx[0:128, 1:2]:
                         e.reciprocal(o, i_), reads=(mxr,), writes=(mxr,))
                    p.op("dve", lambda e, o=etb[0:128, 0:640], i_=etb[0:128, 0:640],
                         s=mx[0:128, 1:2]: e.tensor_scalar_mul(o, i_, s),
                         reads=(etbr, mxr), writes=(etbr,))
                    psO, prO = ctx.psum.alloc()
                    ptr_, prT = ctx.psum.alloc()
                    for j5 in range(5):
                        pstv = ptr_[0:128, j5 * 64:(j5 + 1) * 64].bitcast(BF16)
                        p.op("pe", lambda t, o=pstv,
                             i_=etb[0:128, j5 * 128:(j5 + 1) * 128],
                             idd=idn[0:128, 0:128]: t.transpose(o, i_, idd),
                             reads=(etbr, idnr), writes=(prT,))
                        et_t, et_r = ett[par * 2 + j5 % 2]
                        p.op("dve", lambda e, o=et_t[0:128, 0:128], i_=pstv:
                             e.tensor_copy(o, i_), reads=(prT,), writes=(et_r,))
                        sp = t0 // 128 + j5
                        mm(p, psO[0:64, 0:128], vt_ap(sp, h * 64, 64),
                           et_t[0:128, 0:128], (vt_res[sp], et_r), (prO,),
                           j5 == 0, j5 == 4)
                    if hp == 0:
                        p.op("dve", lambda e, o=q_ap(hc, t0, 128, p0=0, np_=64),
                             i_=psO[0:64, 0:128]: e.tensor_copy(o, i_),
                             reads=(prO,), writes=(qbr,))
                    else:
                        p.op("dve", lambda e, o=oodd[0:64, 0:128],
                             i_=psO[0:64, 0:128]:
                             e.tensor_copy(o, i_), reads=(prO,), writes=(ooddr,))
                        p.dma(lambda q, d=q_ap(hc, t0, 128, p0=64, np_=64),
                              s=oodd[0:64, 0:128]: q.dma_start(d, s),
                              reads=(ooddr,), writes=(qbr,))

        for oc in range(NCH):
            pss = [ctx.psum.alloc() for _ in range(NTT)]
            wo, wor = ws.load(wv2(W["l0f_wo"], D, lay * 512, 4, oc * 128), 512)
            for kc in range(4):
                for tt in range(NTT):
                    ps, pr = pss[tt]
                    mm(p, ps[:, 0:512], wo[0:128, kc * 128:(kc + 1) * 128],
                       q_ap(kc, tt * TTW, 512),
                       (wor,) + tuple(qblk[kc][tt * 4:(tt + 1) * 4]), (pr,),
                       kc == 0, kc == 3)
            for tt in range(NTT):
                ps, pr = pss[tt]
                xap = ctx.Xd_tt(oc, tt)
                p.op("dve", lambda e, o=xap, a=ps[:, 0:512], b=xap:
                     e.tensor_tensor(o, a, b, OP.add), reads=(pr, ctx.Xr[oc]),
                     writes=(ctx.Xr[oc],))
        for c in range(NCH):
            p.op("pool", lambda e, o=ctx.XBdata(c), i_=ctx.Xdata(c): e.tensor_copy(o, i_),
                 reads=(ctx.Xr[c],), writes=(ctx.XBr[c],))

        lnv, lnvr = ctx.small("l0f_ln", [128, 16], F32)
        for c in range(NCH):
            p.dma(lambda q, d=W["l0f_lnb"][lay * D + c * 128:lay * D + (c + 1) * 128,
                                           0:2], o=lnv[0:128, 2 * c:2 * c + 2]:
                  q.dma_start(o, d), reads=(), writes=(lnvr,))
        v43, v43r = ctx.small("l0f_b1", [128, 86], F32)
        for i in range(22):
            rows = 42 if i == 21 else 128
            p.dma(lambda q, d=W["l0f_b1c"][lay:lay + 1, i * 128:i * 128 + rows],
                  o=v43[0:rows, 2 * i + 0:2 * i + 1]: q.dma_start(o, d),
                  reads=(), writes=(v43r,))
            p.dma(lambda q, d=W["l0f_b1c"][lay:lay + 1,
                                           2730 + i * 128:2730 + i * 128 + rows],
                  o=v43[0:rows, 2 * i + 1:2 * i + 2]: q.dma_start(o, d),
                  reads=(), writes=(v43r,))
        b2row, b2rowr = ctx.small("l0f_b2", [1, 1024], BF16)
        for half in range(2):
            bw, bwr = ws.load_f32(W["l0f_b2"][lay:lay + 1, half * 512:(half + 1) * 512],
                                  512)
            p.op("dve", lambda e, o=b2row[0:1, half * 512:(half + 1) * 512], i_=bw:
                 e.tensor_copy(o, i_), reads=(bwr,), writes=(b2rowr,))

        def x_sh(c, tt, w=TTW):
            off = PADL if c < 4 else PADL - 1
            return ctx.X[c][0:128, off + tt * TTW: off + tt * TTW + w]

        def xb_sh2(c, tt):
            off = PADL if c < 4 else PADL - 1
            return ctx.XB[c][0:128, off + tt * TTW: off + (tt + 1) * TTW]

        sq_res = [A.carve(c, "ffnsq") for c in range(NCH)]

        def fsq_ap(c, tt):
            return A.bf16(c, tt * TTW, TTW)

        nrm_res = [A.carve(c, "nrm") for c in range(NCH)]

        def nrm_ap(c, tt):
            return A.bf16(c, 2048 + tt * TTW, TTW)

        for c in range(NCH):
            p.op("act", lambda e, o=A.bf16(c, 0, 2048), i_=x_sh(c, 0, 2048):
                 e.activation(o, i_, AF.Square), reads=(ctx.Xr[c],),
                 writes=(sq_res[c],))
        rowA, rowAr = (lambda t_, r_: (t_[0:1, 512:1024], r_))(*ctx.small("l3a_c", [16, T], F32))
        rowB, rowBr = (lambda t_, r_: (t_[0:1, 1024:1536], r_))(*ctx.small("l3a_c", [16, T], F32))
        rowC, rowCr = (lambda t_, r_: (t_[0:1, 0:512], r_))(*ctx.small("cacc1", [128, 512], F32))
        for tt in range(NTT):
            psA, prA = ctx.psum.alloc()
            psB, prB = ctx.psum.alloc()
            for c in range(NCH):
                mm(p, psA[0:1, 0:512], ctx.onescol_b, xb_sh2(c, tt),
                   (ctx.onescol_b_res, ctx.XBr[c]), (prA,), c == 0, c == NCH - 1)
            for c in range(NCH):
                mm(p, psB[0:1, 0:512], ctx.onescol_b, fsq_ap(c, tt),
                   (ctx.onescol_b_res, sq_res[c]), (prB,), c == 0, c == NCH - 1)
            p.op("dve", lambda e, o=rowA[0:1, :], i_=psA[0:1, 0:512]:
                 e.tensor_single_scalar(o, i_, 1.0 / D, OP.mult), reads=(prA,),
                 writes=(rowAr,))
            p.op("act", lambda e, o=rowC[0:1, :], i_=rowA[0:1, :]:
                 e.activation(o, i_, AF.Square), reads=(rowAr,), writes=(rowCr,))
            p.op("dve", lambda e, o=rowB[0:1, :], i_=psB[0:1, 0:512]:
                 e.tensor_single_scalar(o, i_, 1.0 / D, OP.mult), reads=(prB,),
                 writes=(rowBr,))
            p.op("dve", lambda e, o=rowB[0:1, :], a=rowB[0:1, :], b=rowC[0:1, :]:
                 e.tensor_tensor(o, a, b, OP.subtract), reads=(rowBr, rowCr),
                 writes=(rowBr,))
            p.op("act", lambda e, o=rowB[0:1, :], i_=rowB[0:1, :],
                 b=ctx.eps_col[0:1, 0:1]: e.activation(o, i_, AF.Ln, bias=b),
                 reads=(rowBr, ctx.onescol_f_res), writes=(rowBr,))
            p.op("act", lambda e, o=rowB[0:1, :], i_=rowB[0:1, :],
                 sc=ctx.mhalf_col[0:1, 0:1]: e.activation(o, i_, AF.Exp, scale=sc),
                 reads=(rowBr, ctx.onescol_f_res), writes=(rowBr,))
            p.op("dve", lambda e, o=rowA[0:1, :], a=rowA[0:1, :], b=rowB[0:1, :]:
                 e.tensor_tensor(o, a, b, OP.mult), reads=(rowAr, rowBr),
                 writes=(rowAr,))
            psR, prR = ctx.psum.alloc()
            psM, prM = ctx.psum.alloc()
            mm(p, psR[0:128, 0:512], ctx.ones_f, rowB[0:1, :],
               (ctx.ones_f_res, rowBr), (prR,), True, True)
            mm(p, psM[0:128, 0:512], ctx.negones_f, rowA[0:1, :],
               (ctx.negones_f_res, rowAr), (prM,), True, True)
            for c in range(NCH):
                p.op("dve", lambda e, o=cacc0[0:128, :], a=x_sh(c, tt),
                     b=psR[0:128, 0:512]: e.tensor_tensor(o, a, b, OP.mult),
                     reads=(ctx.Xr[c], prR), writes=(cacc0r,))
                p.op("dve", lambda e, o=cacc0[0:128, :], a=cacc0[0:128, :],
                     b=psM[0:128, 0:512]: e.tensor_tensor(o, a, b, OP.add),
                     reads=(cacc0r, prM), writes=(cacc0r,))
                p.op("dve", lambda e, o=nrm_ap(c, tt), i_=cacc0[0:128, :],
                     s1=lnv[0:128, 2 * c:2 * c + 1], s2=lnv[0:128, 2 * c + 1:2 * c + 2]:
                     e.tensor_scalar(o, i_, s1, s2, OP.mult, OP.add),
                     reads=(cacc0r, lnvr), writes=(nrm_res[c],))
        def mstore(mirror, mres_d, key, dview, wap, wr):
            if key not in mres_d:
                mres_d[key] = Res("mir" + str(key))
            p.dma(lambda q, d=dview, s=wap: q.dma_start(d, s), reads=(wr,),
                  writes=(mres_d[key],), q="act")

        h_res = [None] * 22

        def hq_ap(i, rows=128):
            return A.bf16(i % 8, (i // 8) * 512, 512, p0=0, np_=rows)

        for tt in range(NTT):
            for i in range(22):
                rows = 42 if i == 21 else 128
                psA, prA = ctx.psum.alloc()
                psB, prB = ctx.psum.alloc()
                for kg0 in range(0, NCH, 4):
                    if tt == 0:
                        wa, war = ws.load(wv2(W["l0f_w1"], 5460,
                                              lay * D + kg0 * 128, 4,
                                              i * 128, rows), 4 * rows)
                        mstore(ctx.dram_w1b, ctx.dram_w1b_res, ("a", lay, i, kg0),
                               wv2(ctx.dram_w1b, 5460, lay * D + kg0 * 128, 4,
                                   i * 128, rows), wa, war)
                        wg, wgr = ws.load(wv2(W["l0f_w1"], 5460,
                                              lay * D + kg0 * 128, 4,
                                              2730 + i * 128, rows), 4 * rows)
                        mstore(ctx.dram_w1b, ctx.dram_w1b_res, ("g", lay, i, kg0),
                               wv2(ctx.dram_w1b, 5460, lay * D + kg0 * 128, 4,
                                   2730 + i * 128, rows), wg, wgr)
                    else:
                        wa, war = ws.load_bf16(
                            wv2(ctx.dram_w1b, 5460, lay * D + kg0 * 128, 4,
                                i * 128, rows), 4 * rows,
                            extra_reads=(ctx.dram_w1b_res[("a", lay, i, kg0)],))
                        wg, wgr = ws.load_bf16(
                            wv2(ctx.dram_w1b, 5460, lay * D + kg0 * 128, 4,
                                2730 + i * 128, rows), 4 * rows,
                            extra_reads=(ctx.dram_w1b_res[("g", lay, i, kg0)],))
                    for j in range(4):
                        kc = kg0 + j
                        mm(p, psA[0:rows, 0:512],
                           wa[0:128, j * rows:(j + 1) * rows], nrm_ap(kc, tt),
                           (war, nrm_res[kc]), (prA,), kc == 0, kc == NCH - 1)
                        mm(p, psB[0:rows, 0:512],
                           wg[0:128, j * rows:(j + 1) * rows], nrm_ap(kc, tt),
                           (wgr, nrm_res[kc]), (prB,), kc == 0, kc == NCH - 1)
                esc, escr = (cacc0, cacc0r) if i % 2 == 0 else (cacc1, cacc1r)
                hr = A.carve(i % 8, "hq%d_%d" % (tt, i))
                p.op("act", lambda e, o=esc[0:rows, 0:512], i_=psB[0:rows, 0:512],
                     b=v43[0:rows, 2 * i + 1:2 * i + 2]:
                     e.activation(o, i_, AF.Gelu, bias=b),
                     reads=(prB, v43r), writes=(escr,))
                p.op("dve", lambda e, o=hq_ap(i, rows), i_=psA[0:rows, 0:512],
                     s=v43[0:rows, 2 * i:2 * i + 1], a=esc[0:rows, 0:512]:
                     e.scalar_tensor_tensor(o, i_, s, a, OP.add, OP.mult),
                     reads=(prA, v43r, escr), writes=(hr,))
                h_res[i] = hr
            for oc in range(NCH):
                ps, pr = ctx.psum.alloc()
                for kg0 in range(0, 20, 4):
                    if tt == 0:
                        w2, w2r = ws.load(wv2(W["l0f_w2"], D,
                                              lay * 2730 + kg0 * 128,
                                              4, oc * 128), 512)
                        mstore(ctx.dram_w2b, ctx.dram_w2b_res, (lay, kg0, oc),
                               wv2(ctx.dram_w2b, D, lay * 2730 + kg0 * 128, 4,
                                   oc * 128), w2, w2r)
                    else:
                        w2, w2r = ws.load_bf16(
                            wv2(ctx.dram_w2b, D, lay * 2730 + kg0 * 128, 4,
                                oc * 128), 512,
                            extra_reads=(ctx.dram_w2b_res[(lay, kg0, oc)],))
                    for j in range(4):
                        i = kg0 + j
                        mm(p, ps[:, 0:512], w2[0:128, j * 128:(j + 1) * 128],
                           hq_ap(i, 128), (w2r, h_res[i]), (pr,), i == 0, False)
                for i in (20, 21):
                    rows = 42 if i == 21 else 128
                    if tt == 0:
                        w2, w2r = ws.load(W["l0f_w2"][lay * 2730 + i * 128:
                                                      lay * 2730 + i * 128 + rows,
                                                      oc * 128:(oc + 1) * 128], 128)
                        mstore(ctx.dram_w2b, ctx.dram_w2b_res, (lay, i, oc),
                               ctx.dram_w2b[lay * 2730 + i * 128:
                                            lay * 2730 + i * 128 + rows,
                                            oc * 128:(oc + 1) * 128], w2, w2r)
                    else:
                        w2, w2r = ws.load_bf16(
                            ctx.dram_w2b[lay * 2730 + i * 128:
                                         lay * 2730 + i * 128 + rows,
                                         oc * 128:(oc + 1) * 128], 128,
                            extra_reads=(ctx.dram_w2b_res[(lay, i, oc)],))
                    mm(p, ps[:, 0:512], w2[0:rows, 0:128], hq_ap(i, rows),
                       (w2r, h_res[i]), (pr,), False, False)
                mm(p, ps[:, 0:512], b2row[0:1, oc * 128:(oc + 1) * 128],
                   ctx.onesrow_b, (b2rowr, ctx.onesrow_b_res), (pr,), False, True)
                xap = ctx.Xd_tt(oc, tt)
                p.op("dve", lambda e, o=xap, a=ps[:, 0:512], b=xap:
                     e.tensor_tensor(o, a, b, OP.add), reads=(pr, ctx.Xr[oc]),
                     writes=(ctx.Xr[oc],))
        if lay == 0:
            for c in range(NCH):
                p.op("pool", lambda e, o=ctx.XBdata(c), i_=ctx.Xdata(c):
                     e.tensor_copy(o, i_), reads=(ctx.Xr[c],), writes=(ctx.XBr[c],))

    xin_res = [A.carve(c, "xin") for c in range(NCH)]
    extra = []
    for c in range(NCH):
        ap = A.f32(c, 0, T)
        p.dma(lambda q, o=ap, d=ctx.dram_xin[c * 128:(c + 1) * 128, 0:T]:
              q.dma_start(o, d), reads=(ctx.dram_xin_res,), writes=(xin_res[c],))
        extra.append((ap, xin_res[c]))
    emit_postln(ctx, extra=extra)


def emit_l2_ffn(ctx):
    p, A, ws, W = ctx.p, ctx.arena, ctx.ws, ctx.weights
    A.reset()
    ws_extra(ctx)
    NFC = 4096 * 1024
    gvec, gvr = ctx.small("l2f_g", [128, 4], F32)
    gsum, gsr = ctx.small("l2f_gs", [128, 2], F32)
    junk, junkr = ctx.small("cacc0", [128, 512], F32)
    v32, v32r = ctx.small("l2f_ln", [128, 64], F32)
    for c in range(32):
        p.dma(lambda q, d=W["l2f_ln"][c * 128:(c + 1) * 128, 0:2],
              o=v32[0:128, 2 * c:2 * c + 2]: q.dma_start(o, d), reads=(), writes=(v32r,))

    def quantize(wname, rows, cols, dram_q, dram_q_res, gc0, gc1):
        nkc = rows // 128
        npc = cols // 512
        memset(p, "dve", gsum[0:128, 0:1], 0.0, (gsr,))
        for kc in range(nkc):
            for pc in range(npc):
                wf, wfr = ws.load_f32(W[wname][kc * 128:(kc + 1) * 128,
                                               pc * 512:(pc + 1) * 512], 512)
                p.op("act", lambda e, o=junk[0:128, :], i_=wf, ac=gsum[0:128, 1:2]:
                     e.activation(o, i_, AF.Abs, accum_out=ac),
                     reads=(wfr,), writes=(junkr, gsr))
                p.op("dve", lambda e, o=gsum[0:128, 0:1], a=gsum[0:128, 0:1],
                     b=gsum[0:128, 1:2]: e.tensor_tensor(o, a, b, OP.add),
                     reads=(gsr,), writes=(gsr,))
        pst, pstr = ctx.psum.alloc()
        mm(p, pst[0:1, 0:1], ctx.onescol_f, gsum[0:128, 0:1],
           (ctx.onescol_f_res, gsr), (pstr,), True, True)
        p.op("dve", lambda e, o=gsum[0:1, 1:2], i_=pst[0:1, 0:1]:
             e.tensor_single_scalar(o, i_, 1.0 / NFC, OP.mult), reads=(pstr,),
             writes=(gsr,))
        p.op("dve", lambda e, o=gsum[0:1, 1:2], i_=gsum[0:1, 1:2]:
             e.tensor_single_scalar(o, i_, 1e-6, OP.add), reads=(gsr,), writes=(gsr,))
        p.op("dve", lambda e, o=gsum[0:1, 1:2], i_=gsum[0:1, 1:2]:
             e.tensor_single_scalar(o, i_, 0.5, OP.mult), reads=(gsr,), writes=(gsr,))
        psb, psbr = ctx.psum.alloc()
        mm(p, psb[0:128, 0:1], ctx.ones_f, gsum[0:1, 1:2],
           (ctx.ones_f_res, gsr), (psbr,), True, True)
        p.op("dve", lambda e, o=gvec[0:128, gc0:gc0 + 1], i_=psb[0:128, 0:1]:
             e.tensor_copy(o, i_), reads=(psbr,), writes=(gvr,))
        p.op("dve", lambda e, o=gvec[0:128, gc1:gc1 + 1], i_=gvec[0:128, gc0:gc0 + 1]:
             e.tensor_single_scalar(o, i_, -1.0, OP.mult), reads=(gvr,), writes=(gvr,))
        s1, s1r = ctx.small("sg0", [128, 512], BF16)
        s2, s2r = ctx.small("sg1", [128, 512], BF16)
        s3, s3r = ctx.small("sg2", [128, 512], BF16)
        for kc in range(nkc):
            for pc in range(npc):
                wf, wfr = ws.load_f32(W[wname][kc * 128:(kc + 1) * 128,
                                               pc * 512:(pc + 1) * 512], 512)
                p.op("act", lambda e, o=s1[0:128, :], i_=wf, b=gvec[0:128, gc0:gc0 + 1]:
                     e.activation(o, i_, AF.Sign, bias=b), reads=(wfr, gvr),
                     writes=(s1r,))
                p.op("act", lambda e, o=s2[0:128, :], i_=wf, b=gvec[0:128, gc1:gc1 + 1]:
                     e.activation(o, i_, AF.Sign, bias=b), reads=(wfr, gvr),
                     writes=(s2r,))
                p.op("dve", lambda e, o=s3[0:128, :], a=s1[0:128, :], b=s2[0:128, :]:
                     e.tensor_tensor(o, a, b, OP.add), reads=(s1r, s2r), writes=(s3r,))
                key = (kc, pc)
                if key not in dram_q_res:
                    dram_q_res[key] = Res(wname + str(key))
                p.dma(lambda q, d=dram_q[kc * 128:(kc + 1) * 128,
                                         pc * 512:(pc + 1) * 512], s=s3[0:128, :]:
                      q.dma_start(d, s), reads=(s3r,), writes=(dram_q_res[key],))

    quantize("l2f_wfcT", 1024, 4096, ctx.dram_q1, ctx.dram_q1_res, 0, 1)
    quantize("l2f_wprojT", 4096, 1024, ctx.dram_q2, ctx.dram_q2_res, 2, 3)

    rowA, rowAr = (lambda t_, r_: (t_[0:1, 512:1024], r_))(*ctx.small("l3a_c", [16, T], F32))
    rowB, rowBr = (lambda t_, r_: (t_[0:1, 1024:1536], r_))(*ctx.small("l3a_c", [16, T], F32))
    rowC, rowCr = (lambda t_, r_: (t_[0:1, 0:512], r_))(*ctx.small("cacc1", [128, 512], F32))
    h_res = [A.carve(t, "h") for t in range(8)]
    sq_res = [A.carve(t, "hsq") for t in range(8)]

    def h_ap(oc, tt):
        return A.bf16(oc % 8, (oc // 8) * 512, 512)

    def hsq_ap(oc, tt):
        return A.bf16(oc % 8, 2048 + (oc // 8) * 512, 512)

    for tt in range(NTT):
        for oc in range(32):
            ps, pr = ctx.psum.alloc()
            wq, wqr = ws.load_bf16(
                wv2(ctx.dram_q1, 4096, 0, 8, oc * 128), 1024,
                extra_reads=tuple(ctx.dram_q1_res[(kc, oc // 4)]
                                  for kc in range(NCH)))
            for kc in range(NCH):
                mm(p, ps[:, 0:512], wq[0:128, kc * 128:(kc + 1) * 128],
                   ctx.XBd_tt(kc, tt), (wqr, ctx.XBr[kc]),
                   (pr,), kc == 0, kc == NCH - 1)
            p.op("act", lambda e, o=h_ap(oc, tt), i_=ps[:, 0:512],
                 sc=ctx.half_col[0:128, 0:1]: e.activation(o, i_, AF.Gelu, scale=sc),
                 reads=(pr, ctx.onescol_f_res), writes=(h_res[oc % 8],))
            p.op("act", lambda e, o=hsq_ap(oc, tt), i_=h_ap(oc, tt):
                 e.activation(o, i_, AF.Square), reads=(h_res[oc % 8],),
                 writes=(sq_res[oc % 8],))
        psA, prA = ctx.psum.alloc()
        psB, prB = ctx.psum.alloc()
        for oc in range(32):
            mm(p, psA[0:1, 0:512], ctx.onescol_b, h_ap(oc, tt),
               (ctx.onescol_b_res, h_res[oc % 8]), (prA,), oc == 0, oc == 31)
        for oc in range(32):
            mm(p, psB[0:1, 0:512], ctx.onescol_b, hsq_ap(oc, tt),
               (ctx.onescol_b_res, sq_res[oc % 8]), (prB,), oc == 0, oc == 31)
        p.op("dve", lambda e, o=rowA[0:1, :], i_=psA[0:1, 0:512]:
             e.tensor_single_scalar(o, i_, 1.0 / 4096, OP.mult), reads=(prA,),
             writes=(rowAr,))
        p.op("act", lambda e, o=rowC[0:1, :], i_=rowA[0:1, :]:
             e.activation(o, i_, AF.Square), reads=(rowAr,), writes=(rowCr,))
        p.op("dve", lambda e, o=rowB[0:1, :], i_=psB[0:1, 0:512]:
             e.tensor_single_scalar(o, i_, 1.0 / 4096, OP.mult), reads=(prB,),
             writes=(rowBr,))
        p.op("dve", lambda e, o=rowB[0:1, :], a=rowB[0:1, :], b=rowC[0:1, :]:
             e.tensor_tensor(o, a, b, OP.subtract), reads=(rowBr, rowCr),
             writes=(rowBr,))
        p.op("act", lambda e, o=rowB[0:1, :], i_=rowB[0:1, :], b=ctx.eps_col[0:1, 0:1]:
             e.activation(o, i_, AF.Ln, bias=b), reads=(rowBr, ctx.onescol_f_res),
             writes=(rowBr,))
        p.op("act", lambda e, o=rowB[0:1, :], i_=rowB[0:1, :],
             sc=ctx.mhalf_col[0:1, 0:1]: e.activation(o, i_, AF.Exp, scale=sc),
             reads=(rowBr, ctx.onescol_f_res), writes=(rowBr,))
        p.op("dve", lambda e, o=rowA[0:1, :], a=rowA[0:1, :], b=rowB[0:1, :]:
             e.tensor_tensor(o, a, b, OP.mult), reads=(rowAr, rowBr), writes=(rowAr,))
        psR, prR = ctx.psum.alloc()
        psM, prM = ctx.psum.alloc()
        mm(p, psR[0:128, 0:512], ctx.ones_f, rowB[0:1, :],
           (ctx.ones_f_res, rowBr), (prR,), True, True)
        mm(p, psM[0:128, 0:512], ctx.negones_f, rowA[0:1, :],
           (ctx.negones_f_res, rowAr), (prM,), True, True)
        for oc in range(32):
            wcol = v32[0:128, 2 * (oc):2 * oc + 1]
            bcol = v32[0:128, 2 * oc + 1:2 * oc + 2]
            p.op("dve", lambda e, o=h_ap(oc, tt), i_=h_ap(oc, tt), s=wcol,
                 a=psR[0:128, 0:512]: e.scalar_tensor_tensor(o, i_, s, a, OP.mult,
                                                             OP.mult),
                 reads=(h_res[oc % 8], v32r, prR), writes=(h_res[oc % 8],))
            p.op("dve", lambda e, o=junk[0:128, :], i_=psM[0:128, 0:512], s1=wcol,
                 s2=bcol: e.tensor_scalar(o, i_, s1, s2, OP.mult, OP.add),
                 reads=(prM, v32r), writes=(junkr,))
            p.op("dve", lambda e, o=h_ap(oc, tt), a=h_ap(oc, tt), b=junk[0:128, :]:
                 e.tensor_tensor(o, a, b, OP.add), reads=(h_res[oc % 8], junkr),
                 writes=(h_res[oc % 8],))
        for oc in range(NCH):
            ps, pr = ctx.psum.alloc()
            for rb in range(4):
                wq, wqr = ws.load_bf16(
                    wv2(ctx.dram_q2, D, rb * 1024, 8, oc * 128), 1024,
                    extra_reads=tuple(ctx.dram_q2_res[(rb * 8 + kk, oc // 4)]
                                      for kk in range(8)))
                for kk in range(8):
                    kc = rb * 8 + kk
                    mm(p, ps[:, 0:512], wq[0:128, kk * 128:(kk + 1) * 128],
                       h_ap(kc, tt), (wqr, h_res[kc % 8]),
                       (pr,), kc == 0, kc == 31)
            xap = ctx.Xd_tt(oc, tt)
            p.op("dve", lambda e, o=xap, i_=ps[:, 0:512], s=ctx.half_col[0:128, 0:1],
                 a=xap: e.scalar_tensor_tensor(o, i_, s, a, OP.mult, OP.add),
                 reads=(pr, ctx.onescol_f_res, ctx.Xr[oc]), writes=(ctx.Xr[oc],))
    emit_postln(ctx)


def emit_l3_ffn(ctx):
    p, A, ws, W = ctx.p, ctx.arena, ctx.ws, ctx.weights
    A.reset()
    ws_extra(ctx)
    v32s, v32sr = ctx.small("l3f_s", [128, 128], F32)
    for c in range(32):
        p.dma(lambda q, d=W["l3f_s4"][c * 128:(c + 1) * 128, 0:4],
              o=v32s[0:128, 4 * c:4 * c + 4]: q.dma_start(o, d), reads=(), writes=(v32sr,))
    w3v, w3vr = ctx.small("l3f_w3s", [128, 8], F32)
    for c in range(NCH):
        p.dma(lambda q, d=W["l3f_w3s"][c * 128:(c + 1) * 128, 0:1],
              o=w3v[0:128, c:c + 1]: q.dma_start(o, d), reads=(), writes=(w3vr,))

    sgt, sgtr = ctx.small("sg0", [128, 512], BF16)
    cc0, cc0r = ctx.small("cacc0", [128, 512], F32)
    for oc in range(32):
        psA = [ctx.psum.alloc() for _ in range(NTT)]
        psB = [ctx.psum.alloc() for _ in range(NTT)]
        w1b, w1br = ws.load(wide_view(W["l3f_w1T"], 4096, 0, 4, oc), 512)
        w1b2, w1b2r = ws.load(wide_view(W["l3f_w1T"], 4096, 512, 4, oc), 512)
        for kc in range(NCH):
            src, srcr = (w1b, w1br) if kc < 4 else (w1b2, w1b2r)
            lh = src[0:128, (kc % 4) * 128:(kc % 4 + 1) * 128]
            for tt in range(NTT):
                mm(p, psA[tt][0][:, 0:512], lh, ctx.XBd_tt(kc, tt),
                   (srcr, ctx.XBr[kc]), (psA[tt][1],), kc == 0, kc == NCH - 1)
        w2b, w2br = ws.load(wide_view(W["l3f_w2T"], 4096, 0, 4, oc), 512)
        w2b2, w2b2r = ws.load(wide_view(W["l3f_w2T"], 4096, 512, 4, oc), 512)
        for kc in range(NCH):
            src2, src2r = (w2b, w2br) if kc < 4 else (w2b2, w2b2r)
            lh2 = src2[0:128, (kc % 4) * 128:(kc % 4 + 1) * 128]
            for tt in range(NTT):
                mm(p, psB[tt][0][:, 0:512], lh2, ctx.XBd_tt(kc, tt),
                   (src2r, ctx.XBr[kc]), (psB[tt][1],), kc == 0, kc == NCH - 1)
        for tt in range(NTT):
            sccol = v32s[0:128, 4 * oc:4 * oc + 1]
            sccol2 = v32s[0:128, 4 * oc + 1:4 * oc + 2]
            p.op("act", lambda e, o=cc0[0:128, :], i_=psA[tt][0][:, 0:512], sc=sccol:
                 e.activation(o, i_, AF.Silu, scale=sc),
                 reads=(psA[tt][1], v32sr), writes=(cc0r,))
            p.op("dve", lambda e, o=sgt[0:128, :], i_=psB[tt][0][:, 0:512], s=sccol2,
                 a=cc0[0:128, :]: e.scalar_tensor_tensor(o, i_, s, a, OP.mult, OP.mult),
                 reads=(psB[tt][1], v32sr, cc0r), writes=(sgtr,))
            p.dma(lambda q, d=ctx.dram_h[oc * 128:(oc + 1) * 128,
                                         tt * TTW:(tt + 1) * TTW], s=sgt[0:128, :]:
                  q.dma_start(d, s), reads=(sgtr,), writes=(ctx.dram_h_res,))

    hA_res = [A.carve(t, "hA") for t in range(8)]
    hB_res = [A.carve(t, "hB") for t in range(8)]

    def slotA(oc):
        return A.bf16(oc % 8, (oc // 8) * 512, 512)

    def slotB(oc):
        return A.bf16(oc % 8, 2048 + (oc // 8) * 512, 512)

    for tt in range(NTT):
        for oc in range(32):
            p.dma(lambda q, o=slotA(oc),
                  d=ctx.dram_h[oc * 128:(oc + 1) * 128, tt * TTW:(tt + 1) * TTW]:
                  q.dma_start(o, d), reads=(ctx.dram_h_res,), writes=(hA_res[oc % 8],))
        def mres(mirror_res, oc, rb, hf):
            key = (oc, rb, hf)
            if key not in mirror_res:
                mirror_res[key] = Res("mir%d_%d_%d" % key)
            return mirror_res[key]

        def wstream_mirrored(tt, w_dram, wtot, mirror, mirror_res, oc, rhs_fn,
                             rhs_res_fn, ps, pr, nkc):
            if tt == 0:
                for rb in range(nkc // 8):
                    for hf in range(2):
                        wb, wbr = ws.load(
                            wide_view(w_dram, wtot, rb * 1024 + hf * 512, 4, oc), 512)
                        p.dma(lambda q, d=wide_view(mirror, wtot,
                                                    rb * 1024 + hf * 512, 4, oc),
                              s=wb: q.dma_start(d, s), reads=(wbr,),
                              writes=(mres(mirror_res, oc, rb, hf),), q="act")
                        for kk in range(4):
                            kc = rb * 8 + hf * 4 + kk
                            lh = wb[0:128, kk * 128:(kk + 1) * 128]
                            mm(p, ps[:, 0:512], lh, rhs_fn(kc),
                               (wbr, rhs_res_fn(kc)), (pr,), kc == 0, kc == nkc - 1)
            else:
                for rb in range(nkc // 8):
                    wb, wbr = ws.load_bf16(
                        wide_view(mirror, wtot, rb * 1024, 8, oc), 1024,
                        extra_reads=(mres(mirror_res, oc, rb, 0),
                                     mres(mirror_res, oc, rb, 1)))
                    for kk in range(8):
                        kc = rb * 8 + kk
                        lh = wb[0:128, kk * 128:(kk + 1) * 128]
                        mm(p, ps[:, 0:512], lh, rhs_fn(kc),
                           (wbr, rhs_res_fn(kc)), (pr,), kc == 0, kc == nkc - 1)

        for oc in range(32):
            ps, pr = ctx.psum.alloc()
            wstream_mirrored(tt, W["l3f_h1T"], 4096, ctx.dram_h1b, ctx.dram_h1b_res,
                             oc, slotA, lambda kc: hA_res[kc % 8], ps, pr, 32)
            p.op("act", lambda e, o=slotB(oc), i_=ps[:, 0:512],
                 sc=v32s[0:128, 4 * oc + 2:4 * oc + 3]:
                 e.activation(o, i_, AF.Gelu, scale=sc),
                 reads=(pr, v32sr), writes=(hB_res[oc % 8],))
        for oc in range(32):
            ps, pr = ctx.psum.alloc()
            wstream_mirrored(tt, W["l3f_h2T"], 4096, ctx.dram_h2b, ctx.dram_h2b_res,
                             oc, slotB, lambda kc: hB_res[kc % 8], ps, pr, 32)
            p.op("act", lambda e, o=slotA(oc), i_=ps[:, 0:512],
                 sc=v32s[0:128, 4 * oc + 3:4 * oc + 4]:
                 e.activation(o, i_, AF.Gelu, scale=sc),
                 reads=(pr, v32sr), writes=(hA_res[oc % 8],))
        for oc in range(NCH):
            ps, pr = ctx.psum.alloc()
            wstream_mirrored(tt, W["l3f_w3T"], D, ctx.dram_w3b, ctx.dram_w3b_res,
                             oc, slotA, lambda kc: hA_res[kc % 8], ps, pr, 32)
            xap = ctx.Xd_tt(oc, tt)
            p.op("dve", lambda e, o=xap, i_=ps[:, 0:512], s=w3v[0:128, oc:oc + 1],
                 a=xap: e.scalar_tensor_tensor(o, i_, s, a, OP.mult, OP.add),
                 reads=(pr, w3vr, ctx.Xr[oc]), writes=(ctx.Xr[oc],))
    emit_postln(ctx)


def emit_l3_attn(ctx):
    p, A, ws, W = ctx.p, ctx.arena, ctx.ws, ctx.weights
    A.reset()
    NB = 16

    v7 = [ctx.small("l3a_tv%d" % c, [128, 8], F32) for c in range(NCH)]
    for c in range(NCH):
        vt, vr = v7[c]
        p.dma(lambda q, d=W["l3a_tvec"][c * 128:(c + 1) * 128, 0:7], o=vt[0:128, 0:7]:
              q.dma_start(o, d), reads=(), writes=(vr,))
    dmv = [ctx.small("l3a_dm%d" % c, [128, 2], F32) for c in range(NCH)]
    for c in range(NCH):
        vt, vr = dmv[c]
        p.dma(lambda q, d=W["l3a_dmix"][c * 128:(c + 1) * 128, 0:1], o=vt[0:128, 0:1]:
              q.dma_start(o, d), reads=(), writes=(vr,))
        p.op("act", lambda e, o=vt[0:128, 1:2], i_=vt[0:128, 0:1]:
             e.activation(o, i_, AF.Sigmoid), reads=(vr,), writes=(vr,))
    lnv, lnvr = ctx.small("l3a_ln", [128, 16], F32)
    for c in range(NCH):
        p.dma(lambda q, d=W["l3a_lng"][c * 128:(c + 1) * 128, 0:2],
              o=lnv[0:128, 2 * c:2 * c + 2]: q.dma_start(o, d), reads=(), writes=(lnvr,))
    hrow, hrowr = ctx.small("l3a_hr", [1, 48], F32)
    p.dma(lambda q, d=W["l3a_hrow"][0:1, 0:32], o=hrow[0:1, 0:32]: q.dma_start(o, d),
          reads=(), writes=(hrowr,))
    p.op("act", lambda e, o=hrow[0:1, 32:48], i_=hrow[0:1, 0:16]:
         e.activation(o, i_, AF.Exp), reads=(hrowr,), writes=(hrowr,))
    p.op("act", lambda e, o=hrow[0:1, 16:32], i_=hrow[0:1, 16:32]:
         e.activation(o, i_, AF.Sigmoid), reads=(hrowr,), writes=(hrowr,))
    shb, shbr = ctx.small("l3a_shb", [128, 32], F32)
    psx, psxr = ctx.psum.alloc()
    mm(p, psx[0:128, 0:16], ctx.ones_f, hrow[0:1, 16:32], (ctx.ones_f_res, hrowr),
       (psxr,), True, True)
    p.op("dve", lambda e, o=shb[0:128, 0:16], i_=psx[0:128, 0:16]: e.tensor_copy(o, i_),
         reads=(psxr,), writes=(shbr,))
    p.op("dve", lambda e, o=shb[0:128, 16:32], i_=shb[0:128, 0:16]:
         e.tensor_single_scalar(o, i_, -1.0, OP.mult), reads=(shbr,), writes=(shbr,))
    scol, scolr = ctx.small("l3a_scol", [16, 2], F32)
    pst_, pstr_ = ctx.psum.alloc()
    p.op("pe", lambda t, o=pst_[0:16, 0:1], i_=hrow[0:1, 16:32],
         idd=ctx.identf[0:1, 0:1]: t.transpose(o, i_, idd),
         reads=(hrowr, ctx.identf_res), writes=(pstr_,))
    p.op("dve", lambda e, o=scol[0:16, 0:1], i_=pst_[0:16, 0:1]: e.tensor_copy(o, i_),
         reads=(pstr_,), writes=(scolr,))
    gcs, gcsr = ctx.small("l3a_gcs", [128, 128], BF16)
    psg, psgr = ctx.psum.alloc()
    mm(p, psg[0:128, 0:16], ctx.ones_f, hrow[0:1, 32:48], (ctx.ones_f_res, hrowr),
       (psgr,), True, True)
    for c in range(NCH):
        p.op("dve", lambda e, o=gcs[0:128, 16 * c:16 * c + 16],
             a=ctx.l3a_gmask[0:128, 16 * c:16 * c + 16], b=psg[0:128, 0:16]:
             e.tensor_tensor(o, a, b, OP.mult), reads=(ctx.l3a_gmask_res, psgr),
             writes=(gcsr,))

    dx_res = [A.carve(c, "dx") for c in range(NCH)]

    def dx_ap(c, tt):
        return A.f32(c, tt * TTW, TTW)

    cc0, cc0r = ctx.small("cacc0", [128, 512], F32)
    for c in range(NCH):
        for tt in range(NTT):
            prev = ctx.X[c][0:128, PADL - 1 + tt * TTW: PADL - 1 + (tt + 1) * TTW]
            nxt = ctx.X[c][0:128, PADL + 1 + tt * TTW: PADL + 1 + (tt + 1) * TTW]
            p.op("dve", lambda e, o=cc0[0:128, :], a=prev, b=nxt:
                 e.tensor_tensor(o, a, b, OP.add), reads=(ctx.Xr[c],), writes=(cc0r,))
            p.op("dve", lambda e, o=dx_ap(c, tt), i_=cc0[0:128, :],
                 s=ctx.half_col[0:128, 0:1], a=ctx.Xd_tt(c, tt):
                 e.scalar_tensor_tensor(o, i_, s, a, OP.mult, OP.subtract),
                 reads=(cc0r, ctx.onescol_f_res, ctx.Xr[c]), writes=(dx_res[c],))

    for c in range(NCH):
        vt, vr = v7[c]
        for tt in range(NTT):
            p.op("dve", lambda e, o=ctx.XBd_tt(c, tt), i_=dx_ap(c, tt),
                 s=vt[0:128, 0:1], a=ctx.Xd_tt(c, tt):
                 e.scalar_tensor_tensor(o, i_, s, a, OP.mult, OP.add),
                 reads=(dx_res[c], vr, ctx.Xr[c]), writes=(ctx.XBr[c],))
    sg0, sg0r = ctx.small("sg0", [128, 512], BF16)
    m32, m32r = sg0[0:32, 0:512], sg0r
    for tt in range(NTT):
        psA, prA = ctx.psum.alloc()
        psB, prB = ctx.psum.alloc()
        for kc in range(NCH):
            wq, wqr = ws.load(W["l3a_maa_w1"][kc * 128:(kc + 1) * 128, 0:128], 128)
            wq2, wq2r = ws.load(W["l3a_maa_w1"][kc * 128:(kc + 1) * 128, 128:160], 32)
            mm(p, psA[:, 0:512], wq, ctx.XBd_tt(kc, tt), (wqr, ctx.XBr[kc]),
               (prA,), kc == 0, kc == NCH - 1)
            mm(p, psB[0:32, 0:512], wq2, ctx.XBd_tt(kc, tt), (wq2r, ctx.XBr[kc]),
               (prB,), kc == 0, kc == NCH - 1)
        p.op("act", lambda e, o=sg0[0:128, :], i_=psA[:, 0:512]:
             e.activation(o, i_, AF.Tanh), reads=(prA,), writes=(sg0r,))
        p.dma(lambda q, d=ctx.dram_mx[0:128, tt * TTW:(tt + 1) * TTW], s=sg0[0:128, :]:
              q.dma_start(d, s), reads=(sg0r,), writes=(ctx.dram_mx_res,))
        p.op("act", lambda e, o=m32[0:32, :], i_=psB[0:32, 0:512]:
             e.activation(o, i_, AF.Tanh), reads=(prB,), writes=(m32r,))
        p.dma(lambda q, d=ctx.dram_mx[128:160, tt * TTW:(tt + 1) * TTW], s=m32[0:32, :]:
              q.dma_start(d, s), reads=(m32r,), writes=(ctx.dram_mx_res,))

    def build_xf(f, tmcol):
        for tt in range(NTT):
            mxf, mxfr = sg0[0:32, 0:512], sg0r
            p.dma(lambda q, o=mxf, d=ctx.dram_mx[32 * f:32 * f + 32,
                                                 tt * TTW:(tt + 1) * TTW]:
                  q.dma_start(o, d), reads=(ctx.dram_mx_res,), writes=(sg0r,))
            for c in range(NCH):
                vt, vr = v7[c]
                w2s, w2sr = ws.load(W["l3a_maa_w2"][32 * f:32 * f + 32,
                                                    c * 128:(c + 1) * 128], 128)
                psf, prf = ctx.psum.alloc()
                mm(p, psf[:, 0:512], w2s, mxf, (w2sr, mxfr), (prf,), True, True)
                p.op("dve", lambda e, o=cc0[0:128, :], i_=psf[:, 0:512],
                     s=vt[0:128, tmcol:tmcol + 1], a=dx_ap(c, tt):
                     e.scalar_tensor_tensor(o, i_, s, a, OP.add, OP.mult),
                     reads=(prf, vr, dx_res[c]), writes=(cc0r,))
                p.op("dve", lambda e, o=ctx.XBd_tt(c, tt), a=cc0[0:128, :],
                     b=ctx.Xd_tt(c, tt): e.tensor_tensor(o, a, b, OP.add),
                     reads=(cc0r, ctx.Xr[c]), writes=(ctx.XBr[c],))

    crow, crowr = ctx.small("l3a_c", [16, T], F32)
    t64, t64r = (lambda t_, r_: (t_, r_))(*ctx.small("sg1", [128, 512], BF16))
    build_xf(0, 1)
    for tt in range(NTT):
        ps64, pr64 = ctx.psum.alloc()
        for kc in range(NCH):
            wtd, wtdr = ws.load(W["l3a_td_w1"][kc * 128:(kc + 1) * 128, 0:64], 64)
            mm(p, ps64[0:64, 0:512], wtd, ctx.XBd_tt(kc, tt), (wtdr, ctx.XBr[kc]),
               (pr64,), kc == 0, kc == NCH - 1)
        p.op("act", lambda e, o=t64[0:64, :], i_=ps64[0:64, 0:512]:
             e.activation(o, i_, AF.Tanh), reads=(pr64,), writes=(t64r,))
        psC, prC = ctx.psum.alloc(pin=True)
        for c in range(NCH):
            vt, vr = v7[c]
            wtd2, wtd2r = ws.load(W["l3a_td_w2"][0:64, c * 128:(c + 1) * 128], 128)
            psw, prw = ctx.psum.alloc()
            mm(p, psw[:, 0:512], wtd2, t64[0:64, :], (wtd2r, t64r), (prw,), True, True)
            p.op("act", lambda e, o=sg0[0:128, :], i_=psw[:, 0:512], b=vt[0:128, 6:7]:
                 e.activation(o, i_, AF.Exp, bias=b), reads=(prw, vr), writes=(sg0r,))
            mm(p, psC[0:16, 0:512], gcs[0:128, 16 * c:16 * c + 16], sg0[0:128, :],
               (gcsr, sg0r), (prC,), c == 0, c == NCH - 1)
        p.op("dve", lambda e, o=crow[0:16, tt * TTW:(tt + 1) * TTW],
             i_=psC[0:16, 0:512]: e.tensor_copy(o, i_), reads=(prC,), writes=(crowr,))
        ctx.psum.release(prC)
    ones_b = bass.AP(ctx.cst_t, 0, [[8, 16], [0, T]])
    p.op("dve", lambda e, o=crow[0:16, 0:T], a=ones_b, b=crow[0:16, 0:T]:
         e.tensor_tensor_scan(o, a, b, 0.0, OP.mult, OP.add),
         reads=(ctx.onescol_f_res, crowr), writes=(crowr,))

    mt, mtr = ctx.small("l3a_mt", [16, 40], F32)
    memset(p, "dve", mt[0:16, 0:1], 0.0, (mtr,))
    for i in range(NB):
        p.op("dve", lambda e, o=mt[0:16, i + 1:i + 2],
             i_=crow[0:16, 128 * i + 127:128 * i + 128]: e.tensor_copy(o, i_),
             reads=(crowr,), writes=(mtr,))
        p.op("dve", lambda e, o=mt[0:16, 20 + i:21 + i],
             i_=crow[0:16, 128 * i:128 * i + 1]: e.tensor_copy(o, i_),
             reads=(crowr,), writes=(mtr,))
    p.op("dve", lambda e, o=mt[0:16, 36:37], i_=crow[0:16, T - 1:T]:
         e.tensor_copy(o, i_), reads=(crowr,), writes=(mtr,))

    urF, urFr = ctx.small("l3a_urF", [16, 512], BF16)
    urS, urSr = ctx.small("l3a_urS", [16, 512], BF16)
    urBF, urBFr = ctx.small("l3a_urBF", [16, 512], BF16)
    urBS, urBSr = ctx.small("l3a_urBS", [16, 512], BF16)
    uz, uzr = ctx.small("l3a_uz", [16, 512], F32)

    def build_urows(tt):
        mrF = rawap(mt, tt * 4, [[40, 16], [1, 4], [0, 128]])
        p.op("dve", lambda e, o=uz[0:16, :], a=crow[0:16, tt * TTW:(tt + 1) * TTW],
             b=mrF: e.tensor_tensor(o, a, b, OP.subtract), reads=(crowr, mtr),
             writes=(uzr,))
        p.op("act", lambda e, o=urF[0:16, :], i_=uz[0:16, :], b=ctx.ln8_col[0:16, 0:1]:
             e.activation(o, i_, AF.Exp, bias=b), reads=(uzr, ctx.onescol_f_res),
             writes=(urFr,))
        p.op("act", lambda e, o=urS[0:16, :], i_=uz[0:16, :], sc=scol[0:16, 0:1],
             b=ctx.ln8_col[0:16, 0:1]: e.activation(o, i_, AF.Exp, scale=sc, bias=b),
             reads=(uzr, scolr, ctx.onescol_f_res), writes=(urSr,))
        mrB = rawap(mt, 21 + tt * 4, [[40, 16], [1, 4], [0, 128]])
        p.op("dve", lambda e, o=uz[0:16, :], a=mrB,
             b=crow[0:16, tt * TTW:(tt + 1) * TTW]:
             e.tensor_tensor(o, a, b, OP.subtract), reads=(crowr, mtr), writes=(uzr,))
        p.op("act", lambda e, o=urBF[0:16, :], i_=uz[0:16, :], b=ctx.ln8_col[0:16, 0:1]:
             e.activation(o, i_, AF.Exp, bias=b), reads=(uzr, ctx.onescol_f_res),
             writes=(urBFr,))
        p.op("act", lambda e, o=urBS[0:16, :], i_=uz[0:16, :], sc=scol[0:16, 0:1],
             b=ctx.ln8_col[0:16, 0:1]: e.activation(o, i_, AF.Exp, scale=sc, bias=b),
             reads=(uzr, scolr, ctx.onescol_f_res), writes=(urBSr,))

    RS = ("rF", "rS", "rBF", "rBS")

    def r_scaled(oc, tt, ps, pr):
        if oc == "pre":
            build_urows(tt)
            return
        p.op("dve", lambda e, o=cc0[0:128, :], i_=ps[:, 0:512]: e.tensor_copy(o, i_),
             reads=(pr,), writes=(cc0r,))
        for stream, ur, urr in (("rF", urF, urFr), ("rS", urS, urSr),
                                ("rBF", urBF, urBFr), ("rBS", urBS, urBSr)):
            psu, pru = ctx.psum.alloc()
            mm(p, psu[0:128, 0:512], ctx.l3a_ind[0:16, oc * 128:(oc + 1) * 128],
               ur[0:16, :], (ctx.l3a_ind_res, urr), (pru,), True, True)
            p.op("dve", lambda e, o=sg0[0:128, :], a=cc0[0:128, :], b=psu[0:128, 0:512]:
                 e.tensor_tensor(o, a, b, OP.mult), reads=(cc0r, pru), writes=(sg0r,))
            p.dma(lambda q, d=ctx.dram_rs[stream][oc * 128:(oc + 1) * 128,
                                                  tt * TTW:(tt + 1) * TTW],
                  s=sg0[0:128, :]: q.dma_start(d, s), reads=(sg0r,),
                  writes=(ctx.dram_rs_res[stream],))

    sgp, sgpr = ctx.small("l3a_sgp", [128, 512], BF16)

    def proj_evac_fm(wname, dram, dres, act=None, scaled=None, mul8=False):
        for tt in range(NTT):
            if scaled is not None:
                scaled("pre", tt, None, None)
            for oc in range(NCH):
                sg, sgr = (sg0, sg0r) if oc % 2 == 0 else (sgp, sgpr)
                ps, pr = ctx.psum.alloc()
                for kg0 in range(0, NCH, 4):
                    wq, wqr = ws.load(wv2(W[wname], D, kg0 * 128, 4, oc * 128), 512)
                    for j in range(4):
                        kc = kg0 + j
                        mm(p, ps[:, 0:512], wq[0:128, j * 128:(j + 1) * 128],
                           ctx.XBd_tt(kc, tt), (wqr, ctx.XBr[kc]),
                           (pr,), kc == 0, kc == NCH - 1)
                if act:
                    p.op("act", lambda e, o=sg[0:128, :], i_=ps[:, 0:512], f=act:
                         e.activation(o, i_, f), reads=(pr,), writes=(sgr,))
                elif mul8:
                    p.op("dve", lambda e, o=sg[0:128, :], i_=ps[:, 0:512]:
                         e.tensor_single_scalar(o, i_, 0.125, OP.mult),
                         reads=(pr,), writes=(sgr,))
                else:
                    p.op("dve", lambda e, o=sg[0:128, :], i_=ps[:, 0:512]:
                         e.tensor_copy(o, i_), reads=(pr,), writes=(sgr,))
                p.dma(lambda q, d=dram[oc * 128:(oc + 1) * 128,
                                       tt * TTW:(tt + 1) * TTW],
                      s=sg[0:128, :]: q.dma_start(d, s), reads=(sgr,), writes=(dres,))
                if scaled is not None:
                    scaled(oc, tt, ps, pr)

    build_xf(3, 4)
    proj_evac_fm("l3a_wr", ctx.dram_r, ctx.dram_r_res, scaled=r_scaled, mul8=True)

    wts = {n: ctx.small("l3a_wt" + n, [128, 256], F32) for n in
           ("F", "S", "BF", "BS")}
    ctt, cttr = ctx.small("l3a_ct", [128, 256], F32)
    zt, ztr = ctx.small("l3a_zt", [128, 32], F32)
    mr16, mr16r = ctx.small("l3a_mr16", [1, 16], F32)
    for i in range(NB):
        psq, prq = ctx.psum.alloc()
        p.op("pe", lambda t, o=psq[0:128, 0:16], i_=crow[0:16, 128 * i:128 * (i + 1)],
             idd=ctx.identf[0:16, 0:16]: t.transpose(o, i_, idd),
             reads=(crowr, ctx.identf_res), writes=(prq,))
        p.op("dve", lambda e, o=ctt[0:128, 16 * i:16 * i + 16], i_=psq[0:128, 0:16]:
             e.tensor_copy(o, i_), reads=(prq,), writes=(cttr,))
    for i in range(NB):
        for bwd, fn_, sn_ in ((0, "F", "S"), (1, "BF", "BS")):
            col = (i + 1) if not bwd else (20 + i)
            psq, prq = ctx.psum.alloc()
            p.op("pe", lambda t, o=psq[0:1, 0:16], i_=mt[0:16, col:col + 1],
                 idd=ctx.identf[0:16, 0:16]: t.transpose(o, i_, idd),
                 reads=(mtr, ctx.identf_res), writes=(prq,))
            p.op("dve", lambda e, o=mr16[0:1, 0:16], i_=psq[0:1, 0:16]:
                 e.tensor_copy(o, i_), reads=(prq,), writes=(mr16r,))
            psb, prb = ctx.psum.alloc()
            mm(p, psb[0:128, 0:16], ctx.ones_f, mr16[0:1, 0:16],
               (ctx.ones_f_res, mr16r), (prb,), True, True)
            if not bwd:
                p.op("dve", lambda e, o=zt[0:128, 0:16], a=psb[0:128, 0:16],
                     b=ctt[0:128, 16 * i:16 * i + 16]:
                     e.tensor_tensor(o, a, b, OP.subtract), reads=(prb, cttr),
                     writes=(ztr,))
            else:
                p.op("dve", lambda e, o=zt[0:128, 0:16],
                     a=ctt[0:128, 16 * i:16 * i + 16], b=psb[0:128, 0:16]:
                     e.tensor_tensor(o, a, b, OP.subtract), reads=(cttr, prb),
                     writes=(ztr,))
            wf, wfr_ = wts[fn_]
            wsl, wslr_ = wts[sn_]
            p.op("act", lambda e, o=wf[0:128, 16 * i:16 * i + 16], i_=zt[0:128, 0:16]:
                 e.activation(o, i_, AF.Exp), reads=(ztr,), writes=(wfr_,))
            p.op("dve", lambda e, o=zt[0:128, 16:32], a=zt[0:128, 0:16],
                 b=shb[0:128, 0:16]: e.tensor_tensor(o, a, b, OP.mult),
                 reads=(ztr, shbr), writes=(ztr,))
            p.op("act", lambda e, o=wsl[0:128, 16 * i:16 * i + 16],
                 i_=zt[0:128, 16:32]: e.activation(o, i_, AF.Exp), reads=(ztr,),
                 writes=(wslr_,))

    lams = {n: ctx.small("l3a_l" + n, [64, 256], F32) for n in ("F", "S", "BF", "BS")}
    lr, lrr = ctx.small("l3a_lr", [1, 48], F32)
    for i in range(NB):
        for bwd, fn_, sn_ in ((0, "F", "S"), (1, "BF", "BS")):
            if not bwd:
                ca, cb = i + 1, i
            else:
                ca, cb = 20 + i + 1, 20 + i
            p.op("dve", lambda e, o=mt[0:16, 38:39], a=mt[0:16, ca:ca + 1],
                 b=mt[0:16, cb:cb + 1]: e.tensor_tensor(o, a, b, OP.subtract),
                 reads=(mtr,), writes=(mtr,))
            psq, prq = ctx.psum.alloc()
            p.op("pe", lambda t, o=psq[0:1, 0:16], i_=mt[0:16, 38:39],
                 idd=ctx.identf[0:16, 0:16]: t.transpose(o, i_, idd),
                 reads=(mtr, ctx.identf_res), writes=(prq,))
            p.op("dve", lambda e, o=lr[0:1, 0:16], i_=psq[0:1, 0:16]:
                 e.tensor_copy(o, i_), reads=(prq,), writes=(lrr,))
            p.op("act", lambda e, o=lr[0:1, 16:32], i_=lr[0:1, 0:16]:
                 e.activation(o, i_, AF.Exp), reads=(lrr,), writes=(lrr,))
            p.op("dve", lambda e, o=lr[0:1, 32:48], a=lr[0:1, 0:16],
                 b=hrow[0:1, 16:32]: e.tensor_tensor(o, a, b, OP.mult),
                 reads=(lrr, hrowr), writes=(lrr,))
            p.op("act", lambda e, o=lr[0:1, 32:48], i_=lr[0:1, 32:48]:
                 e.activation(o, i_, AF.Exp), reads=(lrr,), writes=(lrr,))
            psb, prb = ctx.psum.alloc()
            mm(p, psb[0:64, 0:32], ctx.ones_f[0:1, 0:64], lr[0:1, 16:48],
               (ctx.ones_f_res, lrr), (prb,), True, True)
            lf, lfr_ = lams[fn_]
            lsl, lslr_ = lams[sn_]
            p.op("dve", lambda e, o=lf[0:64, 16 * i:16 * i + 16], i_=psb[0:64, 0:16]:
                 e.tensor_copy(o, i_), reads=(prb,), writes=(lfr_,))
            p.op("dve", lambda e, o=lsl[0:64, 16 * i:16 * i + 16],
                 i_=psb[0:64, 16:32]: e.tensor_copy(o, i_), reads=(prb,),
                 writes=(lslr_,))

    build_xf(1, 2)
    proj_evac_fm("l3a_wk", ctx.dram_k, ctx.dram_k_res)

    def tm_proj(wname, scaled_kt):
        for grp in range(2):
            for half in range(2):
                pss = [ctx.psum.alloc() for _ in range(8)]
                for kc in range(NCH):
                    wv, wvr = ws.load(W[wname][kc * 128:(kc + 1) * 128,
                                               half * 512:(half + 1) * 512], 512)
                    for tc in range(8):
                        tchunk = grp * 8 + tc
                        lhs = ctx.XB[kc][0:128, PADL + tchunk * 128:
                                         PADL + (tchunk + 1) * 128]
                        ps, pr = pss[tc]
                        mm(p, ps[:, 0:512], lhs, wv, (ctx.XBr[kc], wvr), (pr,),
                           kc == 0, kc == NCH - 1)
                for tc in range(8):
                    tchunk = grp * 8 + tc
                    ps, pr = pss[tc]
                    if not scaled_kt:
                        p.op("act", lambda e, o=sg0[0:128, :], i_=ps[:, 0:512]:
                             e.copy(o, i_), reads=(pr,), writes=(sg0r,))
                        p.dma(lambda q,
                              d=ctx.dram_vT[tchunk * 128:(tchunk + 1) * 128,
                                            half * 512:(half + 1) * 512],
                              s=sg0[0:128, :]: q.dma_start(d, s), reads=(sg0r,),
                              writes=(ctx.dram_vT_res,))
                    else:
                        for si_, stream in enumerate(("kF", "kS", "kBF", "kBS")):
                            wt_, wt_r = wts[stream[1:]]
                            sg, sgr = (sg0, sg0r) if si_ % 2 == 0 else (sgp, sgpr)
                            wb3 = rawap(wt_, 16 * tchunk + half * 8,
                                        [[256, 128], [1, 8], [0, 64]])
                            o3 = rawap(sg, 0, [[512, 128], [64, 8], [1, 64]])
                            a3 = rawap(ps, 0, [[512, 128], [64, 8], [1, 64]])
                            p.op("dve", lambda e, o=o3, a=a3, b=wb3:
                                 e.tensor_tensor(o, a, b, OP.mult),
                                 reads=(pr, wt_r), writes=(sgr,))
                            p.dma(lambda q,
                                  d=ctx.dram_rs[stream][tchunk * 128:
                                                        (tchunk + 1) * 128,
                                                        half * 512:(half + 1) * 512],
                                  s=sg[0:128, :]: q.dma_start(d, s), reads=(sgr,),
                                  writes=(ctx.dram_rs_res[stream],))

    tm_proj("l3a_wk", True)
    build_xf(2, 3)
    tm_proj("l3a_wv", False)
    build_xf(4, 5)
    proj_evac_fm("l3a_wg", ctx.dram_g, ctx.dram_g_res, act=AF.Silu)

    alc, alcr = ctx.small("l3a_al", [64, 16], F32)
    for c in range(NCH):
        for hh in range(2):
            vt, vr = dmv[c]
            p.dma(lambda q, d=alc[0:64, 2 * c + hh:2 * c + hh + 1],
                  s=vt[hh * 64:(hh + 1) * 64, 1:2]: q.dma_start(d, s),
                  reads=(vr,), writes=(alcr,))
    st, st_cr = ctx.small("l3a_st", [64, 128], F32)
    stB, stB_cr = ctx.small("l3a_stB", [64, 128], F32)
    last_st = {"Ff": st_cr, "Sf": st_cr, "Fb": stB_cr, "Sb": stB_cr}
    snaps = {n: ctx.small("l3a_s" + n, [64, 1024], BF16) for n in
             ("F", "S", "BF", "BS")}
    ebf, ebfr = (lambda t_, r_: (t_[0:128, 0:128], r_))(*ctx.small("cacc1", [128, 512], F32))
    eF, eFr = ctx.small("l3a_eF", [128, 128], BF16)
    eS, eSr = ctx.small("l3a_eS", [128, 128], BF16)
    sd, sdr = ctx.small("l3a_sd", [128, 128], BF16)
    yev, yevr = (lambda t_, r_: (t_[0:128, 0:512], r_))(*ctx.small("l0f_e", [128, 640], F32))
    cr0, cr0r = (lambda t_, r_: (t_[0:1, 0:128], r_))(*ctx.small("l0f_r2", [2, TTW], F32))

    def head_col_load(dram_ap, h, tile, coff, res, src_res):
        dst = A.bf16(tile, coff, 1024).rearrange("p (tc d) -> p tc d", d=64)
        src = bass.AP(dram_ap.tensor, h * 64, [[1024, 128], [131072, 16], [1, 64]])
        p.dma(lambda q, o=dst, s=src: q.dma_start(o, s), reads=(src_res,),
              writes=(res,))

    for c in range(NCH):
        kd_res = A.carve(5, "kd%d" % c)
        rd_res = A.carve(6, "rd%d" % c)
        p.dma(lambda q, o=A.bf16(5, 0, T), d=ctx.dram_k[c * 128:(c + 1) * 128, 0:T]:
              q.dma_start(o, d), reads=(ctx.dram_k_res,), writes=(kd_res,))
        p.dma(lambda q, o=A.bf16(6, 0, T), d=ctx.dram_r[c * 128:(c + 1) * 128, 0:T]:
              q.dma_start(o, d), reads=(ctx.dram_r_res,), writes=(rd_res,))
        for hh in range(2):
            h = 2 * c + hh
            hp = hh * 64
            rF_res = A.carve(0, "rF")
            rS_res = A.carve(1, "rS")
            ktF_res = A.carve(2, "ktF")
            ktS_res = A.carve(3, "ktS")
            vth_res = A.carve(4, "vth")
            p.dma(lambda q, o=A.bf16(0, 0, T, p0=0, np_=64),
                  d=ctx.dram_rs["rF"][c * 128 + hp:c * 128 + hp + 64, 0:T]:
                  q.dma_start(o, d), reads=(ctx.dram_rs_res["rF"],), writes=(rF_res,))
            p.dma(lambda q, o=A.bf16(0, T, T, p0=0, np_=64),
                  d=ctx.dram_rs["rBF"][c * 128 + hp:c * 128 + hp + 64, 0:T]:
                  q.dma_start(o, d), reads=(ctx.dram_rs_res["rBF"],), writes=(rF_res,))
            p.dma(lambda q, o=A.bf16(1, 0, T, p0=0, np_=64),
                  d=ctx.dram_rs["rS"][c * 128 + hp:c * 128 + hp + 64, 0:T]:
                  q.dma_start(o, d), reads=(ctx.dram_rs_res["rS"],), writes=(rS_res,))
            p.dma(lambda q, o=A.bf16(1, T, T, p0=0, np_=64),
                  d=ctx.dram_rs["rBS"][c * 128 + hp:c * 128 + hp + 64, 0:T]:
                  q.dma_start(o, d), reads=(ctx.dram_rs_res["rBS"],), writes=(rS_res,))
            head_col_load(ctx.dram_rs["kF"], h, 2, 0, ktF_res,
                          ctx.dram_rs_res["kF"])
            head_col_load(ctx.dram_rs["kBF"], h, 2, 1024, ktF_res,
                          ctx.dram_rs_res["kBF"])
            head_col_load(ctx.dram_rs["kS"], h, 3, 0, ktS_res,
                          ctx.dram_rs_res["kS"])
            head_col_load(ctx.dram_rs["kBS"], h, 3, 1024, ktS_res,
                          ctx.dram_rs_res["kBS"])
            head_col_load(ctx.dram_vT, h, 4, 0, vth_res, ctx.dram_vT_res)

            chain_r = {}
            for key in ("Ff", "Sf", "Fb", "Sb"):
                r_ = Res("st%s_%d" % (key, h))
                r_.seed_from([last_st[key]])
                last_st[key] = r_
                chain_r[key] = r_
            memset(p, "dve", st[0:64, 0:64], 0.0, (chain_r["Ff"],))
            memset(p, "dve", st[0:64, 64:128], 0.0, (chain_r["Sf"],))
            memset(p, "dve", stB[0:64, 0:64], 0.0, (chain_r["Fb"],))
            memset(p, "dve", stB[0:64, 64:128], 0.0, (chain_r["Sb"],))
            for step in range(NB):
                for bwd in (0, 1):
                    i = step if not bwd else NB - 1 - step
                    ktoff = 0 if not bwd else 1024
                    tile_st = st if not bwd else stB
                    names = ("F", "S") if not bwd else ("BF", "BS")
                    for sidx in (0, 1):
                        snn = names[sidx]
                        stc = 64 * sidx
                        cres = chain_r[("F" if sidx == 0 else "S") +
                                       ("f" if not bwd else "b")]
                        sn_t, sn_r = snaps[snn]
                        lam_t, lam_r = lams[snn]
                        p.op("dve", lambda e, o=sn_t[0:64, 64 * i:64 * (i + 1)],
                             i_=tile_st[0:64, stc:stc + 64]: e.tensor_copy(o, i_),
                             reads=(cres,), writes=(sn_r,))
                        psB, prB = ctx.psum.alloc()
                        ktres = ktF_res if sidx == 0 else ktS_res
                        mm(p, psB[0:64, 0:64],
                           A.bf16(2 + sidx, ktoff + i * 64, 64),
                           A.bf16(4, i * 64, 64), (ktres, vth_res), (prB,),
                           True, True)
                        p.op("dve", lambda e, o=tile_st[0:64, stc:stc + 64],
                             i_=tile_st[0:64, stc:stc + 64],
                             s=lam_t[0:64, 16 * i + h:16 * i + h + 1],
                             a=psB[0:64, 0:64]:
                             e.scalar_tensor_tensor(o, i_, s, a, OP.mult, OP.add),
                             reads=(cres, lam_r, prB), writes=(cres,))

            ebf2r = A.carve(7, "ebf2_%d" % h)
            e2r = A.carve(7, "e2_%d" % h)
            ebf2 = A.f32(7, 192, 128)
            eF2 = A.bf16(7, 0, 128)
            eS2 = A.bf16(7, 128, 128)
            sd2 = A.bf16(7, 256, 128)
            cr0b = A.f32(7, 320, 128, p0=0, np_=1)
            for tt in range(NTT):
                psYf, prYf = ctx.psum.alloc(pin=True)
                psYs, prYs = ctx.psum.alloc(pin=True)
                for ii in range(4):
                    i = 4 * tt + ii
                    coli = ii * 128
                    if ii % 2 == 1:
                        c_ebf, c_ebfr = ebf2, ebf2r
                        c_eF, c_eFr = eF2, e2r
                        c_eS, c_eSr = eS2, e2r
                        c_sd, c_sdr = sd2, e2r
                        c_cr0, c_cr0r = cr0b, ebf2r
                    else:
                        c_ebf, c_ebfr = ebf, ebfr
                        c_eF, c_eFr = eF, eFr
                        c_eS, c_eSr = eS, eSr
                        c_sd, c_sdr = sd, sdr
                        c_cr0, c_cr0r = cr0, cr0r
                    for sidx, (psY, prY) in enumerate(((psYf, prYf), (psYs, prYs))):
                        sn_t, sn_r = snaps["F" if sidx == 0 else "S"]
                        snb_t, snb_r = snaps["BF" if sidx == 0 else "BS"]
                        rres = rF_res if sidx == 0 else rS_res
                        rtile = 0 if sidx == 0 else 1
                        mm(p, psY[0:64, coli:coli + 128],
                           sn_t[0:64, 64 * i:64 * (i + 1)],
                           A.bf16(rtile, i * 128, 128, p0=0, np_=64),
                           (sn_r, rres), (prY,), ii == 0, False)
                        mm(p, psY[0:64, coli:coli + 128],
                           snb_t[0:64, 64 * i:64 * (i + 1)],
                           A.bf16(rtile, T + i * 128, 128, p0=0, np_=64),
                           (snb_r, rres), (prY,), False, False)
                    psS, prS = ctx.psum.alloc()
                    mm(p, psS[0:128, 0:128], A.bf16(5, i * 128, 128, p0=hp, np_=64),
                       A.bf16(6, i * 128, 128, p0=hp, np_=64), (kd_res, rd_res),
                       (prS,), True, True)
                    psq2, prq2 = ctx.psum.alloc()
                    p.op("pe", lambda t, o=psq2[0:1, 0:128],
                         i_=ctt[0:128, 16 * i + h:16 * i + h + 1],
                         idd=ctx.identf[0:128, 0:128]: t.transpose(o, i_, idd),
                         reads=(cttr, ctx.identf_res), writes=(prq2,))
                    p.op("dve", lambda e, o=c_cr0[0:1, 0:128], i_=psq2[0:1, 0:128]:
                         e.tensor_copy(o, i_), reads=(prq2,), writes=(c_cr0r,))
                    psb2, prb2 = ctx.psum.alloc()
                    mm(p, psb2[0:128, 0:128], ctx.ones_f, c_cr0[0:1, 0:128],
                       (ctx.ones_f_res, c_cr0r), (prb2,), True, True)
                    p.op("dve", lambda e, o=c_ebf[0:128, 0:128],
                         i_=psb2[0:128, 0:128],
                         s=ctt[0:128, 16 * i + h:16 * i + h + 1]:
                         e.tensor_scalar_sub(o, i_, s), reads=(prb2, cttr),
                         writes=(c_ebfr,))
                    p.op("act", lambda e, o=c_ebf[0:128, 0:128],
                         i_=c_ebf[0:128, 0:128]:
                         e.activation(o, i_, AF.Abs), reads=(c_ebfr,),
                         writes=(c_ebfr,))
                    p.op("act", lambda e, o=c_eF[0:128, 0:128],
                         i_=c_ebf[0:128, 0:128]:
                         e.activation(o, i_, AF.Exp, scale=-1.0), reads=(c_ebfr,),
                         writes=(c_eFr,))
                    p.op("act", lambda e, o=c_eS[0:128, 0:128],
                         i_=c_ebf[0:128, 0:128],
                         sc=shb[0:128, 16 + h:16 + h + 1]:
                         e.activation(o, i_, AF.Exp, scale=sc),
                         reads=(c_ebfr, shbr), writes=(c_eSr,))
                    for sidx, (psY, prY), (et, etr) in (
                            (0, (psYf, prYf), (c_eF, c_eFr)),
                            (1, (psYs, prYs), (c_eS, c_eSr))):
                        p.op("dve", lambda e, o=c_sd[0:128, 0:128],
                             a=psS[0:128, 0:128],
                             b=et[0:128, 0:128]: e.tensor_tensor(o, a, b, OP.mult),
                             reads=(prS, etr), writes=(c_sdr,))
                        mm(p, psY[0:64, coli:coli + 128], A.bf16(4, i * 64, 64),
                           c_sd[0:128, 0:128], (vth_res, c_sdr), (prY,),
                           False, ii == 3)
                av, avr = dmv[c]
                p.op("dve", lambda e, o=yev[0:64, :], i_=psYs[0:64, 0:512]:
                     e.tensor_copy(o, i_), reads=(prYs,), writes=(yevr,))
                p.op("dve", lambda e, o=cc0[0:64, 0:512], a=psYf[0:64, 0:512],
                     b=yev[0:64, :]: e.tensor_tensor(o, a, b, OP.subtract),
                     reads=(prYf, yevr), writes=(cc0r,))
                p.op("dve", lambda e, o=yev[0:64, :], i_=cc0[0:64, 0:512],
                     s=alc[0:64, 2 * c + hh:2 * c + hh + 1], a=yev[0:64, :]:
                     e.scalar_tensor_tensor(o, i_, s, a, OP.mult, OP.add),
                     reads=(cc0r, alcr, yevr), writes=(yevr,))
                p.dma(lambda q, d=ctx.dram_y[h * 64:(h + 1) * 64,
                                             tt * TTW:(tt + 1) * TTW],
                      s=yev[0:64, :]: q.dma_start(d, s), reads=(yevr,),
                      writes=(ctx.dram_y_res,))
                ctx.psum.release(prYf)
                ctx.psum.release(prYs)

    grp64f, grp64fr = ctx.small("grp64f", [128, 2], F32)
    memset(p, "pool", grp64f[0:128, 0:2], 0.0, (grp64fr,))
    memset(p, "pool", grp64f[0:64, 0:1], 1.0, (grp64fr,))
    memset(p, "pool", grp64f[64:128, 1:2], 1.0, (grp64fr,))
    grp2f, grp2fr = ctx.small("grp2f", [2, 128], F32)
    p.dma(lambda q, d=W["l0f_grp"][0:2, 0:128], o=grp2f[0:2, 0:128]:
          q.dma_start(o, d), reads=(), writes=(grp2fr,))
    yl, ylr = (lambda t_, r_: (t_[0:128, 0:512], r_))(*ctx.small("l0f_e", [128, 640], F32))
    ysq, ysqr = ctx.small("cacc1", [128, 512], F32)
    gl, glr = ctx.small("sg1", [128, 512], BF16)
    r2, r2r = ctx.small("l0f_r2", [2, TTW], F32)
    uz2, uz2r = ctx.small("l3a_uz", [16, 512], F32)
    yh_res = [A.carve(c, "yh") for c in range(NCH)]

    def yh_ap(c, tt):
        return A.bf16(c, tt * TTW, TTW)

    for c in range(NCH):
        for tt in range(NTT):
            p.dma(lambda q, o=yl[0:128, :],
                  d=ctx.dram_y[c * 128:(c + 1) * 128, tt * TTW:(tt + 1) * TTW]:
                  q.dma_start(o, d), reads=(ctx.dram_y_res,), writes=(ylr,))
            p.dma(lambda q, o=gl[0:128, :],
                  d=ctx.dram_g[c * 128:(c + 1) * 128, tt * TTW:(tt + 1) * TTW]:
                  q.dma_start(o, d), reads=(ctx.dram_g_res,), writes=(glr,))
            p.op("act", lambda e, o=ysq[0:128, :], i_=yl[0:128, :]:
                 e.activation(o, i_, AF.Square), reads=(ylr,), writes=(ysqr,))
            psA, prA = ctx.psum.alloc()
            psB, prB = ctx.psum.alloc()
            mm(p, psA[0:2, 0:512], grp64f[0:128, 0:2], yl[0:128, :],
               (grp64fr, ylr), (prA,), True, True)
            mm(p, psB[0:2, 0:512], grp64f[0:128, 0:2], ysq[0:128, :],
               (grp64fr, ysqr), (prB,), True, True)
            p.op("dve", lambda e, o=r2[0:2, 0:512], i_=psA[0:2, 0:512]:
                 e.tensor_single_scalar(o, i_, 1.0 / 64, OP.mult), reads=(prA,),
                 writes=(r2r,))
            p.op("act", lambda e, o=uz2[0:2, 0:512], i_=r2[0:2, 0:512]:
                 e.activation(o, i_, AF.Square), reads=(r2r,), writes=(uz2r,))
            p.op("dve", lambda e, o=ysq[0:2, 0:512], i_=psB[0:2, 0:512]:
                 e.tensor_single_scalar(o, i_, 1.0 / 64, OP.mult), reads=(prB,),
                 writes=(ysqr,))
            p.op("dve", lambda e, o=ysq[0:2, 0:512], a=ysq[0:2, 0:512],
                 b=uz2[0:2, 0:512]: e.tensor_tensor(o, a, b, OP.subtract),
                 reads=(ysqr, uz2r), writes=(ysqr,))
            p.op("act", lambda e, o=ysq[0:2, 0:512], i_=ysq[0:2, 0:512],
                 b=ctx.gn_eps_col[0:2, 0:1]: e.activation(o, i_, AF.Ln, bias=b),
                 reads=(ysqr, ctx.onescol_f_res), writes=(ysqr,))
            p.op("act", lambda e, o=ysq[0:2, 0:512], i_=ysq[0:2, 0:512],
                 sc=ctx.mhalf_col[0:2, 0:1]: e.activation(o, i_, AF.Exp, scale=sc),
                 reads=(ysqr, ctx.onescol_f_res), writes=(ysqr,))
            p.op("dve", lambda e, o=r2[0:2, 0:512], a=r2[0:2, 0:512],
                 b=ysq[0:2, 0:512]: e.tensor_tensor(o, a, b, OP.mult),
                 reads=(r2r, ysqr), writes=(r2r,))
            p.op("dve", lambda e, o=r2[0:2, 0:512], i_=r2[0:2, 0:512]:
                 e.tensor_single_scalar(o, i_, -1.0, OP.mult), reads=(r2r,),
                 writes=(r2r,))
            psR, prR = ctx.psum.alloc()
            psM, prM = ctx.psum.alloc()
            mm(p, psR[0:128, 0:512], grp2f[0:2, 0:128], ysq[0:2, 0:512],
               (grp2fr, ysqr), (prR,), True, True)
            mm(p, psM[0:128, 0:512], grp2f[0:2, 0:128], r2[0:2, 0:512],
               (grp2fr, r2r), (prM,), True, True)
            p.op("dve", lambda e, o=cc0[0:128, :], a=yl[0:128, :],
                 b=psR[0:128, 0:512]: e.tensor_tensor(o, a, b, OP.mult),
                 reads=(ylr, prR), writes=(cc0r,))
            p.op("dve", lambda e, o=cc0[0:128, :], a=cc0[0:128, :],
                 b=psM[0:128, 0:512]: e.tensor_tensor(o, a, b, OP.add),
                 reads=(cc0r, prM), writes=(cc0r,))
            p.op("dve", lambda e, o=yl[0:128, :], i_=cc0[0:128, :],
                 s1=lnv[0:128, 2 * c:2 * c + 1], s2=lnv[0:128, 2 * c + 1:2 * c + 2]:
                 e.tensor_scalar(o, i_, s1, s2, OP.mult, OP.add),
                 reads=(cc0r, lnvr), writes=(ylr,))
            p.op("dve", lambda e, o=yh_ap(c, tt), a=yl[0:128, :], b=gl[0:128, :]:
                 e.tensor_tensor(o, a, b, OP.mult), reads=(ylr, glr),
                 writes=(yh_res[c],))
    for oc in range(NCH):
        pss = [ctx.psum.alloc() for _ in range(NTT)]
        for kg0 in range(0, NCH, 4):
            wo, wor = ws.load(wv2(W["l3a_wo"], D, kg0 * 128, 4, oc * 128), 512)
            for j in range(4):
                kc = kg0 + j
                for tt in range(NTT):
                    ps, pr = pss[tt]
                    mm(p, ps[:, 0:512], wo[0:128, j * 128:(j + 1) * 128],
                       yh_ap(kc, tt), (wor, yh_res[kc]), (pr,),
                       kc == 0, kc == NCH - 1)
        for tt in range(NTT):
            ps, pr = pss[tt]
            xap = ctx.Xd_tt(oc, tt)
            p.op("dve", lambda e, o=xap, a=ps[:, 0:512], b=xap:
                 e.tensor_tensor(o, a, b, OP.add), reads=(pr, ctx.Xr[oc]),
                 writes=(ctx.Xr[oc],))
    emit_postln(ctx)


BLOCK_EMITTERS = {}


def _register():
    BLOCK_EMITTERS["l0a"] = emit_l0_attn
    BLOCK_EMITTERS["l2a"] = emit_l2_attn
    BLOCK_EMITTERS["l1f"] = emit_l1_ffn
    BLOCK_EMITTERS["l1a"] = emit_l1_attn
    BLOCK_EMITTERS["l2f"] = emit_l2_ffn
    BLOCK_EMITTERS["l3f"] = emit_l3_ffn
    BLOCK_EMITTERS["l0f"] = emit_l0_ffn
    BLOCK_EMITTERS["l3a"] = emit_l3_attn


def build_nc(block_ids, weight_specs, trunc=None):
    _register()
    nc = bass.Bass("TRN2", target_bir_lowering=False, dynamic_dma_scratch_size=256)
    stack = ExitStack()
    ctx = Ctx()
    ctx.nc = nc
    ctx.stack = stack
    p = Prog(nc)
    ctx.p = p

    x_in = nc.dram_tensor("x_t", [D, T], F32, kind="ExternalInput").ap()
    y_out = nc.dram_tensor("y_t", [D, T], F32, kind="ExternalOutput").ap()
    W = {}
    for name, spec in weight_specs.items():
        if isinstance(spec[-1], str):
            shape, dt = spec[:-1], (BF16 if spec[-1] == "bf16" else F32)
        else:
            shape, dt = spec, F32
        W[name] = nc.dram_tensor(name, list(shape), dt, kind="ExternalInput").ap()
    ctx.weights = W

    with stack:
        ctx.X = [stack.enter_context(nc.sbuf_tensor(f"X{c}", [128, TF], F32))
                 for c in range(NCH)]
        ctx.XB = [stack.enter_context(nc.sbuf_tensor(f"XB{c}", [128, TF], BF16))
                  for c in range(NCH)]
        ctx.Xr = [Res(f"X{c}") for c in range(NCH)]
        ctx.XBr = [Res(f"XB{c}") for c in range(NCH)]
        ctx.arena = Arena(nc, stack)
        ctx.psum = PsumPool(nc, stack)
        ctx.ws = WStream(nc, stack, p)
        cst, cstr = ctx.small("consts", [128, 8], F32)
        cstb, cstbr = ctx.small("constsb", [128, 8], BF16)
        memset(p, "pool", cst[0:128, 0:1], 1.0, (cstr,))
        memset(p, "pool", cst[0:128, 1:2], -1.0, (cstr,))
        memset(p, "pool", cst[0:128, 2:3], 1e-5, (cstr,))
        memset(p, "pool", cst[0:128, 3:4], -0.5, (cstr,))
        memset(p, "pool", cst[0:128, 4:5], 0.5, (cstr,))
        memset(p, "pool", cst[0:128, 5:6], 6.4e-4, (cstr,))
        memset(p, "pool", cst[0:128, 6:7], -2.0794415416798357, (cstr,))
        ctx.ln8_col = cst[0:128, 6:7]
        ctx.eps_col = cst[0:128, 2:3]
        ctx.mhalf_col = cst[0:128, 3:4]
        ctx.half_col = cst[0:128, 4:5]
        ctx.gn_eps_col = cst[0:128, 5:6]
        ctx.cst_t = cst
        identf, identfr = ctx.small("identf", [128, 128], F32)
        iotf, iotfr = (lambda t_, r_: (t_[0:128, 0:128], r_))(
            *ctx.small("cacc0", [128, 512], F32))
        p.op("pool", lambda e, o=iotf[0:128, 0:128]: e.iota(o, [[1, 128]], base=0,
             channel_multiplier=-1, allow_small_or_imprecise_dtypes=True),
             reads=(), writes=(iotfr,))
        memset(p, "pool", identf[0:128, 0:128], 1.0, (identfr,))
        p.op("pool", lambda e, o=identf[0:128, 0:128], i_=identf[0:128, 0:128]:
             e.affine_select(o, i_, [[1, 128]], OP.is_equal, 0.0, base=0,
                             channel_multiplier=-1), reads=(identfr, iotfr),
             writes=(identfr,))
        ctx.identf = identf
        ctx.identf_res = identfr
        memset(p, "pool", cstb[0:128, 0:1], 1.0, (cstbr,))
        ctx.onescol_f = cst[0:128, 0:1]
        ctx.onescol_f_res = cstr
        ctx.onescol_b = cstb[0:128, 0:1]
        ctx.onescol_b_res = cstbr
        onesrow, onesrow_r = ctx.small("onesrow", [1, 512], F32)
        onesrow_b, onesrow_br = ctx.small("onesrowb", [1, 512], BF16)
        memset(p, "pool", onesrow[0:1, 0:512], 1.0, (onesrow_r,))
        memset(p, "pool", onesrow_b[0:1, 0:512], 1.0, (onesrow_br,))
        ctx.onesrow_b = onesrow_b[0:1, 0:512]
        ctx.onesrow_b_res = onesrow_br
        negrow, negrow_r = ctx.small("negrow", [1, 128], F32)
        memset(p, "pool", negrow[0:1, 0:128], -1.0, (negrow_r,))
        ctx.ones_f = onesrow[0:1, 0:128]
        ctx.ones_f_res = onesrow_r
        ctx.negones_f = negrow[0:1, 0:128]
        ctx.negones_f_res = negrow_r

        if "l0a" in block_ids:
            ctx.dram_sg = nc.dram_tensor("scr_sg", [2048, T], BF16, kind="Internal").ap()
            ctx.dram_sg_res = Res("dram_sg")
        if "l0f" in block_ids:
            ctx.dram_xin = nc.dram_tensor("scr_xin", [D, T], F32, kind="Internal").ap()
            ctx.dram_xin_res = Res("dram_xin")
            ctx.dram_w1b = nc.dram_tensor("scr_w1b", [2 * D, 5460], BF16,
                                          kind="Internal").ap()
            ctx.dram_w1b_res = {}
            ctx.dram_w2b = nc.dram_tensor("scr_w2b", [5460, D], BF16,
                                          kind="Internal").ap()
            ctx.dram_w2b_res = {}
        if "l3a" in block_ids:
            ctx.dram_mx = nc.dram_tensor("scr_mx", [160, T], BF16, kind="Internal").ap()
            ctx.dram_mx_res = Res("dram_mx")
            ctx.dram_r = nc.dram_tensor("scr_l3r", [D, T], BF16, kind="Internal").ap()
            ctx.dram_r_res = Res("dram_r")
            ctx.dram_k = nc.dram_tensor("scr_l3k", [D, T], BF16, kind="Internal").ap()
            ctx.dram_k_res = Res("dram_k")
            ctx.dram_g = nc.dram_tensor("scr_l3g", [D, T], BF16, kind="Internal").ap()
            ctx.dram_g_res = Res("dram_g")
            ctx.dram_vT = nc.dram_tensor("scr_l3vT", [T, D], BF16, kind="Internal").ap()
            ctx.dram_vT_res = Res("dram_vT")
            ctx.dram_y = nc.dram_tensor("scr_l3y", [D, T], F32, kind="Internal").ap()
            ctx.dram_y_res = Res("dram_y")
            ctx.dram_rs = {}
            ctx.dram_rs_res = {}
            for nm_ in ("rF", "rS", "rBF", "rBS"):
                ctx.dram_rs[nm_] = nc.dram_tensor("scr_" + nm_, [D, T], BF16,
                                                  kind="Internal").ap()
                ctx.dram_rs_res[nm_] = Res("dram_" + nm_)
            for nm_ in ("kF", "kS", "kBF", "kBS"):
                ctx.dram_rs[nm_] = nc.dram_tensor("scr_" + nm_, [T, D], BF16,
                                                  kind="Internal").ap()
                ctx.dram_rs_res[nm_] = Res("dram_" + nm_)
            gm, gmr = ctx.small("l3a_gm", [128, 128], F32)
            for c_ in range(NCH):
                p.dma(lambda q, d=W["l3a_gmask"][c_ * 128:(c_ + 1) * 128, 0:16],
                      o=gm[0:128, 16 * c_:16 * c_ + 16]: q.dma_start(o, d),
                      reads=(), writes=(gmr,))
            ctx.l3a_gmask = gm
            ctx.l3a_gmask_res = gmr
            indt, indtr = ctx.small("l3a_indt", [16, 1024], BF16)
            p.dma(lambda q, d=W["l3a_ind"][0:16, 0:1024], o=indt[0:16, 0:1024]:
                  q.dma_start(o, d), reads=(), writes=(indtr,))
            ctx.l3a_ind = indt
            ctx.l3a_ind_res = indtr
        if "l2f" in block_ids:
            ctx.dram_q1 = nc.dram_tensor("scr_q1", [D, 4096], BF16, kind="Internal").ap()
            ctx.dram_q2 = nc.dram_tensor("scr_q2", [4096, D], BF16, kind="Internal").ap()
            ctx.dram_q1_res = {}
            ctx.dram_q2_res = {}
        if "l3f" in block_ids:
            ctx.dram_h = nc.dram_tensor("scr_h", [4096, T], BF16, kind="Internal").ap()
            ctx.dram_h_res = Res("dram_h")
            ctx.dram_h1b = nc.dram_tensor("scr_h1b", [4096, 4096], BF16,
                                          kind="Internal").ap()
            ctx.dram_h1b_res = {}
            ctx.dram_h2b = nc.dram_tensor("scr_h2b", [4096, 4096], BF16,
                                          kind="Internal").ap()
            ctx.dram_h2b_res = {}
            ctx.dram_w3b = nc.dram_tensor("scr_w3b", [4096, D], BF16,
                                          kind="Internal").ap()
            ctx.dram_w3b_res = {}
        if "l1a" in block_ids:
            ctx.dram_a = nc.dram_tensor("scr_a", [D, T], F32, kind="Internal").ap()
            ctx.dram_b = nc.dram_tensor("scr_b", [D, T], F32, kind="Internal").ap()
            ctx.dram_c = nc.dram_tensor("scr_c", [D, T], F32, kind="Internal").ap()
            ctx.dram_d = nc.dram_tensor("scr_d", [D, T], BF16, kind="Internal").ap()
            ctx.dram_a_res = Res("dram_a")
            ctx.dram_b_res = Res("dram_b")
            ctx.dram_c_res = Res("dram_c")
            ctx.dram_d_res = Res("dram_d")

        for c in range(NCH):
            memset(p, "dve", ctx.X[c][0:128, 0:PADL], 0.0, (ctx.Xr[c],))
            memset(p, "dve", ctx.X[c][0:128, PADL + T:TF], 0.0, (ctx.Xr[c],))
            memset(p, "dve", ctx.XB[c][0:128, 0:PADL], 0.0, (ctx.XBr[c],))
            memset(p, "dve", ctx.XB[c][0:128, PADL + T:TF], 0.0, (ctx.XBr[c],))
            p.dma(lambda q, d=x_in[c * 128:(c + 1) * 128, 0:T], o=ctx.Xdata(c):
                  q.dma_start(o, d), reads=(), writes=(ctx.Xr[c],))
            p.op("pool", lambda e, o=ctx.XBdata(c), i_=ctx.Xdata(c): e.tensor_copy(o, i_),
                 reads=(ctx.Xr[c],), writes=(ctx.XBr[c],))

        ctx.pool_init()
        ctx.pool_mode = True
        for b in block_ids:
            ctx.pool_block_boundary()
            ctx.ws.ext = []
            BLOCK_EMITTERS[b](ctx)
        ctx.pool_mode = False

        for c in range(NCH):
            p.dma(lambda q, d=y_out[c * 128:(c + 1) * 128, 0:T], s=ctx.Xdata(c):
                  q.dma_start(d, s), reads=(ctx.Xr[c],), writes=())

        sems = {}
        dma_sems = []
        for e in Prog.ENGINES:
            sems[e] = stack.enter_context(nc.semaphore(f"sem_{e}"))
        for i in range(p.n_dma_sems):
            dma_sems.append(stack.enter_context(nc.semaphore(f"dsem{i}")))
        p.truncate(trunc)
        ranks = p.finalize()
        blk = stack.enter_context(nc.Block())

        @blk.sync
        def _(sync):
            p.replay_dma(sync, sems, dma_sems, ranks)

        @blk.tensor
        def _(tensor):
            p.replay_engine("pe", tensor, sems, dma_sems, ranks)

        @blk.vector
        def _(vector):
            p.replay_engine("dve", vector, sems, dma_sems, ranks)

        @blk.scalar
        def _(scalar):
            p.replay_engine("act", scalar, sems, dma_sems, ranks)

        @blk.gpsimd
        def _(gpsimd):
            p.replay_engine("pool", gpsimd, sems, dma_sems, ranks)

    return nc


WEIGHT_SPECS_L0F = {
    "l0f_wqkv": (2 * D, 1536),
    "l0f_qsk": (256, 2),
    "l0f_wo": (1024, D),
    "l0f_lnb": (2 * D, 2),
    "l0f_w1": (2 * D, 5460),
    "l0f_b1c": (2, 5460),
    "l0f_w2": (5460, D),
    "l0f_b2": (2, D),
    "l0f_grp": (2, 128),
    "l0f_rotm": (128, 128, "bf16"),
    "l0f_rot": (64, 4 * T, "bf16"),
    "l0f_mask": (128, 5 * 640, "bf16"),
}

WEIGHT_SPECS_L3A = {
    "l3a_tvec": (D, 7),
    "l3a_maa_w1": (D, 160),
    "l3a_maa_w2": (160, D),
    "l3a_td_w1": (D, 64),
    "l3a_td_w2": (64, D),
    "l3a_hrow": (1, 32),
    "l3a_dmix": (D, 1),
    "l3a_wr": (D, D),
    "l3a_wk": (D, D),
    "l3a_wv": (D, D),
    "l3a_wg": (D, D),
    "l3a_wo": (D, D),
    "l3a_lng": (D, 2),
    "l3a_gmask": (D, 16),
    "l3a_ind": (16, D, "bf16"),
    "l0f_grp": (2, 128),
}

WEIGHT_SPECS_L2F = {
    "l2f_wfcT": (D, 4096),
    "l2f_wprojT": (4096, D),
    "l2f_ln": (4096, 2),
}

WEIGHT_SPECS_L3F = {
    "l3f_w1T": (D, 4096),
    "l3f_w2T": (D, 4096),
    "l3f_h1T": (4096, 4096),
    "l3f_h2T": (4096, 4096),
    "l3f_w3T": (4096, D),
    "l3f_s4": (4096, 4),
    "l3f_w3s": (D, 1),
}

WEIGHT_SPECS_L2A = {
    "l2_wT": (D, D),
    "l2_b": (D, 1),
}

WEIGHT_SPECS_L1F = {
    "l1f_lora_downT": (100, 4),
    "l1f_down_aux": (100, D),
    "l1f_lora_upT": (4, 50),
    "l1f_up_auxT": (50, D),
    "l1f_wT": (D, D),
    "l1f_b": (1, D),
}

WEIGHT_SPECS_L1A = {
    "l1_tvec": (D, 5),
    "l1_wkT": (D, D),
    "l1_wvT": (D, D),
    "l1_wrT": (D, D),
    "l1_woT": (D, D),
}

WEIGHT_SPECS_L0A = {
    "l0_w_in": (D, 4096),
    "l0_conv_w": (2048, 4),
    "l0_conv_b": (2048, 1),
    "l0_d_param": (2048, 1),
    "l0_w_x": (2048, 33),
    "l0_a_log": (16, 1),
    "l0_w_out": (2048, D),
}


ALL_BLOCKS = ["l0a", "l0f", "l1a", "l1f", "l2a", "l2f", "l3a", "l3f"]

BLOCK_SPECS = {}


def _specs():
    if BLOCK_SPECS:
        return BLOCK_SPECS
    BLOCK_SPECS.update({
        "l0a": WEIGHT_SPECS_L0A, "l0f": WEIGHT_SPECS_L0F,
        "l1a": WEIGHT_SPECS_L1A, "l1f": WEIGHT_SPECS_L1F,
        "l2a": WEIGHT_SPECS_L2A, "l2f": WEIGHT_SPECS_L2F,
        "l3a": WEIGHT_SPECS_L3A, "l3f": WEIGHT_SPECS_L3F,
    })
    return BLOCK_SPECS


def _host_consts():
    import ml_dtypes
    DIM_HEAD, WINDOW, n = 64, 512, T
    half_idx = np.arange(0, DIM_HEAD, 2, dtype=np.float64)
    inv_freq = 1.0 / 10000.0 ** (half_idx / DIM_HEAD)
    t = np.arange(n, dtype=np.float64)
    freqs = np.concatenate([t[:, None] * inv_freq[None, :]] * 2, -1)
    base = (half_idx + 0.4 * DIM_HEAD) / (1.4 * DIM_HEAD)
    power = (t - n // 2) / float(WINDOW // 2)
    scale = np.concatenate([base[None, :] ** power[:, None]] * 2, -1)
    cs = (np.cos(freqs) * scale).T
    ss = (np.sin(freqs) * scale).T
    csi = (np.cos(freqs) / scale).T
    ssi = (np.sin(freqs) / scale).T
    rot = np.concatenate([cs, ss, csi, ssi], 1).astype(ml_dtypes.bfloat16)
    R = np.zeros((128, 128), np.float32)
    for b0 in (0, 64):
        for m_ in range(32):
            R[b0 + m_ + 32, b0 + m_] = -1.0
            R[b0 + m_, b0 + m_ + 32] = 1.0
    masks = np.full((128, 5, 640), -1e30, np.float32)
    pp = np.arange(128)[:, None]
    jf = np.arange(640)[None, :]
    para = (jf >= pp) & (jf <= pp + 512)
    for qb in range(4):
        masks[:, qb, :] = np.where(para & (jf >= 512 - 128 * qb), 0.0, -1e30)
    masks[:, 4, :] = np.where(para, 0.0, -1e30)
    grp = np.zeros((2, 128), np.float32)
    grp[0, 0:64] = 1.0
    grp[1, 64:128] = 1.0
    gmask = np.zeros((D, 16), np.float32)
    for d_ in range(D):
        gmask[d_, d_ // 64] = -1.0 / 64.0
    ind = np.zeros((16, D), np.float32)
    for d_ in range(D):
        ind[d_ // 64, d_] = 1.0
    return {
        "l0f_rot": rot,
        "l0f_rotm": R.astype(ml_dtypes.bfloat16),
        "l0f_mask": masks.reshape(128, 5 * 640).astype(ml_dtypes.bfloat16),
        "l0f_grp": grp,
        "l3a_gmask": gmask,
        "l3a_ind": ind.astype(ml_dtypes.bfloat16),
    }


def _host_weights(inp):
    ct = np.ascontiguousarray
    w = {}
    w["l0_w_in"] = inp["L0_attn_W_in"]
    w["l0_conv_w"] = inp["L0_attn_conv_w"].reshape(2048, 4)
    w["l0_conv_b"] = inp["L0_attn_conv_b"].reshape(2048, 1)
    w["l0_d_param"] = inp["L0_attn_D_param"].reshape(2048, 1)
    w["l0_w_x"] = inp["L0_attn_W_x"]
    w["l0_a_log"] = inp["L0_attn_A_log"].reshape(16, 1)
    w["l0_w_out"] = inp["L0_attn_W_out"]
    w["l0f_wqkv"] = inp["L0_ffn_wqkv"].reshape(2 * D, 1536)
    qs = inp["L0_ffn_q_scale"]
    ks = inp["L0_ffn_k_scale"]
    w["l0f_qsk"] = np.stack([np.stack([np.tile(qs[l], 2), np.tile(ks[l], 2)], 1)
                             for l in range(2)]).reshape(256, 2)
    w["l0f_wo"] = inp["L0_ffn_wo"].reshape(1024, D)
    w["l0f_lnb"] = np.stack([inp["L0_ffn_ln_g"], inp["L0_ffn_ln_b"]],
                            -1).reshape(2 * D, 2)
    w["l0f_w1"] = inp["L0_ffn_w1"].reshape(2 * D, 5460)
    w["l0f_b1c"] = inp["L0_ffn_b1"]
    w["l0f_w2"] = inp["L0_ffn_w2"].reshape(2 * 2730, D)
    w["l0f_b2"] = inp["L0_ffn_b2"]
    w["l1_tvec"] = np.stack([inp["L1_attn_time_mix_k"], inp["L1_attn_time_mix_v"],
                             inp["L1_attn_time_mix_r"], inp["L1_attn_time_decay"],
                             inp["L1_attn_time_first"]], 1)
    w["l1_wkT"] = ct(inp["L1_attn_Wk"].T)
    w["l1_wvT"] = ct(inp["L1_attn_Wv"].T)
    w["l1_wrT"] = ct(inp["L1_attn_Wr"].T)
    w["l1_woT"] = ct(inp["L1_attn_Wo"].T)
    w["l1f_wT"] = ct(inp["L1_ffn_W"].T)
    w["l1f_b"] = inp["L1_ffn_b"].reshape(1, D)
    w["l1f_lora_downT"] = ct(inp["L1_ffn_lora_down"].T)
    w["l1f_lora_upT"] = ct(inp["L1_ffn_lora_up"].T)
    w["l1f_down_aux"] = inp["L1_ffn_down_aux"]
    w["l1f_up_auxT"] = ct(inp["L1_ffn_up_aux"].T)
    w["l2_wT"] = ct(inp["L2_attn_W"].T)
    w["l2_b"] = inp["L2_attn_b"].reshape(D, 1)
    w["l2f_wfcT"] = ct(inp["L2_ffn_W_fc"].T)
    w["l2f_wprojT"] = ct(inp["L2_ffn_W_proj"].T)
    w["l2f_ln"] = np.stack([inp["L2_ffn_ln_w"], inp["L2_ffn_ln_b"]], 1)
    w["l3a_tvec"] = np.stack(
        [inp["L3_attn_time_maa_x"].reshape(D), inp["L3_attn_time_maa_w"].reshape(D),
         inp["L3_attn_time_maa_k"].reshape(D), inp["L3_attn_time_maa_v"].reshape(D),
         inp["L3_attn_time_maa_r"].reshape(D), inp["L3_attn_time_maa_g"].reshape(D),
         inp["L3_attn_time_decay"].reshape(D)], 1)
    w["l3a_maa_w1"] = inp["L3_attn_time_maa_w1"]
    w["l3a_maa_w2"] = inp["L3_attn_time_maa_w2"].reshape(160, D)
    w["l3a_td_w1"] = inp["L3_attn_time_decay_w1"]
    w["l3a_td_w2"] = inp["L3_attn_time_decay_w2"]
    w["l3a_hrow"] = np.concatenate([inp["L3_attn_head_decay_bias"],
                                    inp["L3_attn_slow_scale"]]).reshape(1, 32)
    w["l3a_dmix"] = inp["L3_attn_decay_mix"].reshape(D, 1)
    w["l3a_wr"] = inp["L3_attn_W_r"]
    w["l3a_wk"] = inp["L3_attn_W_k"]
    w["l3a_wv"] = inp["L3_attn_W_v"]
    w["l3a_wg"] = inp["L3_attn_W_g"]
    w["l3a_wo"] = inp["L3_attn_W_o"]
    w["l3a_lng"] = np.stack([inp["L3_attn_ln_gamma"], inp["L3_attn_ln_beta"]], 1)
    w["l3f_w1T"] = ct(inp["L3_ffn_w1_w"].T)
    w["l3f_w2T"] = ct(inp["L3_ffn_w2_w"].T)
    w["l3f_h1T"] = ct(inp["L3_ffn_h1_w"].T)
    w["l3f_h2T"] = ct(inp["L3_ffn_h2_w"].T)
    w["l3f_w3T"] = ct(inp["L3_ffn_w3_w"].T)
    w["l3f_s4"] = np.stack([inp["L3_ffn_w1_s"], inp["L3_ffn_w2_s"],
                            inp["L3_ffn_h1_s"], inp["L3_ffn_h2_s"]], 1)
    w["l3f_w3s"] = inp["L3_ffn_w3_s"].reshape(D, 1)
    w.update(_host_consts())
    return {k: np.ascontiguousarray(np.asarray(v)) for k, v in w.items()}


def kernel(**inputs):
    import os
    x = np.asarray(inputs["L0_attn_input_tensor"])
    w = _host_weights({k: np.asarray(v) for k, v in inputs.items()})
    specs = _specs()
    fused = os.environ.get("KMODE", "fused") != "unfused"
    ncores = 8
    if fused:
        allspec = {}
        for b in ALL_BLOCKS:
            allspec.update(specs[b])
        nc = build_nc(ALL_BLOCKS, allspec)
        in_maps = []
        for core in range(ncores):
            m = {"x_t": np.ascontiguousarray(x[core].T)}
            m.update({k: w[k] for k in allspec})
            in_maps.append(m)
        res = run_bass_kernel_spmd(nc, in_maps, core_ids=list(range(ncores)))
        out = np.stack([np.asarray(r["y_t"], np.float32).T for r in res.results])
        return out
    cur = [np.ascontiguousarray(x[core].T) for core in range(ncores)]
    for b in ALL_BLOCKS:
        nc = build_nc([b], specs[b])
        in_maps = []
        for core in range(ncores):
            m = {"x_t": cur[core]}
            m.update({k: w[k] for k in specs[b]})
            in_maps.append(m)
        res = run_bass_kernel_spmd(nc, in_maps, core_ids=list(range(ncores)))
        cur = [np.ascontiguousarray(np.asarray(r["y_t"], np.float32))
               for r in res.results]
    return np.stack([c.T for c in cur])
